# version 14
# baseline (speedup 1.0000x reference)
import numpy as np
from contextlib import ExitStack

import concourse.bass as bass
import concourse.mybir as mybir
from concourse.bass_utils import run_bass_kernel_spmd

F32 = mybir.dt.float32
BF16 = mybir.dt.bfloat16
I32 = mybir.dt.int32
AF = mybir.ActivationFunctionType
ALU = mybir.AluOpType
AX = mybir.AxisListType.X

H = 4096
DSSM = 8192
NH = 64
EPS = 1e-5
CH = 128
NCHUNK = 8
TOK = 1024
TOKX = 1027
LN2C = float(np.log(2.0))
INVLN2 = float(1.0 / np.log(2.0))
NCC = 131
CC_B, CC_C, CC_DT = 128, 129, 130
SLOTS = [0, 1, 2, 4, 5, 6]


class Ct:
    __slots__ = ("v",)

    def __init__(self):
        self.v = 0

    def add(self, n=1):
        self.v += n
        return self.v


def build(phase_limit=99, want_taps=(), max_stripes=None, dbg_mode=99):
    want_taps = set(want_taps)
    nc = bass.Bass("TRN2", target_bir_lowering=False, debug=False)
    es = ExitStack()

    def sb(name, shape, dtype):
        return es.enter_context(nc.sbuf_tensor(name, shape, dtype))

    xs = nc.dram_tensor("xs", [TOKX, H], F32, kind="ExternalInput").ap()
    wi = nc.dram_tensor("wi", [NCC, 2, 128, 16, 128], F32, kind="ExternalInput").ap()
    wo = nc.dram_tensor("wo", [8, 64, 128, 512], F32, kind="ExternalInput").ap()
    wf = nc.dram_tensor("wf", [4096, 4096], F32, kind="ExternalInput").ap()
    convw_in = nc.dram_tensor("convw_t", [128, 264], F32, kind="ExternalInput").ap()
    dtb_in = nc.dram_tensor("dtb", [NH, 1], F32, kind="ExternalInput").ap()
    alog_in = nc.dram_tensor("alog", [NH, 1], F32, kind="ExternalInput").ap()
    dpar_in = nc.dram_tensor("dpar", [128, NH], F32, kind="ExternalInput").ap()
    gnorm_in = nc.dram_tensor("gnorm", [128, NH], F32, kind="ExternalInput").ap()
    fbias_in = nc.dram_tensor("fbias", [128, 32], F32, kind="ExternalInput").ap()
    mask_in = nc.dram_tensor("maskb", [128, 128], F32, kind="ExternalInput").ap()
    maskT_in = nc.dram_tensor("maskbT", [128, 128], F32, kind="ExternalInput").ap()
    ident_in = nc.dram_tensor("identb", [128, 128], F32, kind="ExternalInput").ap()
    cmask_in = nc.dram_tensor("cmask", [8, 16], F32, kind="ExternalInput").ap()
    out_t = nc.dram_tensor("out", [TOK, H], F32, kind="ExternalOutput").ap()

    xtm_st = nc.dram_tensor("xtm_st", [NCHUNK, NH, 128, 128], BF16).ap()
    sz_st = nc.dram_tensor("sz_st", [NCHUNK, NH, 128, 128], BF16).ap()
    yn_st = nc.dram_tensor("yn_st", [NH, 128, TOK], BF16).ap()
    x1_st = nc.dram_tensor("x1_st", [TOK, H], F32).ap()
    cc_in = nc.dram_tensor("cc_in", [128, DSSM], BF16)
    cc_out = nc.dram_tensor("cc_out", [1024, DSSM], BF16)
    ccd_in = nc.dram_tensor("ccd_in", [1, NH], F32)
    ccd_out = nc.dram_tensor("ccd_out", [8, NH], F32)

    taps = []

    def tap(name, src_ap):
        if name in want_taps:
            d = nc.dram_tensor("tap_" + name, list(src_ap.shape), src_ap.dtype,
                               kind="ExternalOutput").ap()
            taps.append((d, src_ap))

    sK = es.enter_context(nc.semaphore("sK"))
    sWp = [es.enter_context(nc.semaphore(f"sW{i}")) for i in range(2)]
    sXp = [es.enter_context(nc.semaphore(f"sX{i}")) for i in range(2)]
    sXc = es.enter_context(nc.semaphore("sXc"))
    sTp = [es.enter_context(nc.semaphore(f"sT{i}")) for i in range(4)]
    sZp = [es.enter_context(nc.semaphore(f"sZ{i}")) for i in range(2)]
    sNp = [es.enter_context(nc.semaphore(f"sN{i}")) for i in range(2)]
    sYp = [es.enter_context(nc.semaphore(f"sY{i}")) for i in range(2)]
    sOp = [es.enter_context(nc.semaphore(f"sO{i}")) for i in range(2)]
    sG = es.enter_context(nc.semaphore("sG"))
    sP = es.enter_context(nc.semaphore("sP"))
    sV = es.enter_context(nc.semaphore("sV"))
    sS = es.enter_context(nc.semaphore("sS"))
    sC = es.enter_context(nc.semaphore("sC"))
    cK = Ct()
    cWp = [Ct(), Ct()]
    cXp = [Ct(), Ct()]
    cXc = Ct()
    cTp = [Ct() for _ in range(4)]
    cZp = [Ct(), Ct()]
    cNp = [Ct(), Ct()]
    cYp = [Ct(), Ct()]
    cOp = [Ct(), Ct()]
    cG, cP, cV, cS, cC = (Ct() for _ in range(5))
    all_dma_sems = ([(sK, cK), (sXc, cXc), (sG, cG)] +
                    list(zip(sWp, cWp)) + list(zip(sXp, cXp)) +
                    list(zip(sTp, cTp)) + list(zip(sZp, cZp)) +
                    list(zip(sNp, cNp)) + list(zip(sYp, cYp)) +
                    list(zip(sOp, cOp)))

    convw_sb = sb("convw_sb", [128, 264], F32)
    dtb_sb = sb("dtb_sb", [NH, 1], F32)
    alog_sb = sb("alog_sb", [NH, 1], F32)
    dparc = sb("dparc", [128, NH], F32)
    gnormc = sb("gnormc", [128, NH], F32)
    fbiasc = sb("fbiasc", [128, 32], F32)
    identf = sb("identf_s", [128, 128], F32)
    identb = sb("identb_s", [128, 128], BF16)
    onesrow = sb("onesrow", [1, 128], F32)
    onescol_bf = sb("onescol_bf", [128, 1], BF16)
    cmask_sb = sb("cmask_sb", [8, 16], F32)
    cst = sb("cst", [128, 4], F32)

    msctx = ExitStack()
    b_fm = sb("b_fm", [128, TOK], BF16)
    c_fm = sb("c_fm", [128, TOK], BF16)
    b_tm = sb("b_tm", [128, NCHUNK * 128], BF16)
    dt_sb = msctx.enter_context(nc.sbuf_tensor("dt_sb", [NH, TOK], F32, side="right"))
    lda_sb = msctx.enter_context(nc.sbuf_tensor("lda_sb", [NH, TOK], F32, side="right"))
    lndt_sb = msctx.enter_context(nc.sbuf_tensor("lndt_sb", [NH, TOK], F32, side="right"))
    ll_sb = msctx.enter_context(nc.sbuf_tensor("ll_sb", [NH, TOK], F32, side="right"))
    offs_sb = sb("offs_sb", [NH, 16], F32)
    ltot_sb = sb("ltot_sb", [NH, 8], F32)
    etot_sb = sb("etot_sb", [NH, 8], F32)
    cumtm = msctx.enter_context(nc.sbuf_tensor("cumtm", [128, NCHUNK * 64], F32, side="right"))
    maskT_sb = sb("maskT_sb", [128, 128], F32)
    dgT_sb = sb("dgT_sb", [NH, 8], F32)
    coefT_sb = sb("coefT_sb", [NH, 8], F32)
    wtm = msctx.enter_context(nc.sbuf_tensor("wtm", [128, NCHUNK * 64], F32, side="right"))
    ltm = msctx.enter_context(nc.sbuf_tensor("ltm", [128, NCHUNK * 64], F32, side="right"))
    wctm = msctx.enter_context(nc.sbuf_tensor("wctm", [128, NCHUNK * 64], F32, side="right"))
    lndtm = msctx.enter_context(nc.sbuf_tensor("lndtm", [128, NCHUNK * 64], F32, side="right"))
    dbc = msctx.enter_context(nc.sbuf_tensor("dbc", [128, NCHUNK * 64], F32, side="right"))
    sm_a = sb("sm_a", [NH, 128], F32)
    sm_b = sb("sm_b", [NH, 128], F32)
    dslrow = sb("dslrow", [1, NH], F32)
    coef_sb = sb("coef_sb", [8, NH], F32)
    coefbc = sb("coefbc", [128, 8 * NH], F32)
    dg_sb = sb("dg_sb", [8, NH], F32)

    pt = es.enter_context(nc.psum_tensor("pt", [128, 512], F32))
    ph = es.enter_context(nc.psum_tensor("ph", [128, 512], F32))

    block = es.enter_context(nc.Block())

    lnxt_ctx = ExitStack()
    lnxt = lnxt_ctx.enter_context(nc.sbuf_tensor("lnxt", [128, 32 * 1028], BF16))
    ln_ctx = ExitStack()
    x_tile = ln_ctx.enter_context(nc.sbuf_tensor("x_tile", [128, 2 * H], F32))
    xc_f = ln_ctx.enter_context(nc.sbuf_tensor("xc_f", [128, 2 * H], F32))
    ln_small = ln_ctx.enter_context(nc.sbuf_tensor("ln_small", [128, 16], F32))
    rrow = ln_ctx.enter_context(nc.sbuf_tensor("rrow", [1, 128], F32))
    rbcast = ln_ctx.enter_context(nc.sbuf_tensor("rbcast", [128, 128], F32))
    sq_sink = ln_ctx.enter_context(nc.sbuf_tensor("sq_sink", [128, H], BF16))
    pjs = ExitStack()
    pj = [pjs.enter_context(nc.psum_tensor(f"pj{i}", [128, 1536], F32)) for i in range(2)]

    @block.sync
    def _(sync):
        for dst, src in ((convw_sb[:], convw_in), (dtb_sb[:], dtb_in),
                         (alog_sb[:], alog_in), (dparc[:], dpar_in),
                         (gnormc[:], gnorm_in), (fbiasc[:], fbias_in),
                         (maskT_sb[:], maskT_in), (identf[:], ident_in),
                         (cmask_sb[:], cmask_in)):
            sync.dma_start(out=dst, in_=src).then_inc(sK, 16)
            cK.add(16)

    @block.vector
    def _(vector):
        vector.wait_ge(sK, cK.v)
        vector.memset(onesrow[:], 1.0)
        vector.memset(onescol_bf[:], 1.0)
        vector.memset(cst[:, 0:1], EPS)
        vector.memset(cst[:, 1:2], -7.0 * LN2C)
        vector.memset(cst[:, 2:3], 7.0 * LN2C)
        vector.tensor_copy(out=identb[:], in_=identf[:])
        vector.drain()
        vector.drain()
        vector.nop().then_inc(sV, 1)
    cV.add(1)
    ev_consts = cV.v

    xrows = [(t * 128, 128) for t in range(8)] + [(1024, 3)]
    ln_v = [None] * 9
    ln_pe = [None] * 9
    ln_xc = [None] * 9

    for tt, (r0, nr) in enumerate(xrows):
        q = tt % 2

        @block.scalar
        def _(scalar, tt=tt, r0=r0, nr=nr, q=q):
            if tt >= 2:
                scalar.wait_ge(sP, ln_pe[tt - 2])
                scalar.wait_ge(sV, ln_xc[tt - 2])
            scalar.dma_start(out=x_tile[0:nr, q * H:q * H + H],
                             in_=xs[r0:r0 + nr, :]).then_inc(sXp[q], 16)
        cXp[q].add(16)
        ev_xload = (sXp[q], cXp[q].v)

        @block.scalar
        def _(scalar, tt=tt, nr=nr, q=q, ev_xload=ev_xload):
            scalar.wait_ge(ev_xload[0], ev_xload[1])
            if tt == 0:
                scalar.wait_ge(sV, ev_consts)
            scalar.activation(out=sq_sink[0:nr, :], in_=x_tile[0:nr, q * H:q * H + H],
                              func=AF.Square).then_inc(sS, 1)
        cS.add(1)
        ev_sq = cS.v

        @block.vector
        def _(vector, tt=tt, nr=nr, q=q, ev_xload=ev_xload, ev_sq=ev_sq):
            vector.wait_ge(ev_xload[0], ev_xload[1])
            vector.wait_ge(sS, ev_sq)
            xt = x_tile[0:nr, q * H:q * H + H]
            vector.reduce_sum(ln_small[0:nr, 1:2], sq_sink[0:nr, :], axis=AX)
            vector.reduce_sum(ln_small[0:nr, 0:1], xt, axis=AX)
            vector.drain()
            vector.tensor_scalar(out=ln_small[0:nr, 1:2], in0=ln_small[0:nr, 1:2],
                                 scalar1=1.0 / H, scalar2=None, op0=ALU.mult)
            vector.tensor_scalar(out=ln_small[0:nr, 0:1], in0=ln_small[0:nr, 0:1],
                                 scalar1=1.0 / H, scalar2=None, op0=ALU.mult)
            vector.drain()
            vector.tensor_tensor(out=ln_small[0:nr, 2:3], in0=ln_small[0:nr, 0:1],
                                 in1=ln_small[0:nr, 0:1], op=ALU.mult)
            vector.tensor_scalar(out=ln_small[0:nr, 5:6], in0=ln_small[0:nr, 0:1],
                                 scalar1=-1.0, scalar2=None, op0=ALU.mult)
            vector.drain()
            vector.tensor_sub(ln_small[0:nr, 2:3], ln_small[0:nr, 1:2],
                              ln_small[0:nr, 2:3])
            vector.drain()
            vector.drain()
            vector.nop().then_inc(sV, 1)
        cV.add(1)
        ev_stats = cV.v

        @block.scalar
        def _(scalar, nr=nr, ev_stats=ev_stats):
            scalar.wait_ge(sV, ev_stats)
            scalar.activation(out=ln_small[0:nr, 3:4], in_=ln_small[0:nr, 2:3],
                              func=AF.Sqrt, bias=cst[0:nr, 0:1]).then_inc(sS, 1)
        cS.add(1)
        ev_sqrt = cS.v

        @block.vector
        def _(vector, nr=nr, q=q, ev_sqrt=ev_sqrt):
            vector.wait_ge(sS, ev_sqrt)
            vector.reciprocal(out=ln_small[0:nr, 4:5], in_=ln_small[0:nr, 3:4])
            vector.drain()
            vector.tensor_scalar(out=xc_f[0:nr, q * H:q * H + H],
                                 in0=x_tile[0:nr, q * H:q * H + H],
                                 scalar1=ln_small[0:nr, 5:6], scalar2=None,
                                 op0=ALU.add)
            vector.drain()
            vector.drain()
            vector.nop().then_inc(sV, 1)
        cV.add(1)
        ln_xc[tt] = cV.v

        @block.tensor
        def _(tensor, tt=tt, nr=nr):
            tensor.wait_ge(sV, ln_xc[tt])
            if tt >= 1:
                tensor.wait_ge(sV, ln_v[tt - 1])
            tensor.transpose(ph[0:1, 0:nr], ln_small[0:nr, 4:5],
                             identf[0:nr, 0:nr]).then_inc(sP, 1)
        cP.add(1)

        @block.vector
        def _(vector, nr=nr, w=cP.v):
            vector.wait_ge(sP, w)
            vector.tensor_copy(out=rrow[0:1, 0:nr], in_=ph[0:1, 0:nr]).then_inc(sV, 1)
        cV.add(1)

        @block.tensor
        def _(tensor, nr=nr, w=cV.v):
            tensor.wait_ge(sV, w)
            tensor.matmul(ph[:, 256:256 + nr], onesrow[:], rrow[0:1, 0:nr],
                          start=True, stop=True).then_inc(sP, 1)
        cP.add(1)

        @block.vector
        def _(vector, nr=nr, w=cP.v):
            vector.wait_ge(sP, w)
            vector.tensor_copy(out=rbcast[:, 0:nr],
                               in_=ph[:, 256:256 + nr]).then_inc(sV, 1)
        cV.add(1)
        ev_rb = cV.v

        evac_ev = []
        for fc in range(32):
            @block.tensor
            def _(tensor, tt=tt, nr=nr, q=q, fc=fc):
                if fc == 0:
                    tensor.wait_ge(sV, ev_rb)
                if fc >= 2:
                    tensor.wait_ge(sV, evac_ev[fc - 2])
                dstp = pt if fc % 2 == 0 else ph
                tensor.transpose(dstp[:, 0:nr],
                                 xc_f[0:nr, q * H + fc * 128:q * H + fc * 128 + 128],
                                 identf[0:nr, 0:nr]).then_inc(sP, 1)
            cP.add(1)

            @block.vector
            def _(vector, r0=r0, nr=nr, fc=fc, w=cP.v):
                vector.wait_ge(sP, w)
                srcp = pt if fc % 2 == 0 else ph
                vector.tensor_tensor(
                    out=lnxt[:, fc * 1028 + r0:fc * 1028 + r0 + nr],
                    in0=srcp[:, 0:nr], in1=rbcast[:, 0:nr],
                    op=ALU.mult).then_inc(sV, 1)
            cV.add(1)
            evac_ev.append(cV.v)
        ln_v[tt] = cV.v
        ln_pe[tt] = cP.v

    tap("lnxt", lnxt[:, :])

    if phase_limit < 1:
        return _finish(nc, block, es, taps, sV, cV, sG, cG, [], out_t,
                       extra_ctx=[pjs, ln_ctx, lnxt_ctx], all_sems=all_dma_sems)

    ln_ctx.close()
    sctx = ExitStack()
    s_all = sctx.enter_context(nc.sbuf_tensor("s_all", [128, NH * 128], F32, side="right"))
    phA_ctx = ExitStack()
    exch = phA_ctx.enter_context(nc.sbuf_tensor("exch", [128, NH * 128], BF16))
    wbf = phA_ctx.enter_context(nc.sbuf_tensor("wbf", [128, 2 * 2048], BF16))
    conv_acc = phA_ctx.enter_context(nc.sbuf_tensor("conv_acc", [128, TOK], F32))
    xch_f = phA_ctx.enter_context(nc.sbuf_tensor("xch_f", [128, 2 * TOK], F32))
    bc_f = phA_ctx.enter_context(nc.sbuf_tensor("bc_f", [128, TOK], F32))
    dt_raw = phA_ctx.enter_context(nc.sbuf_tensor("dt_raw", [NH, TOK], F32))
    xw_sb = phA_ctx.enter_context(nc.sbuf_tensor("xw_sb", [128, 2 * 128], BF16))
    xtm_sb = phA_ctx.enter_context(nc.sbuf_tensor("xtm_sb", [128, 2 * 128], BF16))
    szh_bf = phA_ctx.enter_context(nc.sbuf_tensor("szh_bf", [128, 2 * TOK], BF16))
    slot_sb = phA_ctx.enter_context(nc.sbuf_tensor("slot_sb", [128, DSSM // 4], BF16))

    @block.vector
    def _(vector):
        vector.memset(conv_acc[:], 0.0).then_inc(sV, 1)
    cV.add(1)

    cc_order = [CC_DT, CC_B, CC_C] + list(range(64, 128)) + list(range(0, 64))
    if max_stripes is not None:
        cc_order = cc_order[:max_stripes]
    NSTRIPE = len(cc_order)
    evloadH = [None] * (2 * NSTRIPE)
    evcastH = [None] * (2 * NSTRIPE)
    evmmH = [None] * (2 * NSTRIPE)
    evmm = [None] * NSTRIPE
    evfree = [None] * NSTRIPE
    evsilu = [None] * NSTRIPE
    ev_hloc_ev = {}
    ev_xtm_dma = {}
    btm_ev = []
    ev_xtm_ev = {}
    ev_xtm_T = {}
    ev_smalls_done = None
    ev_dsl = None
    ev_sz_dma = {}

    GRP_FULL = [(0, 512), (512, 512), (1024, 3)]
    GRP_OWN = [(3, 0, 512), (515, 512, 512)]

    def xbc_ci(cc):
        if cc == CC_B:
            return 64
        if cc == CC_C:
            return 65
        return cc - 64

    last_ln = ln_v[8]

    def emit_tblock(ii, cc2):
        is_b2 = cc2 == CC_B
        is_xh2 = 64 <= cc2 < 128
        if is_b2:
            btm_ev.clear()
            for c in range(NCHUNK):
                @block.tensor
                def _(tensor, i=ii, c=c):
                    if c == 0:
                        tensor.wait_ge(sS, evsilu[i])
                        tensor.wait_ge(sV, ev_smalls_done)
                    if c >= 2:
                        tensor.wait_ge(sV, btm_ev[c - 2])
                    dstp = pt if c % 2 == 0 else ph
                    tensor.transpose(dstp[:, 0:128], bc_f[:, c * 128:c * 128 + 128],
                                     identf[:]).then_inc(sP, 1)
                cP.add(1)

                @block.vector
                def _(vector, c=c, w=cP.v):
                    vector.wait_ge(sP, w)
                    srcp = pt if c % 2 == 0 else ph
                    vector.tensor_copy(out=b_tm[:, c * 128:(c + 1) * 128],
                                       in_=srcp[:, 0:128]).then_inc(sV, 1)
                cV.add(1)
                btm_ev.append(cV.v)
        if is_xh2:
            h = cc2 - 64
            hq = ii % 2
            for c in range(NCHUNK):
                @block.tensor
                def _(tensor, i=ii, h=h, c=c, hq=hq):
                    if c == 0:
                        tensor.wait_ge(sS, evsilu[i])
                        tensor.wait_ge(sV, ev_smalls_done)
                        if h >= 1:
                            tensor.wait_ge(sV, ev_hloc_ev[h - 1])
                    if c >= 2:
                        tensor.wait_ge(sV, ev_xtm_ev[(h, c - 2)])
                    dstp = pt if c % 2 == 0 else ph
                    tensor.transpose(dstp[:, 0:128],
                                     xch_f[:, hq * TOK + c * 128:hq * TOK + c * 128 + 128],
                                     identf[:]).then_inc(sP, 1)
                cP.add(1)
                ev_xtm_T[(h, c)] = cP.v

                @block.vector
                def _(vector, h=h, c=c, w=cP.v):
                    vector.wait_ge(sP, w)
                    if (h, c - 2) in ev_xtm_dma:
                        vector.wait_ge(sTp[(c - 2) % 4], ev_xtm_dma[(h, c - 2)])
                    elif c < 2 and h >= 1 and (h - 1, 6 + c) in ev_xtm_dma:
                        vector.wait_ge(sTp[(6 + c) % 4], ev_xtm_dma[(h - 1, 6 + c)])
                    srcp = pt if c % 2 == 0 else ph
                    vector.tensor_copy(out=xtm_sb[:, (c % 2) * 128:(c % 2) * 128 + 128],
                                       in_=srcp[:, 0:128])
                    vector.tensor_scalar(
                        out=xw_sb[:, (c % 2) * 128:(c % 2) * 128 + 128],
                        in0=srcp[:, 0:128],
                        scalar1=wtm[:, c * 64 + h:c * 64 + h + 1], scalar2=None,
                        op0=ALU.mult).then_inc(sV, 1)
                cV.add(1)
                ev_xtm_ev[(h, c)] = cV.v

                @block.sync
                def _(sync, h=h, c=c, w=cV.v):
                    sync.wait_ge(sV, w)
                    sync.dma_start(out=xtm_st[c, h],
                                   in_=xtm_sb[:, (c % 2) * 128:(c % 2) * 128 + 128]
                                   ).then_inc(sTp[c % 4], 16)
                cTp[c % 4].add(16)
                ev_xtm_dma[(h, c)] = cTp[c % 4].v

                @block.tensor
                def _(tensor, h=h, c=c, ii=ii, w=cV.v):
                    q2 = ii % 2
                    tensor.wait_ge(sV, w)
                    tensor.matmul(pj[q2][:, 1152:1280],
                                  b_tm[:, c * 128:(c + 1) * 128],
                                  xw_sb[:, (c % 2) * 128:(c % 2) * 128 + 128],
                                  start=(c == 0), stop=(c == NCHUNK - 1)).then_inc(sP, 1)
                cP.add(1)
            ev_hloc_mm = cP.v

            @block.vector
            def _(vector, h=h, ii=ii, w=ev_hloc_mm):
                q2 = ii % 2
                vector.wait_ge(sP, w)
                vector.tensor_copy(out=exch[:, h * 128:(h + 1) * 128],
                                   in_=pj[q2][:, 1152:1280]).then_inc(sV, 1)
            cV.add(1)
            ev_hloc_ev[h] = cV.v
            evfree[ii] = (sV, cV.v)

            if h == 63:
                @block.gpsimd
                def _(gpsimd):
                    gpsimd.wait_ge(sV, ev_hloc_ev[63])
                    gpsimd.wait_ge(sV, ev_dsl)
                    gpsimd.dma_start(out=cc_in.ap(), in_=exch[:]).then_inc(sG, 16)
                    gpsimd.dma_start(out=ccd_in.ap(), in_=dslrow[:]).then_inc(sG, 16)
                    gpsimd.wait_ge(sG, cG.v + 32)
                    gpsimd.collective_compute(
                        "AllGather", ALU.bypass, replica_groups=[list(range(8))],
                        ins=[cc_in.ap().opt()], outs=[cc_out.ap().opt()]).then_inc(sC, 1)
                    gpsimd.collective_compute(
                        "AllGather", ALU.bypass, replica_groups=[list(range(8))],
                        ins=[ccd_in.ap().opt()], outs=[ccd_out.ap().opt()]).then_inc(sC, 1)
                cG.add(32)
                cC.add(2)


    for i, cc in enumerate(cc_order):
        q = i % 2
        is_dt = cc == CC_DT
        is_b = cc == CC_B
        is_c = cc == CC_C
        is_xh = 64 <= cc < 128
        is_z = cc < 64
        h = cc - 64 if is_xh else None

        for hf in range(2):
            j = 2 * i + hf
            qj = j % 2

            @block.gpsimd
            def _(gpsimd, i=i, cc=cc, hf=hf, j=j, qj=qj):
                if j < 2:
                    gpsimd.wait_ge(sV, last_ln)
                if j >= 2:
                    gpsimd.wait_ge(sP, evmmH[j - 2])
                gpsimd.dma_start(out=wbf[:, qj * 2048:(qj + 1) * 2048],
                                 in_=wi[cc, hf]).then_inc(sWp[qj], 16)
            cWp[qj].add(16)
            evloadH[j] = (sWp[qj], cWp[qj].v)

            @block.tensor
            def _(tensor, i=i, q=q, is_z=is_z, hf=hf, j=j, qj=qj):
                if dbg_mode < 2:
                    tensor.nop().then_inc(sP, 1)
                    return
                tensor.wait_ge(evloadH[j][0], evloadH[j][1])
                if hf == 0:
                    if i == 0:
                        tensor.wait_ge(sV, last_ln)
                    if i >= 2:
                        s, v = evfree[i - 2]
                        tensor.wait_ge(s, v)
                last = None
                for kk16 in range(16):
                    kk = hf * 16 + kk16
                    a = wbf[:, qj * 2048 + kk16 * 128:qj * 2048 + kk16 * 128 + 128]
                    if is_z:
                        for (ro, po, n) in GRP_OWN:
                            last = tensor.matmul(
                                pj[q][:, po:po + n], a,
                                lnxt[:, kk * 1028 + ro:kk * 1028 + ro + n],
                                start=(kk == 0), stop=(kk == 31))
                    else:
                        for (o, n) in GRP_FULL:
                            last = tensor.matmul(
                                pj[q][:, o:o + n], a,
                                lnxt[:, kk * 1028 + o:kk * 1028 + o + n],
                                start=(kk == 0), stop=(kk == 31))
                last.then_inc(sP, 1)
            cP.add(1)
            evmmH[j] = cP.v
        evmm[i] = evmmH[2 * i + 1]

        if i >= 1:
            emit_tblock(i - 1, cc_order[i - 1])
        if i == NSTRIPE - 1:
            _defer_last = True

        if is_dt:
            @block.vector
            def _(vector, i=i, q=q):
                vector.wait_ge(sP, evmm[i])
                if dbg_mode < 3:
                    vector.memset(dt_raw[:, :], 0.1)
                    vector.drain()
                    vector.nop().then_inc(sV, 1)
                    return
                vector.tensor_copy(out=dt_raw[:, :],
                                   in_=pj[q][0:NH, 3:TOKX]).then_inc(sV, 1)
            cV.add(1)
            evfree[i] = (sV, cV.v)
        elif is_z:
            zi = cc

            @block.scalar
            def _(scalar, i=i, q=q, zi=zi):
                scalar.wait_ge(sP, evmm[i])
                if zi >= 2:
                    scalar.wait_ge(sZp[zi % 2], ev_sz_dma[zi - 2])
                scalar.activation(out=szh_bf[:, (zi % 2) * TOK:(zi % 2 + 1) * TOK],
                                  in_=pj[q][:, 0:TOK],
                                  func=AF.Silu).then_inc(sS, 1)
            cS.add(1)
            evfree[i] = (sS, cS.v)
            evsilu[i] = cS.v

            @block.sync
            def _(sync, zi=zi, w=cS.v):
                sync.wait_ge(sS, w)
                for c in range(NCHUNK):
                    sync.dma_start(
                        out=sz_st[c, zi],
                        in_=szh_bf[:, (zi % 2) * TOK + c * 128:
                                   (zi % 2) * TOK + c * 128 + 128]
                    ).then_inc(sZp[zi % 2], 16)
            cZp[zi % 2].add(16 * NCHUNK)
            ev_sz_dma[zi] = cZp[zi % 2].v
        else:
            ci = xbc_ci(cc)

            @block.vector
            def _(vector, i=i, q=q, ci=ci):
                vector.wait_ge(sP, evmm[i])
                if i >= 1 and evsilu[i - 1] is not None:
                    vector.wait_ge(sS, evsilu[i - 1])
                ca = conv_acc[:, 0:TOK]
                vector.scalar_tensor_tensor(
                    out=ca, in0=pj[q][:, 0:1024],
                    scalar=convw_sb[:, ci * 4:ci * 4 + 1], in1=ca,
                    op0=ALU.mult, op1=ALU.bypass)
                for j in (1, 2, 3):
                    vector.drain()
                    vector.scalar_tensor_tensor(
                        out=ca, in0=pj[q][:, j:1024 + j],
                        scalar=convw_sb[:, ci * 4 + j:ci * 4 + j + 1], in1=ca,
                        op0=ALU.mult, op1=ALU.add)
                vector.drain()
                vector.drain()
                vector.nop().then_inc(sV, 1)
            cV.add(1)
            evfree[i] = (sV, cV.v)

            @block.scalar
            def _(scalar, i=i, cc=cc, w=cV.v):
                scalar.wait_ge(sV, w)
                if cc == CC_C:
                    scalar.wait_ge(sV, btm_ev[NCHUNK - 1])
                if 64 <= cc < 128 and (cc - 64) >= 2:
                    scalar.wait_ge(sP, ev_xtm_T[(cc - 64 - 2, NCHUNK - 1)])
                dst = (xch_f[:, (i % 2) * TOK:(i % 2 + 1) * TOK] if 64 <= cc < 128
                       else bc_f[:])
                scalar.activation(out=dst, in_=conv_acc[:, 0:TOK],
                                  func=AF.Silu).then_inc(sS, 1)
            cS.add(1)
            evsilu[i] = cS.v

            if cc in (CC_B, CC_C):
                @block.vector
                def _(vector, cc=cc, w=cS.v):
                    vector.wait_ge(sS, w)
                    vector.tensor_copy(out=(b_fm[:] if cc == CC_B else c_fm[:]),
                                       in_=bc_f[:]).then_inc(sV, 1)
                cV.add(1)
                evsilu[i] = (cS.v, cV.v)[1] if False else evsilu[i]
                ev_bcf = cV.v

        if is_dt and dbg_mode >= 1:
            ev_smalls_done, ev_dsl = _dt_math(
                block, locals(), sV, cV, sS, cS, sP, cP,
                upto=(dbg_mode - 10 if 10 < dbg_mode < 20 else 99))
        elif is_dt:
            ev_smalls_done, ev_dsl = (cV.v, cV.v)

    tap("dt", dt_sb[:, :])
    tap("ll", ll_sb[:, :])
    tap("cum", cumtm[:, :])
    tap("wtm", wtm[:, :])
    tap("b_fm", b_fm[:, :])
    tap("c_fm", c_fm[:, :])
    tap("exch", exch[:, :])

    if phase_limit < 2:
        return _finish(nc, block, es, taps, sV, cV, sG, cG, [], out_t,
                       extra_ctx=[pjs, phA_ctx, lnxt_ctx, sctx], all_sems=all_dma_sems)

    @block.gpsimd
    def _(gpsimd):
        gpsimd.wait_ge(sC, 2)
        gpsimd.dma_start(out=dg_sb[:], in_=ccd_out.ap()).then_inc(sG, 16)
    cG.add(16)
    ev_dg = cG.v

    @block.vector
    def _(vector):
        vector.wait_ge(sG, ev_dg)
        vector.memset(s_all[:], 0.0)
        vector.memset(coef_sb[:], 1.0)
        vector.drain()
        vector.drain()
        vector.nop().then_inc(sV, 1)
    cV.add(1)

    @block.tensor
    def _(tensor, w=cV.v):
        tensor.wait_ge(sV, w)
        tensor.transpose(pt[0:NH, 0:8], dg_sb[:], identf[0:8, 0:8]).then_inc(sP, 1)
    cP.add(1)

    @block.vector
    def _(vector, w=cP.v):
        vector.wait_ge(sP, w)
        vector.tensor_copy(out=dgT_sb[:], in_=pt[0:NH, 0:8]).then_inc(sV, 1)
    cV.add(1)

    for k in (1, 2, 5, 6):
        @block.vector
        def _(vector, k=k, w=cV.v):
            vector.wait_ge(sV, w)
            vector.wait_ge(sP, cP.v)
            vector.drain()
            vector.tensor_scalar(out=sm_a[0:NH, 0:8], in0=sm_a[0:NH, 0:8],
                                 scalar1=0.0, scalar2=dgT_sb[:, k:k + 1],
                                 op0=ALU.mult, op1=ALU.add)
            vector.drain()
            vector.drain()
            vector.nop().then_inc(sV, 1)
        cV.add(1)

        @block.tensor
        def _(tensor, k=k, w=cV.v):
            tensor.wait_ge(sV, w)
            tensor.transpose(pt[0:8, 64:128], sm_a[0:NH, 0:8],
                             identf[0:NH, 0:NH]).then_inc(sP, 1)
        cP.add(1)

        @block.vector
        def _(vector, k=k, w=cP.v):
            vector.wait_ge(sP, w)
            vector.tensor_scalar(out=sm_b[0:8, 0:NH], in0=pt[0:8, 64:128],
                                 scalar1=cmask_sb[:, k:k + 1],
                                 scalar2=cmask_sb[:, 8 + k:9 + k],
                                 op0=ALU.mult, op1=ALU.add)
            vector.drain()
            vector.tensor_mul(coef_sb[:], coef_sb[:], sm_b[0:8, 0:NH])
            vector.drain()
            vector.drain()
            vector.nop().then_inc(sV, 1)
        cV.add(1)

    @block.vector
    def _(vector):
        vector.tensor_scalar(out=coef_sb[:], in0=coef_sb[:],
                             scalar1=cmask_sb[:, 15:16], scalar2=None,
                             op0=ALU.mult)
        vector.drain()
        vector.drain()
        vector.nop().then_inc(sV, 1)
    cV.add(1)
    ev_coef = cV.v

    @block.tensor
    def _(tensor, w=cV.v):
        tensor.wait_ge(sV, w)
        tensor.transpose(pt[0:NH, 8:16], coef_sb[:], identf[0:8, 0:8]).then_inc(sP, 1)
    cP.add(1)

    @block.vector
    def _(vector, w=cP.v):
        vector.wait_ge(sP, w)
        vector.tensor_copy(out=coefT_sb[:], in_=pt[0:NH, 8:16]).then_inc(sV, 1)
    cV.add(1)

    for ji, j in enumerate(SLOTS):
        @block.vector
        def _(vector, j=j, w=cV.v):
            vector.wait_ge(sV, w)
            vector.wait_ge(sP, cP.v)
            vector.drain()
            vector.tensor_scalar(out=sm_a[0:NH, 0:128], in0=sm_a[0:NH, 0:128],
                                 scalar1=0.0, scalar2=coefT_sb[:, j:j + 1],
                                 op0=ALU.mult, op1=ALU.add)
            vector.drain()
            vector.drain()
            vector.nop().then_inc(sV, 1)
        cV.add(1)

        @block.tensor
        def _(tensor, j=j, w=cV.v):
            tensor.wait_ge(sV, w)
            tensor.transpose(ph[0:128, 192:256], sm_a[0:NH, 0:128],
                             identf[0:NH, 0:NH]).then_inc(sP, 1)
        cP.add(1)

        @block.vector
        def _(vector, j=j, w=cP.v):
            vector.wait_ge(sP, w)
            vector.tensor_copy(out=coefbc[:, j * NH:(j + 1) * NH],
                               in_=ph[0:128, 192:256]).then_inc(sV, 1)
        cV.add(1)
    ev_coefbc = cV.v

    for ji, j in enumerate(SLOTS):
        for qtr in range(4):
            @block.gpsimd
            def _(gpsimd, j=j, ji=ji, qtr=qtr):
                if ji > 0 or qtr > 0:
                    gpsimd.wait_ge(sV, cV.v)
                gpsimd.dma_start(
                    out=slot_sb[:],
                    in_=cc_out.ap()[j * 128:(j + 1) * 128,
                                    qtr * (DSSM // 4):(qtr + 1) * (DSSM // 4)]
                ).then_inc(sG, 16)
            cG.add(16)

            @block.vector
            def _(vector, j=j, qtr=qtr, w=cG.v):
                vector.wait_ge(sG, w)
                for hh in range(NH // 4):
                    hg = qtr * (NH // 4) + hh
                    vector.scalar_tensor_tensor(
                        out=s_all[:, hg * 128:(hg + 1) * 128],
                        in0=slot_sb[:, hh * 128:(hh + 1) * 128],
                        scalar=coefbc[:, j * NH + hg:j * NH + hg + 1],
                        in1=s_all[:, hg * 128:(hg + 1) * 128],
                        op0=ALU.mult, op1=ALU.add)
                vector.drain()
                vector.nop().then_inc(sV, 1)
            cV.add(1)
    ev_hin = cV.v
    tap("h_in", s_all[:, :])

    if phase_limit < 3:
        return _finish(nc, block, es, taps, sV, cV, sG, cG, [], out_t,
                       extra_ctx=[pjs, phA_ctx, lnxt_ctx, sctx], all_sems=all_dma_sems)

    pjs.close()
    phA_ctx.close()
    lnxt_ctx.close()
    yctx = ExitStack()

    def ysb(name, shape, dtype):
        return yctx.enter_context(nc.sbuf_tensor(name, shape, dtype))

    xtmc = ysb("xtmc", [128, 2 * DSSM], BF16)
    szc = ysb("szc", [128, 2 * DSSM], BF16)
    yg_all = ysb("yg_all", [128, DSSM], BF16)
    e_sb = ysb("e_sb", [128, 2 * 128], BF16)
    m1_sb = ysb("m1_sb", [128, 2 * 128], BF16)
    identd = ysb("identd", [128, NH * 128], BF16)
    mv_sb = ysb("mv_sb", [128, 2 * 128], BF16)
    sbf_sb = ysb("sbf_sb", [128, 2 * 128], BF16)
    xw2_sb = ysb("xw2_sb", [128, 2 * 128], BF16)
    yg2_sb = ysb("yg2_sb", [128, 2 * 128], BF16)
    ynt_sb = ysb("ynt_sb", [128, 2 * 128], BF16)
    rms_sb = ysb("rms_sb", [1, 128], F32)
    lbt_sb = ysb("lbt_sb", [128, 2 * 128], F32)
    cbt_sb = ysb("cbt_sb", [128, 2 * 128], F32)
    rmsb_sb = ysb("rmsb_sb", [128, 128], F32)

    pyc = ExitStack()
    pA = [pyc.enter_context(nc.psum_tensor(f"pA{i}", [128, 512], F32)) for i in range(2)]
    pY = [pyc.enter_context(nc.psum_tensor(f"pY{i}", [128, 512], F32)) for i in range(2)]
    pss = pyc.enter_context(nc.psum_tensor("pss", [128, 512], F32))

    v0, s0, p0 = cV.v + 1, cS.v, cP.v
    n0 = [cNp[0].v, cNp[1].v]
    y0 = [cYp[0].v, cYp[1].v]
    tot_sT = [c_.v for c_ in cTp]
    peA_ev, peB_ev, peC_ev = {}, {}, {}
    v0t_ev, vA_ev, vB_ev, scA_ev, scB_ev = {}, {}, {}, {}, {}
    g1_ev, g2_ev = {}, {}
    ev_cdma, ev_rms_sc, ev_recip, ev_rmsbmm, ev_rmsb = {}, {}, {}, {}, {}
    ev_yn_stt, ev_yn_dma = {}, {}

    pe_seq = []
    for c in range(NCHUNK):
        pe_seq += [("A", c * 64), ("A", c * 64 + 1)]
        for hh in range(NH):
            t = c * 64 + hh
            pe_seq.append(("B", t))
            if hh < 62:
                pe_seq.append(("A", t + 2))
            pe_seq.append(("C", t))
        pe_seq.append(("R", c))
    pv = p0
    for kind, t in pe_seq:
        pv += 1
        if kind == "A":
            peA_ev[t] = pv
        elif kind == "B":
            peB_ev[t] = pv
        elif kind == "C":
            peC_ev[t] = pv
        else:
            ev_rmsbmm[t] = pv
    for c in range(NCHUNK):
        base = v0 + c * 258
        v0t_ev[c * 64 + 0] = base + 1
        v0t_ev[c * 64 + 1] = base + 2
        for hh in range(NH):
            t = c * 64 + hh
            if hh < 62:
                v0t_ev[t + 2] = base + 2 + hh * 3 + 1
                vA_ev[t] = base + 2 + hh * 3 + 2
                vB_ev[t] = base + 2 + hh * 3 + 3
            else:
                vA_ev[t] = base + 2 + 62 * 3 + (hh - 62) * 2 + 1
                vB_ev[t] = base + 2 + 62 * 3 + (hh - 62) * 2 + 2
            scA_ev[t] = s0 + c * 129 + hh * 2 + 1
            scB_ev[t] = s0 + c * 129 + hh * 2 + 2
            ev_yn_stt[(c, hh)] = base + 194 + hh + 1
            ev_yn_dma[(c, hh)] = n0[hh % 2] + (c * 32 + hh // 2 + 1) * 16
        ev_cdma[c] = y0[c % 2] + (c // 2 + 1) * 32
        ev_rms_sc[c] = s0 + c * 129 + 129
        ev_recip[c] = base + 193
        ev_rmsb[c] = base + 194

    @block.vector
    def _(vector):
        for hh in range(NH):
            vector.tensor_scalar(out=identd[:, hh * 128:(hh + 1) * 128],
                                 in0=identb[:], scalar1=dparc[:, hh:hh + 1],
                                 scalar2=None, op0=ALU.mult)
        vector.drain()
        vector.nop().then_inc(sV, 1)
    cV.add(1)
    assert cV.v == v0

    for c in range(NCHUNK):
        qc = c % 2

        @block.sync
        def _(sync, c=c, qc=qc):
            if c == 0:
                sync.wait_ge(sV, ev_hin)
                sync.wait_ge(sZp[0], ev_sz_dma[62])
                sync.wait_ge(sZp[1], ev_sz_dma[63])
                for ii in range(4):
                    sync.wait_ge(sTp[ii], tot_sT[ii])
            if c >= 2:
                sync.wait_ge(sV, vB_ev[(c - 2) * 64 + 63])
            sync.dma_start(
                out=xtmc[:, qc * DSSM:(qc + 1) * DSSM],
                in_=xtm_st[c].rearrange("h p t -> p h t")).then_inc(sYp[qc], 16)
            sync.dma_start(
                out=szc[:, qc * DSSM:(qc + 1) * DSSM],
                in_=sz_st[c].rearrange("h p t -> p h t")).then_inc(sYp[qc], 16)
        cYp[qc].add(32)
        assert cYp[qc].v == ev_cdma[c]

        @block.tensor
        def _(tensor, c=c, qc=qc):
            def emit_A(t):
                hh = t - c * 64
                q = t % 2
                if hh == 0:
                    tensor.wait_ge(sYp[qc], ev_cdma[c])
                    if c >= 1:
                        tensor.wait_ge(sV, ev_rmsb[c - 1])
                tensor.wait_ge(sV, v0t_ev[t])
                if t >= 2:
                    tensor.wait_ge(sV, vB_ev[t - 2])
                    tensor.wait_ge(sS, scA_ev[t - 2])
                tensor.matmul(pA[q][:, 0:128], b_fm[:, c * 128:c * 128 + 128],
                              c_fm[:, c * 128:c * 128 + 128], start=True, stop=True)
                tensor.transpose(pA[q][:, 128:256],
                                 lbt_sb[:, q * 128:q * 128 + 128], identf[:])
                tensor.transpose(pA[q][:, 256:384],
                                 cbt_sb[:, q * 128:q * 128 + 128],
                                 identf[:]).then_inc(sP, 1)
                assert cP.add(1) == peA_ev[t]

            def emit_B(t):
                hh = t - c * 64
                q = t % 2
                tensor.wait_ge(sV, vA_ev[t])
                tensor.matmul(pY[q][:, 0:128],
                              xtmc[:, qc * DSSM + hh * 128:qc * DSSM + hh * 128 + 128],
                              m1_sb[:, q * 128:q * 128 + 128], start=True, stop=False)
                tensor.matmul(pY[q][:, 0:128],
                              xtmc[:, qc * DSSM + hh * 128:qc * DSSM + hh * 128 + 128],
                              identd[:, hh * 128:(hh + 1) * 128], start=False, stop=False)
                tensor.matmul(pY[q][:, 0:128], sbf_sb[:, q * 128:q * 128 + 128],
                              mv_sb[:, q * 128:q * 128 + 128], start=False, stop=True)
                tensor.matmul(pA[q][:, 384:512], b_tm[:, c * 128:(c + 1) * 128],
                              xw2_sb[:, q * 128:q * 128 + 128],
                              start=True, stop=True).then_inc(sP, 1)
                assert cP.add(1) == peB_ev[t]

            def emit_C(t):
                hh = t - c * 64
                q = t % 2
                tensor.wait_ge(sS, scB_ev[t])
                tensor.matmul(pss[0:1, 0:128], onescol_bf[:],
                              yg2_sb[:, q * 128:q * 128 + 128],
                              start=(hh == 0), stop=(hh == 63)).then_inc(sP, 1)
                assert cP.add(1) == peC_ev[t]

            emit_A(c * 64)
            emit_A(c * 64 + 1)
            for hh in range(NH):
                t = c * 64 + hh
                emit_B(t)
                if hh < 62:
                    emit_A(t + 2)
                emit_C(t)
            tensor.wait_ge(sV, ev_recip[c])
            tensor.matmul(pss[:, 128:256], onesrow[:], rms_sb[0:1, :],
                          start=True, stop=True).then_inc(sP, 1)
            assert cP.add(1) == ev_rmsbmm[c]

        @block.scalar
        def _(scalar, c=c):
            for hh in range(NH):
                t = c * 64 + hh
                q = t % 2
                scalar.wait_ge(sP, peA_ev[t])
                scalar.activation(out=e_sb[:, q * 128:q * 128 + 128],
                                  in_=pA[q][:, 128:256], func=AF.Exp,
                                  bias=lndtm[:, c * 64 + hh:c * 64 + hh + 1]
                                  ).then_inc(sS, 1)
                assert cS.add(1) == scA_ev[t]
                scalar.wait_ge(sV, vB_ev[t])
                if t >= 2:
                    scalar.wait_ge(sP, peC_ev[t - 2])
                scalar.activation(out=yg2_sb[:, q * 128:q * 128 + 128],
                                  in_=yg_all[:, hh * 128:hh * 128 + 128],
                                  func=AF.Square).then_inc(sS, 1)
                assert cS.add(1) == scB_ev[t]
            scalar.wait_ge(sP, peC_ev[c * 64 + 63])
            scalar.activation(out=rms_sb[:], in_=pss[0:1, 0:128], func=AF.Sqrt,
                              scale=1.0 / DSSM, bias=cst[0:1, 0:1]).then_inc(sS, 1)
            assert cS.add(1) == ev_rms_sc[c]

        @block.vector
        def _(vector, c=c, qc=qc):
            def emit_v0(t):
                hh = t - c * 64
                q = t % 2
                if t >= 2:
                    vector.wait_ge(sP, peA_ev[t - 2])
                vector.tensor_scalar(out=lbt_sb[:, q * 128:q * 128 + 128],
                                     in0=maskT_sb[:],
                                     scalar1=ltm[:, c * 64 + hh:c * 64 + hh + 1],
                                     scalar2=None, op0=ALU.add)
                vector.tensor_scalar(out=cbt_sb[:, q * 128:q * 128 + 128],
                                     in0=maskT_sb[:], scalar1=0.0,
                                     scalar2=cumtm[:, c * 64 + hh:c * 64 + hh + 1],
                                     op0=ALU.mult, op1=ALU.add).then_inc(sV, 1)
                assert cV.add(1) == v0t_ev[t]

            emit_v0(c * 64)
            emit_v0(c * 64 + 1)
            for hh in range(NH):
                t = c * 64 + hh
                q = t % 2
                if hh < 62:
                    emit_v0(t + 2)
                vector.wait_ge(sS, scA_ev[t])
                if t >= 64:
                    vector.wait_ge(sV, vB_ev[t - 64])
                vector.tensor_copy(out=sbf_sb[:, q * 128:q * 128 + 128],
                                   in_=s_all[:, hh * 128:hh * 128 + 128])
                vector.tensor_scalar(
                    out=xw2_sb[:, q * 128:q * 128 + 128],
                    in0=xtmc[:, qc * DSSM + hh * 128:qc * DSSM + hh * 128 + 128],
                    scalar1=wctm[:, c * 64 + hh:c * 64 + hh + 1],
                    scalar2=None, op0=ALU.mult)
                vector.tensor_tensor(out=m1_sb[:, q * 128:q * 128 + 128],
                                     in0=e_sb[:, q * 128:q * 128 + 128],
                                     in1=pA[q][:, 0:128], op=ALU.mult)
                vector.tensor_tensor(out=mv_sb[:, q * 128:q * 128 + 128],
                                     in0=c_fm[:, c * 128:c * 128 + 128],
                                     in1=pA[q][:, 256:384], op=ALU.mult).then_inc(sV, 1)
                assert cV.add(1) == vA_ev[t]
                vector.wait_ge(sP, peB_ev[t])
                vector.scalar_tensor_tensor(
                    out=s_all[:, hh * 128:hh * 128 + 128],
                    in0=s_all[:, hh * 128:hh * 128 + 128],
                    scalar=dbc[:, c * 64 + hh:c * 64 + hh + 1],
                    in1=pA[q][:, 384:512], op0=ALU.mult, op1=ALU.add)
                vector.tensor_tensor(out=yg_all[:, hh * 128:hh * 128 + 128],
                                     in0=pY[q][:, 0:128],
                                     in1=szc[:, qc * DSSM + hh * 128:qc * DSSM + hh * 128 + 128],
                                     op=ALU.mult).then_inc(sV, 1)
                assert cV.add(1) == vB_ev[t]

        @block.vector
        def _(vector, c=c):
            vector.wait_ge(sS, ev_rms_sc[c])
            vector.reciprocal(out=rms_sb[:], in_=rms_sb[:]).then_inc(sV, 1)
            assert cV.add(1) == ev_recip[c]

        @block.vector
        def _(vector, c=c):
            vector.wait_ge(sP, ev_rmsbmm[c])
            vector.tensor_copy(out=rmsb_sb[:], in_=pss[:, 128:256]).then_inc(sV, 1)
            assert cV.add(1) == ev_rmsb[c]

        @block.vector
        def _(vector, c=c):
            for hh in range(NH):
                if hh == 0:
                    vector.wait_ge(sV, ev_rmsb[c])
                if hh >= 2:
                    vector.wait_ge(sNp[hh % 2], ev_yn_dma[(c, hh - 2)])
                elif c >= 1:
                    vector.wait_ge(sNp[hh % 2], ev_yn_dma[(c - 1, 62 + hh)])
                vector.scalar_tensor_tensor(
                    out=ynt_sb[:, (hh % 2) * 128:(hh % 2) * 128 + 128],
                    in0=yg_all[:, hh * 128:hh * 128 + 128],
                    scalar=gnormc[:, hh:hh + 1], in1=rmsb_sb[:],
                    op0=ALU.mult, op1=ALU.mult).then_inc(sV, 1)
                assert cV.add(1) == ev_yn_stt[(c, hh)]

        @block.scalar
        def _(scalar, c=c):
            for hh in range(NH):
                scalar.wait_ge(sV, ev_yn_stt[(c, hh)])
                scalar.dma_start(out=yn_st[hh][:, c * 128:c * 128 + 128],
                                 in_=ynt_sb[:, (hh % 2) * 128:(hh % 2) * 128 + 128]
                                 ).then_inc(sNp[hh % 2], 16)
                assert cNp[hh % 2].add(16) == ev_yn_dma[(c, hh)]

    tap("yg7", yg_all[:, :])
    tap("s_end", s_all[:, :])

    if phase_limit < 4:
        return _finish(nc, block, es, taps, sV, cV, sG, cG, [], out_t,
                       extra_ctx=[pyc, yctx, sctx], all_sems=all_dma_sems)

    yctx.close()
    pyc.close()
    sctx.close()
    msctx.close()
    wctx = ExitStack()

    def wsb(name, shape, dtype):
        return wctx.enter_context(nc.sbuf_tensor(name, shape, dtype))

    ynp = wsb("ynp", [128, NH * 512], BF16)
    wobf = wsb("wobf", [128, 2 * 4096], BF16)
    attn_f = wsb("attn_f", [128, 512], F32)
    xres_sb = wsb("xres_sb", [128, 2 * 128], F32)
    x1t_sb = wsb("x1t_sb", [128, 2 * 128], F32)

    pwc = ExitStack()
    pw = [pwc.enter_context(nc.psum_tensor(f"pw{i}", [128, 512], F32)) for i in range(4)]

    ev_wold = {}
    ev_wocast = {}
    ev_att_ev = {}
    ev_x1dma = {}
    last_yn0 = cNp[0].v
    last_yn1 = cNp[1].v

    for tg in range(2):
        @block.sync
        def _(sync, tg=tg):
            if tg == 0:
                sync.wait_ge(sNp[0], last_yn0)
                sync.wait_ge(sNp[1], last_yn1)
            else:
                sync.wait_ge(sV, cV.v)
            for hh in range(NH):
                sync.dma_start(out=ynp[:, hh * 512:(hh + 1) * 512],
                               in_=yn_st[hh][:, tg * 512:(tg + 1) * 512]
                               ).then_inc(sYp[tg], 16)
        cYp[tg].add(16 * NH)
        ev_ynp = (sYp[tg], cYp[tg].v)

        for ng in range(8):
            for kg in range(8):
                si = (tg * 8 + ng) * 8 + kg
                qw = si % 2

                @block.gpsimd
                def _(gpsimd, tg=tg, ng=ng, kg=kg, si=si, qw=qw):
                    if si < 2:
                        gpsimd.wait_ge(sNp[0], last_yn0)
                        gpsimd.wait_ge(sNp[1], last_yn1)
                    if si >= 2:
                        gpsimd.wait_ge(sP, _later(ev_att_ev, si - 2, cP))
                    gpsimd.dma_start(
                        out=wobf[:, qw * 4096:(qw + 1) * 4096],
                        in_=wo[ng, kg * 8:(kg + 1) * 8].rearrange(
                            "k p n -> p k n")).then_inc(sWp[qw], 16)
                cWp[qw].add(16)
                ev_wold[si] = (sWp[qw], cWp[qw].v)

                @block.tensor
                def _(tensor, tg=tg, ng=ng, kg=kg, si=si, qw=qw, ev_ynp=ev_ynp):
                    tensor.wait_ge(ev_wold[si][0], ev_wold[si][1])
                    if kg == 0:
                        tensor.wait_ge(ev_ynp[0], ev_ynp[1])
                        if not (tg == 0 and ng == 0):
                            tensor.wait_ge(sV, cV.v)
                    last = None
                    for kk8 in range(8):
                        kc = kg * 8 + kk8
                        for ncx in range(4):
                            last = tensor.matmul(
                                pw[ncx][:, 0:512],
                                wobf[:, qw * 4096 + kk8 * 512 + ncx * 128:
                                     qw * 4096 + kk8 * 512 + ncx * 128 + 128],
                                ynp[:, kc * 512:(kc + 1) * 512],
                                start=(kc == 0), stop=(kc == 63))
                    last.then_inc(sP, 1)
                cP.add(1)
                ev_att_ev[si] = cP.v

            for ncx in range(4):
                @block.vector
                def _(vector, ncx=ncx, w=ev_att_ev[(tg * 8 + ng) * 8 + 7]):
                    vector.wait_ge(sP, w)
                    vector.tensor_copy(out=attn_f[:], in_=pw[ncx][:, 0:512]).then_inc(sV, 1)
                cV.add(1)
                ev_evac = cV.v

                for tb in range(4):
                    ti = ncx * 4 + tb
                    qt = ti % 2

                    @block.tensor
                    def _(tensor, tb=tb, qt=qt, w=ev_evac, ti=ti, tg=tg, ng=ng):
                        tensor.wait_ge(sV, w)
                        if ti >= 2 or not (tg == 0 and ng == 0):
                            tensor.wait_ge(sV, cV.v)
                        dstp = pt if qt == 0 else ph
                        tensor.transpose(dstp[:, 0:128],
                                         attn_f[:, tb * 128:tb * 128 + 128],
                                         identf[:]).then_inc(sP, 1)
                    cP.add(1)
                    ev_tp = cP.v

                    @block.scalar
                    def _(scalar, tg=tg, ng=ng, ncx=ncx, tb=tb, qt=qt, wv=cV.v):
                        scalar.wait_ge(sV, wv)
                        r0 = 3 + tg * 512 + tb * 128
                        c0 = ng * 512 + ncx * 128
                        scalar.dma_start(out=xres_sb[:, qt * 128:qt * 128 + 128],
                                         in_=xs[r0:r0 + 128, c0:c0 + 128]
                                         ).then_inc(sXp[qt], 16)
                    cXp[qt].add(16)
                    ev_xr = (sXp[qt], cXp[qt].v)

                    @block.vector
                    def _(vector, qt=qt, w1=ev_tp, w2=ev_xr,
                          w3=ev_x1dma.get(len(ev_x1dma) - 2)):
                        vector.wait_ge(sP, w1)
                        vector.wait_ge(w2[0], w2[1])
                        if w3 is not None:
                            vector.wait_ge(w3[0], w3[1])
                        srcp = pt if qt == 0 else ph
                        vector.tensor_add(x1t_sb[:, qt * 128:qt * 128 + 128],
                                          srcp[:, 0:128],
                                          xres_sb[:, qt * 128:qt * 128 + 128])
                        vector.drain()
                        vector.nop().then_inc(sV, 1)
                    cV.add(1)

                    @block.sync
                    def _(sync, tg=tg, ng=ng, ncx=ncx, tb=tb, qt=qt, w=cV.v):
                        sync.wait_ge(sV, w)
                        r0 = tg * 512 + tb * 128
                        c0 = ng * 512 + ncx * 128
                        sync.dma_start(out=x1_st[r0:r0 + 128, c0:c0 + 128],
                                       in_=x1t_sb[:, qt * 128:qt * 128 + 128]
                                       ).then_inc(sTp[qt], 16)
                    cTp[qt].add(16)
                    ev_x1dma[len(ev_x1dma)] = (sTp[qt], cTp[qt].v)

    ev_x1_done = [(sTp[0], cTp[0].v), (sTp[1], cTp[1].v)]
    tap("x1st", x1_st[:, :])

    if phase_limit < 5:
        return _finish(nc, block, es, taps, sV, cV, sG, cG, [], out_t,
                       extra_ctx=[pwc, wctx], all_sems=all_dma_sems)

    wctx.close()
    pwc.close()
    fctx = ExitStack()

    def fsb(name, shape, dtype):
        return fctx.enter_context(nc.sbuf_tensor(name, shape, dtype))

    qxT = fsb("qxT", [128, 32 * 1024], BF16)
    x1row2 = fsb("x1row2", [128, 2 * 4096], F32)
    qxr = fsb("qxr", [128, 2 * 4096], F32)
    qs_f = fsb("qs_f", [128, 4096], F32)
    qi_i = fsb("qi_i", [128, 4096], I32)
    q256a = fsb("q256a", [128, 256], F32)
    q256b = fsb("q256b", [128, 256], F32)
    q256c = fsb("q256c", [128, 256], F32)
    q256d = fsb("q256d", [128, 256], F32)
    qwT_sb = fsb("qwT_sb", [128, 32 * 128], BF16)
    ffn_f = fsb("ffn_f", [128, 1024], F32)
    ost_sb = fsb("ost_sb", [128, 2 * 128], F32)
    x1r_sb = fsb("x1r_sb", [128, 2 * 128], F32)
    lnq_small = fsb("lnq_small", [128, 16], F32)

    pfc = ExitStack()
    pf = [pfc.enter_context(nc.psum_tensor(f"pf{i}", [128, 512], F32)) for i in range(2)]

    ev_q_src_done = {}
    ev_qr_done = {}
    ev_bfp_done = {}
    ev_ld = {}
    ev_outdma = {}
    ev_fev = {}
    ev_fmm = {}

    for tc in range(NCHUNK):
        i = tc
        qp = i % 2

        @block.scalar
        def _(scalar, tc=tc, i=i, qp=qp):
            if tc == 0:
                for s_, v_ in ev_x1_done:
                    scalar.wait_ge(s_, v_)
            if i >= 2:
                scalar.wait_ge(sV, ev_q_src_done[i - 2])
            scalar.dma_start(out=x1row2[:, qp * 4096:(qp + 1) * 4096],
                             in_=x1_st[tc * 128:(tc + 1) * 128, :]).then_inc(sXp[qp], 16)
        cXp[qp].add(16)
        ev_ld[i] = (sXp[qp], cXp[qp].v)

        @block.scalar
        def _(scalar, i=i, qp=qp, w=ev_ld[i]):
            scalar.wait_ge(w[0], w[1])
            if i >= 2:
                scalar.wait_ge(sP, ev_qr_done[i - 2])
            scalar.activation(out=qxr[:, qp * 4096:(qp + 1) * 4096],
                              in_=x1row2[:, qp * 4096:(qp + 1) * 4096],
                              func=AF.Square).then_inc(sS, 1)
        cS.add(1)

        @block.vector
        def _(vector, i=i, qp=qp, w=ev_ld[i], w2=cS.v):
            vector.wait_ge(w[0], w[1])
            vector.wait_ge(sS, w2)
            if i >= 2:
                vector.wait_ge(sP, ev_qr_done[i - 2])
            xt = x1row2[:, qp * 4096:(qp + 1) * 4096]
            vector.reduce_sum(lnq_small[:, 1:2], qxr[:, qp * 4096:(qp + 1) * 4096],
                              axis=AX)
            vector.reduce_sum(lnq_small[:, 0:1], xt, axis=AX)
            vector.drain()
            vector.tensor_scalar(out=lnq_small[:, 1:2], in0=lnq_small[:, 1:2],
                                 scalar1=1.0 / H, scalar2=None, op0=ALU.mult)
            vector.tensor_scalar(out=lnq_small[:, 0:1], in0=lnq_small[:, 0:1],
                                 scalar1=1.0 / H, scalar2=None, op0=ALU.mult)
            vector.drain()
            vector.tensor_tensor(out=lnq_small[:, 2:3], in0=lnq_small[:, 0:1],
                                 in1=lnq_small[:, 0:1], op=ALU.mult)
            vector.tensor_scalar(out=lnq_small[:, 5:6], in0=lnq_small[:, 0:1],
                                 scalar1=-1.0, scalar2=None, op0=ALU.mult)
            vector.drain()
            vector.tensor_sub(lnq_small[:, 2:3], lnq_small[:, 1:2], lnq_small[:, 2:3])
            vector.drain()
            vector.nop().then_inc(sV, 1)
        cV.add(1)

        @block.scalar
        def _(scalar, w=cV.v):
            scalar.wait_ge(sV, w)
            scalar.activation(out=lnq_small[:, 3:4], in_=lnq_small[:, 2:3],
                              func=AF.Sqrt, bias=cst[:, 0:1]).then_inc(sS, 1)
        cS.add(1)

        @block.vector
        def _(vector, i=i, qp=qp, w=cS.v):
            vector.wait_ge(sS, w)
            vector.reciprocal(out=lnq_small[:, 4:5], in_=lnq_small[:, 3:4])
            vector.drain()
            vector.tensor_scalar(out=qxr[:, qp * 4096:(qp + 1) * 4096],
                                 in0=x1row2[:, qp * 4096:(qp + 1) * 4096],
                                 scalar1=lnq_small[:, 5:6], scalar2=lnq_small[:, 4:5],
                                 op0=ALU.add, op1=ALU.mult)
            vector.drain()
            vector.nop().then_inc(sV, 1)
        cV.add(1)
        ev_q_src_done[i] = cV.v

        _emit_bfp(block, sV, cV, sS, cS, qxr[:, qp * 4096:(qp + 1) * 4096],
                  qs_f, qi_i, q256a, q256b, q256c, q256d,
                  qxr[:, qp * 4096:(qp + 1) * 4096], INVLN2, LN2C, cst)
        ev_bfp_done[i] = cV.v

        tev = []
        for fc in range(32):
            @block.tensor
            def _(tensor, tc=tc, fc=fc, qp=qp, w=cV.v):
                if fc == 0:
                    tensor.wait_ge(sV, w)
                if fc >= 2:
                    tensor.wait_ge(sS, tev[fc - 2])
                dstp = pt if fc % 2 == 0 else ph
                tensor.transpose(dstp[:, 0:128],
                                 qxr[:, qp * 4096 + fc * 128:qp * 4096 + fc * 128 + 128],
                                 identf[:]).then_inc(sP, 1)
            cP.add(1)

            @block.scalar
            def _(scalar, tc=tc, fc=fc, w=cP.v):
                scalar.wait_ge(sP, w)
                srcp = pt if fc % 2 == 0 else ph
                scalar.activation(out=qxT[:, fc * 1024 + tc * 128:fc * 1024 + tc * 128 + 128],
                                  in_=srcp[:, 0:128],
                                  func=AF.Identity).then_inc(sS, 1)
            cS.add(1)
            tev.append(cS.v)
        ev_qr_done[i] = cP.v
    ev_qx_done = cV.v

    def emit_outstage(ncx):
        for tb in range(8):
            qt = tb % 2

            @block.tensor
            def _(tensor, ncx=ncx, tb=tb, qt=qt, w=ev_fev[ncx]):
                tensor.wait_ge(sS, w)
                if tb >= 2:
                    tensor.wait_ge(sV, cV.v)
                dstp = pt if qt == 0 else ph
                tensor.transpose(dstp[:, 128:256], ffn_f[:, tb * 128:tb * 128 + 128],
                                 identf[:]).then_inc(sP, 1)
            cP.add(1)
            ev_otp = cP.v

            @block.scalar
            def _(scalar, ncx=ncx, tb=tb, qt=qt, wv=cV.v):
                scalar.wait_ge(sV, wv)
                scalar.dma_start(out=x1r_sb[:, qt * 128:qt * 128 + 128],
                                 in_=x1_st[tb * 128:(tb + 1) * 128,
                                           ncx * 128:(ncx + 1) * 128]
                                 ).then_inc(sXp[qt], 16)
            cXp[qt].add(16)
            ev_x1r = (sXp[qt], cXp[qt].v)

            @block.vector
            def _(vector, qt=qt, w1=ev_otp, w2=ev_x1r,
                  w3=ev_outdma.get(len(ev_outdma) - 2)):
                vector.wait_ge(sP, w1)
                vector.wait_ge(w2[0], w2[1])
                if w3 is not None:
                    vector.wait_ge(w3[0], w3[1])
                srcp = pt if qt == 0 else ph
                vector.tensor_add(ost_sb[:, qt * 128:qt * 128 + 128],
                                  srcp[:, 128:256], x1r_sb[:, qt * 128:qt * 128 + 128])
                vector.nop().then_inc(sV, 1)
            cV.add(1)

            @block.sync
            def _(sync, ncx=ncx, tb=tb, qt=qt, w=cV.v):
                sync.wait_ge(sV, w)
                sync.dma_start(out=out_t[tb * 128:(tb + 1) * 128,
                                         ncx * 128:(ncx + 1) * 128],
                               in_=ost_sb[:, qt * 128:qt * 128 + 128]
                               ).then_inc(sOp[qt], 16)
            cOp[qt].add(16)
            ev_outdma[len(ev_outdma)] = (sOp[qt], cOp[qt].v)

    def emit_ffn_tail(n):
        @block.scalar
        def _(scalar, ncx=n, w=ev_fmm[n], wv=cV.v):
            scalar.wait_ge(sP, w)
            if ncx >= 1:
                scalar.wait_ge(sV, wv)
            for tg in range(2):
                scalar.activation(out=ffn_f[:, tg * 512:(tg + 1) * 512],
                                  in_=pf[tg][:, 0:512], func=AF.Identity,
                                  bias=fbiasc[:, ncx:ncx + 1])
            scalar.drain()
            scalar.nop().then_inc(sS, 1)
        cS.add(1)
        ev_fev[n] = cS.v
        emit_outstage(n)

    for ncx in range(32):
        i = 8 + ncx
        qp = i % 2

        @block.scalar
        def _(scalar, ncx=ncx, i=i, qp=qp):
            if i >= 2:
                scalar.wait_ge(sV, ev_q_src_done[i - 2])
            scalar.dma_start(out=x1row2[:, qp * 4096:(qp + 1) * 4096],
                             in_=wf[ncx * 128:(ncx + 1) * 128, :]).then_inc(sXp[qp], 16)
        cXp[qp].add(16)
        ev_ld[i] = (sXp[qp], cXp[qp].v)

        @block.vector
        def _(vector, i=i, qp=qp, w=ev_ld[i]):
            vector.wait_ge(w[0], w[1])
            if i >= 2:
                vector.wait_ge(sP, ev_qr_done[i - 2])
            vector.nop().then_inc(sV, 1)
        cV.add(1)

        _emit_bfp(block, sV, cV, sS, cS, x1row2[:, qp * 4096:(qp + 1) * 4096],
                  qs_f, qi_i, q256a, q256b, q256c, q256d,
                  qxr[:, qp * 4096:(qp + 1) * 4096], INVLN2, LN2C, cst)
        ev_q_src_done[i] = cV.v

        tev = []
        for kk in range(32):
            @block.tensor
            def _(tensor, kk=kk, qp=qp, w=cV.v):
                if kk == 0:
                    tensor.wait_ge(sV, w)
                if kk >= 2:
                    tensor.wait_ge(sS, tev[kk - 2])
                dstp = pt if kk % 2 == 0 else ph
                tensor.transpose(dstp[:, 0:128],
                                 qxr[:, qp * 4096 + kk * 128:qp * 4096 + kk * 128 + 128],
                                 identf[:]).then_inc(sP, 1)
            cP.add(1)

            @block.scalar
            def _(scalar, kk=kk, w=cP.v):
                scalar.wait_ge(sP, w)
                srcp = pt if kk % 2 == 0 else ph
                scalar.activation(out=qwT_sb[:, kk * 128:kk * 128 + 128],
                                  in_=srcp[:, 0:128],
                                  func=AF.Identity).then_inc(sS, 1)
            cS.add(1)
            tev.append(cS.v)
        ev_qr_done[i] = cP.v
        ev_qwT = cS.v

        if ncx >= 1:
            emit_ffn_tail(ncx - 1)

        @block.tensor
        def _(tensor, ncx=ncx, w=ev_qwT):
            tensor.wait_ge(sS, w)
            if ncx >= 1:
                tensor.wait_ge(sS, ev_fev[ncx - 1])
            last = None
            for kk in range(32):
                for tg in range(2):
                    last = tensor.matmul(
                        pf[tg][:, 0:512], qwT_sb[:, kk * 128:kk * 128 + 128],
                        qxT[:, kk * 1024 + tg * 512:kk * 1024 + tg * 512 + 512],
                        start=(kk == 0), stop=(kk == 31))
            last.then_inc(sP, 1)
        cP.add(1)
        ev_fmm[ncx] = cP.v

    emit_ffn_tail(31)

    return _finish(nc, block, es, taps, sV, cV, sG, cG,
                   [(sOp[0], cOp[0].v), (sOp[1], cOp[1].v)], out_t,
                   extra_ctx=[pfc, fctx], all_sems=all_dma_sems)


def _later(d, t, c=None):
    return d.get(t, 0) if isinstance(d, dict) else d[t]


def _dt_math(block, L, sV, cV, sS, cS, sP, cP, upto=99):
    nc_l = L
    dt_raw = L["dt_raw"]; dtb_sb = L["dtb_sb"]; dt_sb = L["dt_sb"]
    lndt_sb = L["lndt_sb"]; alog_sb = L["alog_sb"]; lda_sb = L["lda_sb"]
    ll_sb = L["ll_sb"]; offs_sb = L["offs_sb"]
    ltot_sb = L["ltot_sb"]; wtm = L["wtm"]
    wctm = L["wctm"]; lndtm = L["lndtm"]; dbc = L["dbc"]
    etot_sb = L["etot_sb"]; cumtm = L["cumtm"]; ltm = L["ltm"]
    sm_a = L["sm_a"]; sm_b = L["sm_b"]; dslrow = L["dslrow"]
    identf = L["identf"]; onesrow = L["onesrow"]
    pt = L["pt"]; ph = L["ph"]
    evfree_dt = L["evfree"][0]

    @block.scalar
    def _(scalar):
        scalar.wait_ge(sV, evfree_dt[1])
        scalar.activation(out=dt_sb[:], in_=dt_raw[:], func=AF.Abs, bias=dtb_sb[:])
        scalar.drain()
        scalar.activation(out=dt_sb[:], in_=dt_sb[:], func=AF.Exp, scale=-1.0)
        scalar.drain()
        scalar.activation(out=dt_sb[:], in_=dt_sb[:], func=AF.Ln, bias=1.0)
        scalar.activation(out=lda_sb[:], in_=dt_raw[:], func=AF.Relu, bias=dtb_sb[:])
        scalar.activation(out=sm_a[:, 0:1], in_=alog_sb[:], func=AF.Exp)
        scalar.drain()
        scalar.drain()
        scalar.nop().then_inc(sS, 1)
    cS.add(1)

    if upto < 2:
        return (cV.v, cV.v)

    @block.vector
    def _(vector, w=cS.v):
        vector.wait_ge(sS, w)
        vector.tensor_add(dt_sb[:], dt_sb[:], lda_sb[:])
        vector.tensor_scalar(out=sm_a[:, 1:2], in0=sm_a[:, 0:1], scalar1=-1.0,
                             scalar2=None, op0=ALU.mult)
        vector.memset(offs_sb[:, 15:16], 0.0)
        vector.drain()
        vector.tensor_scalar(out=lda_sb[:], in0=dt_sb[:], scalar1=sm_a[:, 1:2],
                             scalar2=None, op0=ALU.mult)
        vector.drain()
        for c in range(NCHUNK):
            vector.tensor_tensor_scan(
                out=ll_sb[:, c * 128:(c + 1) * 128],
                data0=lda_sb[:, c * 128:(c + 1) * 128],
                data1=offs_sb[:, 15:16].broadcast_to([NH, 128]),
                initial=0.0, op0=ALU.add, op1=ALU.add)
        vector.drain()
        vector.tensor_copy(out=ltot_sb[:], in_=ll_sb[:, 127::128])
        vector.memset(offs_sb[:, 0:1], 0.0)
        vector.drain()
        vector.tensor_tensor_scan(
            out=offs_sb[:, 1:9], data0=ltot_sb[:],
            data1=offs_sb[:, 15:16].broadcast_to([NH, 8]),
            initial=0.0, op0=ALU.add, op1=ALU.add)
        vector.drain()
        vector.tensor_scalar(out=sm_b[:, 0:8],
                             in0=offs_sb[:, 0:8], scalar1=offs_sb[:, 8:9],
                             scalar2=-1.0, op0=ALU.subtract, op1=ALU.mult)
        vector.drain()
        vector.nop().then_inc(sV, 1)
    cV.add(1)

    if upto < 3:
        return (cV.v, cV.v)

    @block.scalar
    def _(scalar, w=cV.v):
        scalar.wait_ge(sV, w)
        scalar.activation(out=lndt_sb[:], in_=dt_sb[:], func=AF.Ln)
        scalar.activation(out=sm_a[:, 2:3], in_=offs_sb[:, 8:9], func=AF.Exp)
        scalar.drain()
        scalar.nop().then_inc(sS, 1)
    cS.add(1)
    ev_base = cS.v

    if upto < 4:
        return (cV.v, cV.v)

    ev_tr = []
    for c in range(NCHUNK):
        @block.scalar
        def _(scalar, c=c, w=ev_base):
            scalar.wait_ge(sS, w)
            if c >= 1:
                scalar.wait_ge(sV, ev_tr[c - 1])
            scalar.activation(out=sm_a[:, 0:128],
                              in_=ll_sb[:, c * 128:(c + 1) * 128], func=AF.Exp,
                              scale=-1.0, bias=sm_b[:, c:c + 1])
            scalar.drain()
            scalar.nop().then_inc(sS, 1)
        cS.add(1)

        @block.vector
        def _(vector, c=c, w=cS.v):
            vector.wait_ge(sS, w)
            vector.tensor_mul(sm_a[:, 0:128], sm_a[:, 0:128],
                              dt_sb[:, c * 128:(c + 1) * 128]).then_inc(sV, 1)
        cV.add(1)

        @block.tensor
        def _(tensor, c=c, w=cV.v):
            tensor.wait_ge(sV, w)
            tensor.transpose(pt[0:128, 0:64], sm_a[0:NH, 0:128],
                             identf[0:NH, 0:NH]).then_inc(sP, 1)
        cP.add(1)

        @block.vector
        def _(vector, c=c, w=cP.v):
            vector.wait_ge(sP, w)
            vector.tensor_copy(out=wtm[:, c * 64:(c + 1) * 64],
                               in_=pt[0:128, 0:64]).then_inc(sV, 1)
        cV.add(1)

        @block.scalar
        def _(scalar, c=c, w=cV.v):
            scalar.wait_ge(sV, w)
            scalar.activation(out=sm_a[:, 0:128],
                              in_=ll_sb[:, c * 128:(c + 1) * 128], func=AF.Exp,
                              scale=-1.0, bias=ltot_sb[:, c:c + 1])
            scalar.drain()
            scalar.nop().then_inc(sS, 1)
        cS.add(1)

        @block.vector
        def _(vector, c=c, w=cS.v):
            vector.wait_ge(sS, w)
            vector.tensor_mul(sm_a[:, 0:128], sm_a[:, 0:128],
                              dt_sb[:, c * 128:(c + 1) * 128])
            vector.tensor_sub(dt_raw[:, 0:128],
                              lndt_sb[:, c * 128:(c + 1) * 128],
                              ll_sb[:, c * 128:(c + 1) * 128])
            vector.drain()
            vector.nop().then_inc(sV, 1)
        cV.add(1)

        @block.tensor
        def _(tensor, c=c, w=cV.v):
            tensor.wait_ge(sV, w)
            tensor.transpose(pt[0:128, 64:128], sm_a[0:NH, 0:128],
                             identf[0:NH, 0:NH])
            tensor.transpose(ph[0:128, 0:64], dt_raw[0:NH, 0:128],
                             identf[0:NH, 0:NH])
            tensor.transpose(ph[0:128, 64:128], ll_sb[0:NH, c * 128:(c + 1) * 128],
                             identf[0:NH, 0:NH]).then_inc(sP, 1)
        cP.add(1)

        @block.vector
        def _(vector, c=c, w=cP.v):
            vector.wait_ge(sP, w)
            vector.tensor_copy(out=wctm[:, c * 64:(c + 1) * 64], in_=pt[0:128, 64:128])
            vector.tensor_copy(out=lndtm[:, c * 64:(c + 1) * 64],
                               in_=ph[0:128, 0:64])
            vector.tensor_copy(out=ltm[:, c * 64:(c + 1) * 64],
                               in_=ph[0:128, 64:128]).then_inc(sV, 1)
        cV.add(1)
        ev_tr.append(cV.v)

        @block.scalar
        def _(scalar, c=c, w=cV.v):
            scalar.wait_ge(sV, w)
            scalar.activation(out=cumtm[:, c * 64:(c + 1) * 64],
                              in_=ltm[:, c * 64:(c + 1) * 64],
                              func=AF.Exp).then_inc(sS, 1)
        cS.add(1)


    if upto < 5:
        return (cV.v, cV.v)

    @block.scalar
    def _(scalar, w=ev_tr[NCHUNK - 1]):
        scalar.wait_ge(sV, w)
        scalar.activation(out=etot_sb[:], in_=ltot_sb[:], func=AF.Exp).then_inc(sS, 1)
    cS.add(1)

    for c in range(NCHUNK):
        @block.vector
        def _(vector, c=c, w=cS.v):
            vector.wait_ge(sS, w)
            if c >= 1:
                vector.wait_ge(sP, cP.v)
            vector.drain()
            vector.tensor_scalar(out=sm_a[0:NH, 0:128], in0=sm_a[0:NH, 0:128],
                                 scalar1=0.0, scalar2=etot_sb[:, c:c + 1],
                                 op0=ALU.mult, op1=ALU.add)
            vector.drain()
            vector.drain()
            vector.nop().then_inc(sV, 1)
        cV.add(1)

        @block.tensor
        def _(tensor, c=c, w=cV.v, ws=cS.v):
            tensor.wait_ge(sV, w)
            tensor.wait_ge(sS, ws)
            tensor.transpose(ph[0:128, 128:192], sm_a[0:NH, 0:128],
                             identf[0:NH, 0:NH]).then_inc(sP, 1)
        cP.add(1)

        @block.vector
        def _(vector, c=c, w=cP.v):
            vector.wait_ge(sP, w)
            vector.tensor_copy(out=dbc[:, c * 64:(c + 1) * 64],
                               in_=ph[0:128, 128:192]).then_inc(sV, 1)
        cV.add(1)

    if upto < 6:
        return (cV.v, cV.v)

    pj0 = L["pj"][0]

    @block.tensor
    def _(tensor, w=cS.v):
        tensor.wait_ge(sS, w)
        tensor.transpose(pj0[0:1, 1280:1344], sm_a[0:NH, 2:3],
                         identf[0:NH, 0:NH]).then_inc(sP, 1)
    cP.add(1)

    @block.vector
    def _(vector, w=cP.v):
        vector.wait_ge(sP, w)
        vector.tensor_copy(out=dslrow[:], in_=pj0[0:1, 1280:1344]).then_inc(sV, 1)
    cV.add(1)
    ev_dsl = cV.v
    ev_smalls_done = cV.v
    return ev_smalls_done, ev_dsl


def _emit_bfp(block, sV, cV, sS, cS, srcf, qs_f, qi_i, q256a, q256b, q256c,
              q256d, dstf, invln2, ln2c, cstq=None):
    @block.vector
    def _(vector, w=cV.v):
        vector.wait_ge(sV, w)
        vector.tensor_reduce(
            out=q256a[:], in_=srcf.rearrange("p (g k) -> p g k", k=16),
            axis=AX, op=ALU.max, apply_absolute_value=True)
        vector.drain()
        vector.tensor_scalar(out=q256a[:], in0=q256a[:], scalar1=1e-38,
                             scalar2=None, op0=ALU.max)
        vector.drain()
        vector.drain()
        vector.nop().then_inc(sV, 1)
    cV.add(1)

    @block.scalar
    def _(scalar, w=cV.v):
        scalar.wait_ge(sV, w)
        scalar.activation(out=q256b[:], in_=q256a[:], func=AF.Ln).then_inc(sS, 1)
    cS.add(1)

    @block.vector
    def _(vector, w=cS.v):
        vector.wait_ge(sS, w)
        vector.tensor_scalar(out=q256b[:], in0=q256b[:], scalar1=invln2,
                             scalar2=None, op0=ALU.mult)
        vector.drain()
        vector.tensor_copy(out=qi_i[:, 0:256], in_=q256b[:])
        vector.drain()
        vector.tensor_copy(out=q256c[:], in_=qi_i[:, 0:256])
        vector.drain()
        vector.tensor_tensor(out=q256d[:], in0=q256c[:], in1=q256b[:], op=ALU.is_gt)
        vector.drain()
        vector.tensor_sub(q256c[:], q256c[:], q256d[:])
        vector.drain()
        vector.drain()
        vector.nop().then_inc(sV, 1)
    cV.add(1)

    @block.scalar
    def _(scalar, w=cV.v):
        scalar.wait_ge(sV, w)
        scalar.activation(out=q256a[:], in_=q256c[:], func=AF.Exp,
                          scale=ln2c, bias=cstq[:, 1:2])
        scalar.activation(out=q256b[:], in_=q256c[:], func=AF.Exp,
                          scale=-ln2c, bias=cstq[:, 2:3])
        scalar.drain()
        scalar.nop().then_inc(sS, 1)
    cS.add(1)

    @block.vector
    def _(vector, w=cS.v):
        vector.wait_ge(sS, w)
        vector.tensor_tensor(
            out=qs_f[:].rearrange("p (g k) -> p g k", k=16),
            in0=srcf.rearrange("p (g k) -> p g k", k=16),
            in1=q256b[:].unsqueeze(2).broadcast_to([128, 256, 16]), op=ALU.mult)
        vector.drain()
        vector.tensor_scalar(out=qs_f[:], in0=qs_f[:], scalar1=127.49999,
                             scalar2=-127.49999, op0=ALU.min, op1=ALU.max)
        vector.drain()
        vector.tensor_copy(out=qi_i[:], in_=qs_f[:])
        vector.drain()
        vector.tensor_tensor(
            out=dstf.rearrange("p (g k) -> p g k", k=16),
            in0=qi_i[:].rearrange("p (g k) -> p g k", k=16),
            in1=q256a[:].unsqueeze(2).broadcast_to([128, 256, 16]),
            op=ALU.mult).then_inc(sV, 1)
    cV.add(1)


def _finish(nc, block, es, taps, sV, cV, sG, cG, out_waits, out_t, extra_ctx=(),
            all_sems=()):
    @block.gpsimd
    def _(gpsimd):
        gpsimd.wait_ge(sV, cV.v)
        for d, src in taps:
            gpsimd.wait_ge(sG, cG.v)
            gpsimd.dma_start(out=d, in_=src).then_inc(sG, 16)
            cG.add(16)
        gpsimd.wait_ge(sG, cG.v)
        for s_, v_ in out_waits:
            gpsimd.wait_ge(s_, v_)
        for s_, c_ in all_sems:
            gpsimd.wait_ge(s_, c_.v)

    for ctx in extra_ctx:
        ctx.close()
    es.close()
    return nc


def _host_prep(inputs):
    x = np.ascontiguousarray(np.asarray(inputs["L0_attn_hidden_states"], np.float32))
    Wi = np.asarray(inputs["L0_attn_W_in"], np.float32)
    convw = np.asarray(inputs["L0_attn_conv_w"], np.float32)
    dtb = np.asarray(inputs["L0_attn_dt_bias"], np.float32)
    alog = np.asarray(inputs["L0_attn_A_log"], np.float32)
    dpar = np.asarray(inputs["L0_attn_D_param"], np.float32)
    gnorm = np.asarray(inputs["L0_attn_norm_weight"], np.float32)
    Wo = np.asarray(inputs["L0_attn_W_out"], np.float32)
    Wf = np.asarray(inputs["L0_ffn_weight"], np.float32)
    fb = np.asarray(inputs["L0_ffn_bias"], np.float32)

    wi_t = np.zeros((H, NCC * 128), np.float32)
    wi_t[:, :16704] = Wi
    wi_t = np.ascontiguousarray(
        wi_t.reshape(2, 16, 128, NCC, 128).transpose(3, 0, 2, 1, 4))
    wo_t = np.ascontiguousarray(
        Wo.reshape(64, 128, 8, 512).transpose(2, 0, 1, 3))
    convw_t = np.ascontiguousarray(convw.reshape(66, 128, 4).transpose(1, 0, 2)
                                   ).reshape(128, 264)
    dpar_bc = np.ascontiguousarray(np.broadcast_to(dpar[None, :], (128, NH)))
    gnorm_t = np.ascontiguousarray(gnorm.reshape(NH, 128).T)
    fbias_t = np.ascontiguousarray(fb.reshape(32, 128).T)
    j_idx = np.arange(128)
    maskbias = np.where(j_idx[:, None] <= j_idx[None, :], 0.0, -1e30).astype(np.float32)
    ident = np.eye(128, dtype=np.float32)

    shared = dict(wi=wi_t, wo=wo_t, wf=Wf, convw_t=convw_t,
                  dtb=dtb.reshape(NH, 1), alog=alog.reshape(NH, 1),
                  dpar=dpar_bc, gnorm=gnorm_t, fbias=fbias_t,
                  maskb=maskbias, maskbT=np.ascontiguousarray(maskbias.T),
                  identb=ident)

    in_maps = []
    for core in range(8):
        b, rk = core // 4, core % 4
        xsl = np.zeros((TOKX, H), np.float32)
        if rk > 0:
            xsl[0:3] = x[b, rk * 1024 - 3:rk * 1024]
        xsl[3:] = x[b, rk * 1024:(rk + 1) * 1024]
        cm = np.zeros((8, 16), np.float32)
        grp = range((core // 4) * 4, (core // 4) * 4 + 4)
        msel = np.zeros(8, np.float32)
        for j in grp:
            rj = j % 4
            if rj < rk:
                msel[j] = 1.0
                for k in grp:
                    if rj < (k % 4) < rk:
                        cm[j, k] = 1.0
        cm[:, 8:16] = 1.0 - cm[:, 0:8]
        cm[:, 15] = msel
        in_maps.append(dict(xs=np.ascontiguousarray(xsl), cmask=cm, **shared))
    return x, in_maps


_NC_CACHE = {}


def kernel(**inputs):
    x, in_maps = _host_prep(inputs)
    key = ("full",)
    if key not in _NC_CACHE:
        _NC_CACHE[key] = build()
    nc = _NC_CACHE[key]
    res = run_bass_kernel_spmd(nc, in_maps, core_ids=list(range(8)))
    out = np.zeros_like(x)
    for core in range(8):
        b, rk = core // 4, core % 4
        out[b, rk * 1024:(rk + 1) * 1024] = res.results[core]["out"]
    return out


# version 15
# speedup vs baseline: 1.1577x; 1.1577x over previous
import numpy as np
from contextlib import ExitStack

import concourse.bass as bass
import concourse.mybir as mybir
from concourse.bass_utils import run_bass_kernel_spmd

F32 = mybir.dt.float32
BF16 = mybir.dt.bfloat16
I32 = mybir.dt.int32
AF = mybir.ActivationFunctionType
ALU = mybir.AluOpType
AX = mybir.AxisListType.X

H = 4096
DSSM = 8192
NH = 64
EPS = 1e-5
CH = 128
NCHUNK = 8
TOK = 1024
TOKX = 1027
LN2C = float(np.log(2.0))
INVLN2 = float(1.0 / np.log(2.0))
NCC = 131
CC_B, CC_C, CC_DT = 128, 129, 130
SLOTS = [0, 1, 2, 4, 5, 6]


class Ct:
    __slots__ = ("v",)

    def __init__(self):
        self.v = 0

    def add(self, n=1):
        self.v += n
        return self.v


def build(phase_limit=99, want_taps=(), max_stripes=None, dbg_mode=99):
    want_taps = set(want_taps)
    nc = bass.Bass("TRN2", target_bir_lowering=False, debug=False)
    es = ExitStack()

    def sb(name, shape, dtype):
        return es.enter_context(nc.sbuf_tensor(name, shape, dtype))

    xs = nc.dram_tensor("xs", [TOKX, H], F32, kind="ExternalInput").ap()
    wi = nc.dram_tensor("wi", [NCC, 2, 128, 16, 128], F32, kind="ExternalInput").ap()
    wo = nc.dram_tensor("wo", [8, 64, 128, 512], F32, kind="ExternalInput").ap()
    wf = nc.dram_tensor("wf", [4096, 4096], F32, kind="ExternalInput").ap()
    convw_in = nc.dram_tensor("convw_t", [128, 264], F32, kind="ExternalInput").ap()
    dtb_in = nc.dram_tensor("dtb", [NH, 1], F32, kind="ExternalInput").ap()
    alog_in = nc.dram_tensor("alog", [NH, 1], F32, kind="ExternalInput").ap()
    dpar_in = nc.dram_tensor("dpar", [128, NH], F32, kind="ExternalInput").ap()
    gnorm_in = nc.dram_tensor("gnorm", [128, NH], F32, kind="ExternalInput").ap()
    fbias_in = nc.dram_tensor("fbias", [128, 32], F32, kind="ExternalInput").ap()
    mask_in = nc.dram_tensor("maskb", [128, 128], F32, kind="ExternalInput").ap()
    maskT_in = nc.dram_tensor("maskbT", [128, 128], F32, kind="ExternalInput").ap()
    ident_in = nc.dram_tensor("identb", [128, 128], F32, kind="ExternalInput").ap()
    cmask_in = nc.dram_tensor("cmask", [8, 16], F32, kind="ExternalInput").ap()
    out_t = nc.dram_tensor("out", [TOK, H], F32, kind="ExternalOutput").ap()

    xtm_st = nc.dram_tensor("xtm_st", [NCHUNK, NH, 128, 128], BF16).ap()
    sz_st = nc.dram_tensor("sz_st", [NCHUNK, NH, 128, 128], BF16).ap()
    yn_st = nc.dram_tensor("yn_st", [NH, 128, TOK], BF16).ap()
    x1_st = nc.dram_tensor("x1_st", [TOK, H], F32).ap()
    cc_in = nc.dram_tensor("cc_in", [128, DSSM], BF16)
    cc_out = nc.dram_tensor("cc_out", [1024, DSSM], BF16)
    ccd_in = nc.dram_tensor("ccd_in", [1, NH], F32)
    ccd_out = nc.dram_tensor("ccd_out", [8, NH], F32)

    taps = []

    def tap(name, src_ap):
        if name in want_taps:
            d = nc.dram_tensor("tap_" + name, list(src_ap.shape), src_ap.dtype,
                               kind="ExternalOutput").ap()
            taps.append((d, src_ap))

    sK = es.enter_context(nc.semaphore("sK"))
    sWp = [es.enter_context(nc.semaphore(f"sW{i}")) for i in range(2)]
    sXp = [es.enter_context(nc.semaphore(f"sX{i}")) for i in range(2)]
    sXc = es.enter_context(nc.semaphore("sXc"))
    sTp = [es.enter_context(nc.semaphore(f"sT{i}")) for i in range(4)]
    sZp = [es.enter_context(nc.semaphore(f"sZ{i}")) for i in range(2)]
    sNp = [es.enter_context(nc.semaphore(f"sN{i}")) for i in range(2)]
    sYp = [es.enter_context(nc.semaphore(f"sY{i}")) for i in range(2)]
    sOp = [es.enter_context(nc.semaphore(f"sO{i}")) for i in range(2)]
    sG = es.enter_context(nc.semaphore("sG"))
    sP = es.enter_context(nc.semaphore("sP"))
    sV = es.enter_context(nc.semaphore("sV"))
    sS = es.enter_context(nc.semaphore("sS"))
    sC = es.enter_context(nc.semaphore("sC"))
    cK = Ct()
    cWp = [Ct(), Ct()]
    cXp = [Ct(), Ct()]
    cXc = Ct()
    cTp = [Ct() for _ in range(4)]
    cZp = [Ct(), Ct()]
    cNp = [Ct(), Ct()]
    cYp = [Ct(), Ct()]
    cOp = [Ct(), Ct()]
    cG, cP, cV, cS, cC = (Ct() for _ in range(5))
    all_dma_sems = ([(sK, cK), (sXc, cXc), (sG, cG)] +
                    list(zip(sWp, cWp)) + list(zip(sXp, cXp)) +
                    list(zip(sTp, cTp)) + list(zip(sZp, cZp)) +
                    list(zip(sNp, cNp)) + list(zip(sYp, cYp)) +
                    list(zip(sOp, cOp)))

    convw_sb = sb("convw_sb", [128, 264], F32)
    dtb_sb = sb("dtb_sb", [NH, 1], F32)
    alog_sb = sb("alog_sb", [NH, 1], F32)
    dparc = sb("dparc", [128, NH], F32)
    gnormc = sb("gnormc", [128, NH], F32)
    fbiasc = sb("fbiasc", [128, 32], F32)
    identf = sb("identf_s", [128, 128], F32)
    identb = sb("identb_s", [128, 128], BF16)
    onesrow = sb("onesrow", [1, 128], F32)
    onescol_bf = sb("onescol_bf", [128, 1], BF16)
    cmask_sb = sb("cmask_sb", [8, 16], F32)
    cst = sb("cst", [128, 4], F32)

    msctx = ExitStack()
    b_fm = sb("b_fm", [128, TOK], BF16)
    c_fm = sb("c_fm", [128, TOK], BF16)
    b_tm = sb("b_tm", [128, NCHUNK * 128], BF16)
    dt_sb = msctx.enter_context(nc.sbuf_tensor("dt_sb", [NH, TOK], F32, side="right"))
    lda_sb = msctx.enter_context(nc.sbuf_tensor("lda_sb", [NH, TOK], F32, side="right"))
    lndt_sb = msctx.enter_context(nc.sbuf_tensor("lndt_sb", [NH, TOK], F32, side="right"))
    ll_sb = msctx.enter_context(nc.sbuf_tensor("ll_sb", [NH, TOK], F32, side="right"))
    offs_sb = sb("offs_sb", [NH, 16], F32)
    ltot_sb = sb("ltot_sb", [NH, 8], F32)
    etot_sb = sb("etot_sb", [NH, 8], F32)
    cumtm = msctx.enter_context(nc.sbuf_tensor("cumtm", [128, NCHUNK * 64], F32, side="right"))
    maskT_sb = sb("maskT_sb", [128, 128], F32)
    dgT_sb = sb("dgT_sb", [NH, 8], F32)
    coefT_sb = sb("coefT_sb", [NH, 8], F32)
    wtm = msctx.enter_context(nc.sbuf_tensor("wtm", [128, NCHUNK * 64], F32, side="right"))
    ltm = msctx.enter_context(nc.sbuf_tensor("ltm", [128, NCHUNK * 64], F32, side="right"))
    wctm = msctx.enter_context(nc.sbuf_tensor("wctm", [128, NCHUNK * 64], F32, side="right"))
    lndtm = msctx.enter_context(nc.sbuf_tensor("lndtm", [128, NCHUNK * 64], F32, side="right"))
    dbc = msctx.enter_context(nc.sbuf_tensor("dbc", [128, NCHUNK * 64], F32, side="right"))
    sm_a = sb("sm_a", [NH, 128], F32)
    sm_b = sb("sm_b", [NH, 128], F32)
    dslrow = sb("dslrow", [1, NH], F32)
    coef_sb = sb("coef_sb", [8, NH], F32)
    coefbc = sb("coefbc", [128, 8 * NH], F32)
    dg_sb = sb("dg_sb", [8, NH], F32)

    pt = es.enter_context(nc.psum_tensor("pt", [128, 512], F32))
    ph = es.enter_context(nc.psum_tensor("ph", [128, 512], F32))

    block = es.enter_context(nc.Block())

    lnxt_ctx = ExitStack()
    lnxt = lnxt_ctx.enter_context(nc.sbuf_tensor("lnxt", [128, 32 * 1028], BF16))
    ln_ctx = ExitStack()
    x_tile = ln_ctx.enter_context(nc.sbuf_tensor("x_tile", [128, 2 * H], F32))
    xc_f = ln_ctx.enter_context(nc.sbuf_tensor("xc_f", [128, 2 * H], F32))
    ln_small = ln_ctx.enter_context(nc.sbuf_tensor("ln_small", [128, 16], F32))
    rrow = ln_ctx.enter_context(nc.sbuf_tensor("rrow", [1, 128], F32))
    rbcast = ln_ctx.enter_context(nc.sbuf_tensor("rbcast", [128, 128], F32))
    sq_sink = ln_ctx.enter_context(nc.sbuf_tensor("sq_sink", [128, H], BF16))
    pjs = ExitStack()
    pj = [pjs.enter_context(nc.psum_tensor(f"pj{i}", [128, 1536], F32)) for i in range(2)]

    @block.sync
    def _(sync):
        for dst, src in ((convw_sb[:], convw_in), (dtb_sb[:], dtb_in),
                         (alog_sb[:], alog_in), (dparc[:], dpar_in),
                         (gnormc[:], gnorm_in), (fbiasc[:], fbias_in),
                         (maskT_sb[:], maskT_in), (identf[:], ident_in),
                         (cmask_sb[:], cmask_in)):
            sync.dma_start(out=dst, in_=src).then_inc(sK, 16)
            cK.add(16)

    @block.vector
    def _(vector):
        vector.wait_ge(sK, cK.v)
        vector.memset(onesrow[:], 1.0)
        vector.memset(onescol_bf[:], 1.0)
        vector.memset(cst[:, 0:1], EPS)
        vector.memset(cst[:, 1:2], -7.0 * LN2C)
        vector.memset(cst[:, 2:3], 7.0 * LN2C)
        vector.tensor_copy(out=identb[:], in_=identf[:])
        vector.drain()
        vector.drain()
        vector.nop().then_inc(sV, 1)
    cV.add(1)
    ev_consts = cV.v

    xrows = [(t * 128, 128) for t in range(8)] + [(1024, 3)]
    ln_v = [None] * 9
    ln_pe = [None] * 9
    ln_xc = [None] * 9

    for tt, (r0, nr) in enumerate(xrows):
        q = tt % 2

        @block.scalar
        def _(scalar, tt=tt, r0=r0, nr=nr, q=q):
            if tt >= 2:
                scalar.wait_ge(sP, ln_pe[tt - 2])
                scalar.wait_ge(sV, ln_xc[tt - 2])
            scalar.dma_start(out=x_tile[0:nr, q * H:q * H + H],
                             in_=xs[r0:r0 + nr, :]).then_inc(sXp[q], 16)
        cXp[q].add(16)
        ev_xload = (sXp[q], cXp[q].v)

        @block.scalar
        def _(scalar, tt=tt, nr=nr, q=q, ev_xload=ev_xload):
            scalar.wait_ge(ev_xload[0], ev_xload[1])
            if tt == 0:
                scalar.wait_ge(sV, ev_consts)
            scalar.activation(out=sq_sink[0:nr, :], in_=x_tile[0:nr, q * H:q * H + H],
                              func=AF.Square).then_inc(sS, 1)
        cS.add(1)
        ev_sq = cS.v

        @block.vector
        def _(vector, tt=tt, nr=nr, q=q, ev_xload=ev_xload, ev_sq=ev_sq):
            vector.wait_ge(ev_xload[0], ev_xload[1])
            vector.wait_ge(sS, ev_sq)
            xt = x_tile[0:nr, q * H:q * H + H]
            vector.reduce_sum(ln_small[0:nr, 1:2], sq_sink[0:nr, :], axis=AX)
            vector.reduce_sum(ln_small[0:nr, 0:1], xt, axis=AX)
            vector.drain()
            vector.tensor_scalar(out=ln_small[0:nr, 1:2], in0=ln_small[0:nr, 1:2],
                                 scalar1=1.0 / H, scalar2=None, op0=ALU.mult)
            vector.tensor_scalar(out=ln_small[0:nr, 0:1], in0=ln_small[0:nr, 0:1],
                                 scalar1=1.0 / H, scalar2=None, op0=ALU.mult)
            vector.drain()
            vector.tensor_tensor(out=ln_small[0:nr, 2:3], in0=ln_small[0:nr, 0:1],
                                 in1=ln_small[0:nr, 0:1], op=ALU.mult)
            vector.tensor_scalar(out=ln_small[0:nr, 5:6], in0=ln_small[0:nr, 0:1],
                                 scalar1=-1.0, scalar2=None, op0=ALU.mult)
            vector.drain()
            vector.tensor_sub(ln_small[0:nr, 2:3], ln_small[0:nr, 1:2],
                              ln_small[0:nr, 2:3])
            vector.drain()
            vector.drain()
            vector.nop().then_inc(sV, 1)
        cV.add(1)
        ev_stats = cV.v

        @block.scalar
        def _(scalar, nr=nr, ev_stats=ev_stats):
            scalar.wait_ge(sV, ev_stats)
            scalar.activation(out=ln_small[0:nr, 3:4], in_=ln_small[0:nr, 2:3],
                              func=AF.Sqrt, bias=cst[0:nr, 0:1]).then_inc(sS, 1)
        cS.add(1)
        ev_sqrt = cS.v

        @block.vector
        def _(vector, nr=nr, q=q, ev_sqrt=ev_sqrt):
            vector.wait_ge(sS, ev_sqrt)
            vector.reciprocal(out=ln_small[0:nr, 4:5], in_=ln_small[0:nr, 3:4])
            vector.drain()
            vector.tensor_scalar(out=xc_f[0:nr, q * H:q * H + H],
                                 in0=x_tile[0:nr, q * H:q * H + H],
                                 scalar1=ln_small[0:nr, 5:6], scalar2=None,
                                 op0=ALU.add)
            vector.drain()
            vector.drain()
            vector.nop().then_inc(sV, 1)
        cV.add(1)
        ln_xc[tt] = cV.v

        @block.tensor
        def _(tensor, tt=tt, nr=nr):
            tensor.wait_ge(sV, ln_xc[tt])
            if tt >= 1:
                tensor.wait_ge(sV, ln_v[tt - 1])
            tensor.transpose(ph[0:1, 0:nr], ln_small[0:nr, 4:5],
                             identf[0:nr, 0:nr]).then_inc(sP, 1)
        cP.add(1)

        @block.vector
        def _(vector, nr=nr, w=cP.v):
            vector.wait_ge(sP, w)
            vector.tensor_copy(out=rrow[0:1, 0:nr], in_=ph[0:1, 0:nr]).then_inc(sV, 1)
        cV.add(1)

        @block.tensor
        def _(tensor, nr=nr, w=cV.v):
            tensor.wait_ge(sV, w)
            tensor.matmul(ph[:, 256:256 + nr], onesrow[:], rrow[0:1, 0:nr],
                          start=True, stop=True).then_inc(sP, 1)
        cP.add(1)

        @block.vector
        def _(vector, nr=nr, w=cP.v):
            vector.wait_ge(sP, w)
            vector.tensor_copy(out=rbcast[:, 0:nr],
                               in_=ph[:, 256:256 + nr]).then_inc(sV, 1)
        cV.add(1)
        ev_rb = cV.v

        evac_ev = []
        for fc in range(32):
            @block.tensor
            def _(tensor, tt=tt, nr=nr, q=q, fc=fc):
                if fc == 0:
                    tensor.wait_ge(sV, ev_rb)
                if fc >= 2:
                    tensor.wait_ge(sV, evac_ev[fc - 2])
                dstp = pt if fc % 2 == 0 else ph
                tensor.transpose(dstp[:, 0:nr],
                                 xc_f[0:nr, q * H + fc * 128:q * H + fc * 128 + 128],
                                 identf[0:nr, 0:nr]).then_inc(sP, 1)
            cP.add(1)

            @block.vector
            def _(vector, r0=r0, nr=nr, fc=fc, w=cP.v):
                vector.wait_ge(sP, w)
                srcp = pt if fc % 2 == 0 else ph
                vector.tensor_tensor(
                    out=lnxt[:, fc * 1028 + r0:fc * 1028 + r0 + nr],
                    in0=srcp[:, 0:nr], in1=rbcast[:, 0:nr],
                    op=ALU.mult).then_inc(sV, 1)
            cV.add(1)
            evac_ev.append(cV.v)
        ln_v[tt] = cV.v
        ln_pe[tt] = cP.v

    tap("lnxt", lnxt[:, :])

    if phase_limit < 1:
        return _finish(nc, block, es, taps, sV, cV, sG, cG, [], out_t,
                       extra_ctx=[pjs, ln_ctx, lnxt_ctx], all_sems=all_dma_sems)

    ln_ctx.close()
    sctx = ExitStack()
    s_all = sctx.enter_context(nc.sbuf_tensor("s_all", [128, NH * 128], F32, side="right"))
    phA_ctx = ExitStack()
    exch = phA_ctx.enter_context(nc.sbuf_tensor("exch", [128, NH * 128], BF16))
    wbf = phA_ctx.enter_context(nc.sbuf_tensor("wbf", [128, 2 * 2048], BF16))
    conv_acc = phA_ctx.enter_context(nc.sbuf_tensor("conv_acc", [128, TOK], F32))
    xch_f = phA_ctx.enter_context(nc.sbuf_tensor("xch_f", [128, 2 * TOK], F32))
    bc_f = phA_ctx.enter_context(nc.sbuf_tensor("bc_f", [128, TOK], F32))
    dt_raw = phA_ctx.enter_context(nc.sbuf_tensor("dt_raw", [NH, TOK], F32))
    xw_sb = phA_ctx.enter_context(nc.sbuf_tensor("xw_sb", [128, 2 * 128], BF16))
    xtm_sb = phA_ctx.enter_context(nc.sbuf_tensor("xtm_sb", [128, 2 * 128], BF16))
    szh_bf = phA_ctx.enter_context(nc.sbuf_tensor("szh_bf", [128, 2 * TOK], BF16))
    slot_sb = phA_ctx.enter_context(nc.sbuf_tensor("slot_sb", [128, DSSM // 4], BF16))

    @block.vector
    def _(vector):
        vector.memset(conv_acc[:], 0.0).then_inc(sV, 1)
    cV.add(1)

    cc_order = [CC_DT, CC_B, CC_C] + list(range(64, 128)) + list(range(0, 64))
    if max_stripes is not None:
        cc_order = cc_order[:max_stripes]
    NSTRIPE = len(cc_order)
    evloadH = [None] * (2 * NSTRIPE)
    evcastH = [None] * (2 * NSTRIPE)
    evmmH = [None] * (2 * NSTRIPE)
    evmm = [None] * NSTRIPE
    evfree = [None] * NSTRIPE
    evsilu = [None] * NSTRIPE
    ev_hloc_ev = {}
    ev_xtm_dma = {}
    btm_ev = []
    ev_xtm_ev = {}
    ev_xtm_T = {}
    ev_smalls_done = None
    ev_dsl = None
    ev_sz_dma = {}

    GRP_FULL = [(0, 512), (512, 512), (1024, 3)]
    GRP_OWN = [(3, 0, 512), (515, 512, 512)]

    def xbc_ci(cc):
        if cc == CC_B:
            return 64
        if cc == CC_C:
            return 65
        return cc - 64

    last_ln = ln_v[8]

    def emit_tblock(ii, cc2):
        is_b2 = cc2 == CC_B
        is_xh2 = 64 <= cc2 < 128
        if is_b2:
            btm_ev.clear()
            for c in range(NCHUNK):
                @block.tensor
                def _(tensor, i=ii, c=c):
                    if c == 0:
                        tensor.wait_ge(sS, evsilu[i])
                        tensor.wait_ge(sV, ev_smalls_done)
                    if c >= 2:
                        tensor.wait_ge(sV, btm_ev[c - 2])
                    dstp = pt if c % 2 == 0 else ph
                    tensor.transpose(dstp[:, 0:128], bc_f[:, c * 128:c * 128 + 128],
                                     identf[:]).then_inc(sP, 1)
                cP.add(1)

                @block.vector
                def _(vector, c=c, w=cP.v):
                    vector.wait_ge(sP, w)
                    srcp = pt if c % 2 == 0 else ph
                    vector.tensor_copy(out=b_tm[:, c * 128:(c + 1) * 128],
                                       in_=srcp[:, 0:128]).then_inc(sV, 1)
                cV.add(1)
                btm_ev.append(cV.v)
        if is_xh2:
            h = cc2 - 64
            hq = ii % 2
            for c in range(NCHUNK):
                @block.tensor
                def _(tensor, i=ii, h=h, c=c, hq=hq):
                    if c == 0:
                        tensor.wait_ge(sS, evsilu[i])
                        tensor.wait_ge(sV, ev_smalls_done)
                        if h >= 1:
                            tensor.wait_ge(sV, ev_hloc_ev[h - 1])
                    if c >= 2:
                        tensor.wait_ge(sV, ev_xtm_ev[(h, c - 2)])
                    dstp = pt if c % 2 == 0 else ph
                    tensor.transpose(dstp[:, 0:128],
                                     xch_f[:, hq * TOK + c * 128:hq * TOK + c * 128 + 128],
                                     identf[:]).then_inc(sP, 1)
                cP.add(1)
                ev_xtm_T[(h, c)] = cP.v

                @block.vector
                def _(vector, h=h, c=c, w=cP.v):
                    vector.wait_ge(sP, w)
                    if (h, c - 2) in ev_xtm_dma:
                        vector.wait_ge(sTp[(c - 2) % 4], ev_xtm_dma[(h, c - 2)])
                    elif c < 2 and h >= 1 and (h - 1, 6 + c) in ev_xtm_dma:
                        vector.wait_ge(sTp[(6 + c) % 4], ev_xtm_dma[(h - 1, 6 + c)])
                    srcp = pt if c % 2 == 0 else ph
                    vector.tensor_copy(out=xtm_sb[:, (c % 2) * 128:(c % 2) * 128 + 128],
                                       in_=srcp[:, 0:128])
                    vector.tensor_scalar(
                        out=xw_sb[:, (c % 2) * 128:(c % 2) * 128 + 128],
                        in0=srcp[:, 0:128],
                        scalar1=wtm[:, c * 64 + h:c * 64 + h + 1], scalar2=None,
                        op0=ALU.mult).then_inc(sV, 1)
                cV.add(1)
                ev_xtm_ev[(h, c)] = cV.v

                @block.sync
                def _(sync, h=h, c=c, w=cV.v):
                    sync.wait_ge(sV, w)
                    sync.dma_start(out=xtm_st[c, h],
                                   in_=xtm_sb[:, (c % 2) * 128:(c % 2) * 128 + 128]
                                   ).then_inc(sTp[c % 4], 16)
                cTp[c % 4].add(16)
                ev_xtm_dma[(h, c)] = cTp[c % 4].v

                @block.tensor
                def _(tensor, h=h, c=c, ii=ii, w=cV.v):
                    q2 = ii % 2
                    tensor.wait_ge(sV, w)
                    tensor.matmul(pj[q2][:, 1152:1280],
                                  b_tm[:, c * 128:(c + 1) * 128],
                                  xw_sb[:, (c % 2) * 128:(c % 2) * 128 + 128],
                                  start=(c == 0), stop=(c == NCHUNK - 1)).then_inc(sP, 1)
                cP.add(1)
            ev_hloc_mm = cP.v

            @block.vector
            def _(vector, h=h, ii=ii, w=ev_hloc_mm):
                q2 = ii % 2
                vector.wait_ge(sP, w)
                vector.tensor_copy(out=exch[:, h * 128:(h + 1) * 128],
                                   in_=pj[q2][:, 1152:1280]).then_inc(sV, 1)
            cV.add(1)
            ev_hloc_ev[h] = cV.v
            evfree[ii] = (sV, cV.v)

            if h == 63:
                @block.gpsimd
                def _(gpsimd):
                    gpsimd.wait_ge(sV, ev_hloc_ev[63])
                    gpsimd.wait_ge(sV, ev_dsl)
                    gpsimd.dma_start(out=cc_in.ap(), in_=exch[:]).then_inc(sG, 16)
                    gpsimd.dma_start(out=ccd_in.ap(), in_=dslrow[:]).then_inc(sG, 16)
                    gpsimd.wait_ge(sG, cG.v + 32)
                    gpsimd.collective_compute(
                        "AllGather", ALU.bypass, replica_groups=[list(range(8))],
                        ins=[cc_in.ap().opt()], outs=[cc_out.ap().opt()]).then_inc(sC, 1)
                    gpsimd.collective_compute(
                        "AllGather", ALU.bypass, replica_groups=[list(range(8))],
                        ins=[ccd_in.ap().opt()], outs=[ccd_out.ap().opt()]).then_inc(sC, 1)
                cG.add(32)
                cC.add(2)


    for i, cc in enumerate(cc_order):
        q = i % 2
        is_dt = cc == CC_DT
        is_b = cc == CC_B
        is_c = cc == CC_C
        is_xh = 64 <= cc < 128
        is_z = cc < 64
        h = cc - 64 if is_xh else None

        for hf in range(2):
            j = 2 * i + hf
            qj = j % 2

            @block.gpsimd
            def _(gpsimd, i=i, cc=cc, hf=hf, j=j, qj=qj):
                if j < 2:
                    gpsimd.wait_ge(sV, last_ln)
                if j >= 2:
                    gpsimd.wait_ge(sP, evmmH[j - 2])
                gpsimd.dma_start(out=wbf[:, qj * 2048:(qj + 1) * 2048],
                                 in_=wi[cc, hf]).then_inc(sWp[qj], 16)
            cWp[qj].add(16)
            evloadH[j] = (sWp[qj], cWp[qj].v)

            @block.tensor
            def _(tensor, i=i, q=q, is_z=is_z, hf=hf, j=j, qj=qj):
                if dbg_mode < 2:
                    tensor.nop().then_inc(sP, 1)
                    return
                tensor.wait_ge(evloadH[j][0], evloadH[j][1])
                if hf == 0:
                    if i == 0:
                        tensor.wait_ge(sV, last_ln)
                    if i >= 2:
                        s, v = evfree[i - 2]
                        tensor.wait_ge(s, v)
                last = None
                for kk16 in range(16):
                    kk = hf * 16 + kk16
                    a = wbf[:, qj * 2048 + kk16 * 128:qj * 2048 + kk16 * 128 + 128]
                    if is_z:
                        for (ro, po, n) in GRP_OWN:
                            last = tensor.matmul(
                                pj[q][:, po:po + n], a,
                                lnxt[:, kk * 1028 + ro:kk * 1028 + ro + n],
                                start=(kk == 0), stop=(kk == 31))
                    else:
                        for (o, n) in GRP_FULL:
                            last = tensor.matmul(
                                pj[q][:, o:o + n], a,
                                lnxt[:, kk * 1028 + o:kk * 1028 + o + n],
                                start=(kk == 0), stop=(kk == 31))
                last.then_inc(sP, 1)
            cP.add(1)
            evmmH[j] = cP.v
        evmm[i] = evmmH[2 * i + 1]

        if i >= 1:
            emit_tblock(i - 1, cc_order[i - 1])
        if i == NSTRIPE - 1:
            _defer_last = True

        if is_dt:
            @block.vector
            def _(vector, i=i, q=q):
                vector.wait_ge(sP, evmm[i])
                if dbg_mode < 3:
                    vector.memset(dt_raw[:, :], 0.1)
                    vector.drain()
                    vector.nop().then_inc(sV, 1)
                    return
                vector.tensor_copy(out=dt_raw[:, :],
                                   in_=pj[q][0:NH, 3:TOKX]).then_inc(sV, 1)
            cV.add(1)
            evfree[i] = (sV, cV.v)
        elif is_z:
            zi = cc

            @block.scalar
            def _(scalar, i=i, q=q, zi=zi):
                scalar.wait_ge(sP, evmm[i])
                if zi >= 2:
                    scalar.wait_ge(sZp[zi % 2], ev_sz_dma[zi - 2])
                scalar.activation(out=szh_bf[:, (zi % 2) * TOK:(zi % 2 + 1) * TOK],
                                  in_=pj[q][:, 0:TOK],
                                  func=AF.Silu).then_inc(sS, 1)
            cS.add(1)
            evfree[i] = (sS, cS.v)
            evsilu[i] = cS.v

            @block.sync
            def _(sync, zi=zi, w=cS.v):
                sync.wait_ge(sS, w)
                for c in range(NCHUNK):
                    sync.dma_start(
                        out=sz_st[c, zi],
                        in_=szh_bf[:, (zi % 2) * TOK + c * 128:
                                   (zi % 2) * TOK + c * 128 + 128]
                    ).then_inc(sZp[zi % 2], 16)
            cZp[zi % 2].add(16 * NCHUNK)
            ev_sz_dma[zi] = cZp[zi % 2].v
        else:
            ci = xbc_ci(cc)

            @block.vector
            def _(vector, i=i, q=q, ci=ci):
                vector.wait_ge(sP, evmm[i])
                if i >= 1 and evsilu[i - 1] is not None:
                    vector.wait_ge(sS, evsilu[i - 1])
                ca = conv_acc[:, 0:TOK]
                vector.scalar_tensor_tensor(
                    out=ca, in0=pj[q][:, 0:1024],
                    scalar=convw_sb[:, ci * 4:ci * 4 + 1], in1=ca,
                    op0=ALU.mult, op1=ALU.bypass)
                for j in (1, 2, 3):
                    vector.drain()
                    vector.scalar_tensor_tensor(
                        out=ca, in0=pj[q][:, j:1024 + j],
                        scalar=convw_sb[:, ci * 4 + j:ci * 4 + j + 1], in1=ca,
                        op0=ALU.mult, op1=ALU.add)
                vector.drain()
                vector.drain()
                vector.nop().then_inc(sV, 1)
            cV.add(1)
            evfree[i] = (sV, cV.v)

            @block.scalar
            def _(scalar, i=i, cc=cc, w=cV.v):
                scalar.wait_ge(sV, w)
                if cc == CC_C:
                    scalar.wait_ge(sV, btm_ev[NCHUNK - 1])
                if 64 <= cc < 128 and (cc - 64) >= 2:
                    scalar.wait_ge(sP, ev_xtm_T[(cc - 64 - 2, NCHUNK - 1)])
                dst = (xch_f[:, (i % 2) * TOK:(i % 2 + 1) * TOK] if 64 <= cc < 128
                       else bc_f[:])
                scalar.activation(out=dst, in_=conv_acc[:, 0:TOK],
                                  func=AF.Silu).then_inc(sS, 1)
            cS.add(1)
            evsilu[i] = cS.v

            if cc in (CC_B, CC_C):
                @block.vector
                def _(vector, cc=cc, w=cS.v):
                    vector.wait_ge(sS, w)
                    vector.tensor_copy(out=(b_fm[:] if cc == CC_B else c_fm[:]),
                                       in_=bc_f[:]).then_inc(sV, 1)
                cV.add(1)
                evsilu[i] = (cS.v, cV.v)[1] if False else evsilu[i]
                ev_bcf = cV.v

        if is_dt and dbg_mode >= 1:
            ev_smalls_done, ev_dsl = _dt_math(
                block, locals(), sV, cV, sS, cS, sP, cP,
                upto=(dbg_mode - 10 if 10 < dbg_mode < 20 else 99))
        elif is_dt:
            ev_smalls_done, ev_dsl = (cV.v, cV.v)

    tap("dt", dt_sb[:, :])
    tap("ll", ll_sb[:, :])
    tap("cum", cumtm[:, :])
    tap("wtm", wtm[:, :])
    tap("b_fm", b_fm[:, :])
    tap("c_fm", c_fm[:, :])
    tap("exch", exch[:, :])

    if phase_limit < 2:
        return _finish(nc, block, es, taps, sV, cV, sG, cG, [], out_t,
                       extra_ctx=[pjs, phA_ctx, lnxt_ctx, sctx], all_sems=all_dma_sems)

    @block.gpsimd
    def _(gpsimd):
        gpsimd.wait_ge(sC, 2)
        gpsimd.dma_start(out=dg_sb[:], in_=ccd_out.ap()).then_inc(sG, 16)
    cG.add(16)
    ev_dg = cG.v

    @block.vector
    def _(vector):
        vector.wait_ge(sG, ev_dg)
        vector.memset(s_all[:], 0.0)
        vector.memset(coef_sb[:], 1.0)
        vector.drain()
        vector.drain()
        vector.nop().then_inc(sV, 1)
    cV.add(1)

    @block.tensor
    def _(tensor, w=cV.v):
        tensor.wait_ge(sV, w)
        tensor.transpose(pt[0:NH, 0:8], dg_sb[:], identf[0:8, 0:8]).then_inc(sP, 1)
    cP.add(1)

    @block.vector
    def _(vector, w=cP.v):
        vector.wait_ge(sP, w)
        vector.tensor_copy(out=dgT_sb[:], in_=pt[0:NH, 0:8]).then_inc(sV, 1)
    cV.add(1)

    for k in (1, 2, 5, 6):
        @block.vector
        def _(vector, k=k, w=cV.v):
            vector.wait_ge(sV, w)
            vector.wait_ge(sP, cP.v)
            vector.drain()
            vector.tensor_scalar(out=sm_a[0:NH, 0:8], in0=sm_a[0:NH, 0:8],
                                 scalar1=0.0, scalar2=dgT_sb[:, k:k + 1],
                                 op0=ALU.mult, op1=ALU.add)
            vector.drain()
            vector.drain()
            vector.nop().then_inc(sV, 1)
        cV.add(1)

        @block.tensor
        def _(tensor, k=k, w=cV.v):
            tensor.wait_ge(sV, w)
            tensor.transpose(pt[0:8, 64:128], sm_a[0:NH, 0:8],
                             identf[0:NH, 0:NH]).then_inc(sP, 1)
        cP.add(1)

        @block.vector
        def _(vector, k=k, w=cP.v):
            vector.wait_ge(sP, w)
            vector.tensor_scalar(out=sm_b[0:8, 0:NH], in0=pt[0:8, 64:128],
                                 scalar1=cmask_sb[:, k:k + 1],
                                 scalar2=cmask_sb[:, 8 + k:9 + k],
                                 op0=ALU.mult, op1=ALU.add)
            vector.drain()
            vector.tensor_mul(coef_sb[:], coef_sb[:], sm_b[0:8, 0:NH])
            vector.drain()
            vector.drain()
            vector.nop().then_inc(sV, 1)
        cV.add(1)

    @block.vector
    def _(vector):
        vector.tensor_scalar(out=coef_sb[:], in0=coef_sb[:],
                             scalar1=cmask_sb[:, 15:16], scalar2=None,
                             op0=ALU.mult)
        vector.drain()
        vector.drain()
        vector.nop().then_inc(sV, 1)
    cV.add(1)
    ev_coef = cV.v

    @block.tensor
    def _(tensor, w=cV.v):
        tensor.wait_ge(sV, w)
        tensor.transpose(pt[0:NH, 8:16], coef_sb[:], identf[0:8, 0:8]).then_inc(sP, 1)
    cP.add(1)

    @block.vector
    def _(vector, w=cP.v):
        vector.wait_ge(sP, w)
        vector.tensor_copy(out=coefT_sb[:], in_=pt[0:NH, 8:16]).then_inc(sV, 1)
    cV.add(1)

    for ji, j in enumerate(SLOTS):
        @block.vector
        def _(vector, j=j, w=cV.v):
            vector.wait_ge(sV, w)
            vector.wait_ge(sP, cP.v)
            vector.drain()
            vector.tensor_scalar(out=sm_a[0:NH, 0:128], in0=sm_a[0:NH, 0:128],
                                 scalar1=0.0, scalar2=coefT_sb[:, j:j + 1],
                                 op0=ALU.mult, op1=ALU.add)
            vector.drain()
            vector.drain()
            vector.nop().then_inc(sV, 1)
        cV.add(1)

        @block.tensor
        def _(tensor, j=j, w=cV.v):
            tensor.wait_ge(sV, w)
            tensor.transpose(ph[0:128, 192:256], sm_a[0:NH, 0:128],
                             identf[0:NH, 0:NH]).then_inc(sP, 1)
        cP.add(1)

        @block.vector
        def _(vector, j=j, w=cP.v):
            vector.wait_ge(sP, w)
            vector.tensor_copy(out=coefbc[:, j * NH:(j + 1) * NH],
                               in_=ph[0:128, 192:256]).then_inc(sV, 1)
        cV.add(1)
    ev_coefbc = cV.v

    for ji, j in enumerate(SLOTS):
        for qtr in range(4):
            @block.gpsimd
            def _(gpsimd, j=j, ji=ji, qtr=qtr):
                if ji > 0 or qtr > 0:
                    gpsimd.wait_ge(sV, cV.v)
                gpsimd.dma_start(
                    out=slot_sb[:],
                    in_=cc_out.ap()[j * 128:(j + 1) * 128,
                                    qtr * (DSSM // 4):(qtr + 1) * (DSSM // 4)]
                ).then_inc(sG, 16)
            cG.add(16)

            @block.vector
            def _(vector, j=j, qtr=qtr, w=cG.v):
                vector.wait_ge(sG, w)
                for hh in range(NH // 4):
                    hg = qtr * (NH // 4) + hh
                    vector.scalar_tensor_tensor(
                        out=s_all[:, hg * 128:(hg + 1) * 128],
                        in0=slot_sb[:, hh * 128:(hh + 1) * 128],
                        scalar=coefbc[:, j * NH + hg:j * NH + hg + 1],
                        in1=s_all[:, hg * 128:(hg + 1) * 128],
                        op0=ALU.mult, op1=ALU.add)
                vector.drain()
                vector.nop().then_inc(sV, 1)
            cV.add(1)
    ev_hin = cV.v
    tap("h_in", s_all[:, :])

    if phase_limit < 3:
        return _finish(nc, block, es, taps, sV, cV, sG, cG, [], out_t,
                       extra_ctx=[pjs, phA_ctx, lnxt_ctx, sctx], all_sems=all_dma_sems)

    pjs.close()
    phA_ctx.close()
    lnxt_ctx.close()
    yctx = ExitStack()

    def ysb(name, shape, dtype):
        return yctx.enter_context(nc.sbuf_tensor(name, shape, dtype))

    xtmc = ysb("xtmc", [128, 2 * DSSM], BF16)
    szc = ysb("szc", [128, 2 * DSSM], BF16)
    yg_all = ysb("yg_all", [128, DSSM], BF16)
    e_sb = ysb("e_sb", [128, 2 * 128], BF16)
    m1_sb = ysb("m1_sb", [128, 2 * 128], BF16)
    identd = ysb("identd", [128, NH * 128], BF16)
    mv_sb = ysb("mv_sb", [128, 2 * 128], BF16)
    sbf_sb = ysb("sbf_sb", [128, 2 * 128], BF16)
    xw2_sb = ysb("xw2_sb", [128, 2 * 128], BF16)
    yg2_sb = ysb("yg2_sb", [128, 2 * 128], BF16)
    ynt_sb = ysb("ynt_sb", [128, 2 * 128], BF16)
    rms_sb = ysb("rms_sb", [1, 128], F32)
    lbt_sb = ysb("lbt_sb", [128, 2 * 128], F32)
    cbt_sb = ysb("cbt_sb", [128, 2 * 128], F32)
    rmsb_sb = ysb("rmsb_sb", [128, 128], F32)

    pyc = ExitStack()
    pA = [pyc.enter_context(nc.psum_tensor(f"pA{i}", [128, 512], F32)) for i in range(2)]
    pY = [pyc.enter_context(nc.psum_tensor(f"pY{i}", [128, 512], F32)) for i in range(2)]
    pss = pyc.enter_context(nc.psum_tensor("pss", [128, 512], F32))

    v0, s0, p0 = cV.v + 1, cS.v, cP.v
    n0 = [cNp[0].v, cNp[1].v]
    y0 = [cYp[0].v, cYp[1].v]
    tot_sT = [c_.v for c_ in cTp]
    peA_ev, peB_ev, peC_ev = {}, {}, {}
    v0t_ev, vA_ev, vB_ev, scA_ev, scB_ev = {}, {}, {}, {}, {}
    g1_ev, g2_ev = {}, {}
    ev_cdma, ev_rms_sc, ev_recip, ev_rmsbmm, ev_rmsb = {}, {}, {}, {}, {}
    ev_yn_stt, ev_yn_dma = {}, {}

    pe_seq = []
    for c in range(NCHUNK):
        pe_seq += [("A", c * 64), ("A", c * 64 + 1)]
        for hh in range(NH):
            t = c * 64 + hh
            pe_seq.append(("B", t))
            if hh < 62:
                pe_seq.append(("A", t + 2))
            pe_seq.append(("C", t))
        pe_seq.append(("R", c))
    pv = p0
    for kind, t in pe_seq:
        pv += 1
        if kind == "A":
            peA_ev[t] = pv
        elif kind == "B":
            peB_ev[t] = pv
        elif kind == "C":
            peC_ev[t] = pv
        else:
            ev_rmsbmm[t] = pv
    for c in range(NCHUNK):
        base = v0 + c * 258
        v0t_ev[c * 64 + 0] = base + 1
        v0t_ev[c * 64 + 1] = base + 2
        for hh in range(NH):
            t = c * 64 + hh
            if hh < 62:
                v0t_ev[t + 2] = base + 2 + hh * 3 + 1
                vA_ev[t] = base + 2 + hh * 3 + 2
                vB_ev[t] = base + 2 + hh * 3 + 3
            else:
                vA_ev[t] = base + 2 + 62 * 3 + (hh - 62) * 2 + 1
                vB_ev[t] = base + 2 + 62 * 3 + (hh - 62) * 2 + 2
            scA_ev[t] = s0 + c * 129 + hh * 2 + 1
            scB_ev[t] = s0 + c * 129 + hh * 2 + 2
            ev_yn_stt[(c, hh)] = base + 194 + hh + 1
            ev_yn_dma[(c, hh)] = n0[hh % 2] + (c * 32 + hh // 2 + 1) * 16
        ev_cdma[c] = y0[c % 2] + (c // 2 + 1) * 32
        ev_rms_sc[c] = s0 + c * 129 + 129
        ev_recip[c] = base + 193
        ev_rmsb[c] = base + 194

    @block.vector
    def _(vector):
        for hh in range(NH):
            vector.tensor_scalar(out=identd[:, hh * 128:(hh + 1) * 128],
                                 in0=identb[:], scalar1=dparc[:, hh:hh + 1],
                                 scalar2=None, op0=ALU.mult)
        vector.drain()
        vector.nop().then_inc(sV, 1)
    cV.add(1)
    assert cV.v == v0

    for c in range(NCHUNK):
        qc = c % 2

        @block.sync
        def _(sync, c=c, qc=qc):
            if c == 0:
                sync.wait_ge(sV, ev_hin)
                sync.wait_ge(sZp[0], ev_sz_dma[62])
                sync.wait_ge(sZp[1], ev_sz_dma[63])
                for ii in range(4):
                    sync.wait_ge(sTp[ii], tot_sT[ii])
            if c >= 2:
                sync.wait_ge(sV, vB_ev[(c - 2) * 64 + 63])
            sync.dma_start(
                out=xtmc[:, qc * DSSM:(qc + 1) * DSSM],
                in_=xtm_st[c].rearrange("h p t -> p h t")).then_inc(sYp[qc], 16)
            sync.dma_start(
                out=szc[:, qc * DSSM:(qc + 1) * DSSM],
                in_=sz_st[c].rearrange("h p t -> p h t")).then_inc(sYp[qc], 16)
        cYp[qc].add(32)
        assert cYp[qc].v == ev_cdma[c]

        @block.tensor
        def _(tensor, c=c, qc=qc):
            def emit_A(t):
                hh = t - c * 64
                q = t % 2
                if hh == 0:
                    tensor.wait_ge(sYp[qc], ev_cdma[c])
                    if c >= 1:
                        tensor.wait_ge(sV, ev_rmsb[c - 1])
                tensor.wait_ge(sV, v0t_ev[t])
                if t >= 2:
                    tensor.wait_ge(sV, vB_ev[t - 2])
                    tensor.wait_ge(sS, scA_ev[t - 2])
                tensor.matmul(pA[q][:, 0:128], b_fm[:, c * 128:c * 128 + 128],
                              c_fm[:, c * 128:c * 128 + 128], start=True, stop=True)
                tensor.transpose(pA[q][:, 128:256],
                                 lbt_sb[:, q * 128:q * 128 + 128], identf[:])
                tensor.transpose(pA[q][:, 256:384],
                                 cbt_sb[:, q * 128:q * 128 + 128],
                                 identf[:]).then_inc(sP, 1)
                assert cP.add(1) == peA_ev[t]

            def emit_B(t):
                hh = t - c * 64
                q = t % 2
                tensor.wait_ge(sV, vA_ev[t])
                tensor.matmul(pY[q][:, 0:128],
                              xtmc[:, qc * DSSM + hh * 128:qc * DSSM + hh * 128 + 128],
                              m1_sb[:, q * 128:q * 128 + 128], start=True, stop=False)
                tensor.matmul(pY[q][:, 0:128],
                              xtmc[:, qc * DSSM + hh * 128:qc * DSSM + hh * 128 + 128],
                              identd[:, hh * 128:(hh + 1) * 128], start=False, stop=False)
                tensor.matmul(pY[q][:, 0:128], sbf_sb[:, q * 128:q * 128 + 128],
                              mv_sb[:, q * 128:q * 128 + 128], start=False, stop=True)
                tensor.matmul(pA[q][:, 384:512], b_tm[:, c * 128:(c + 1) * 128],
                              xw2_sb[:, q * 128:q * 128 + 128],
                              start=True, stop=True).then_inc(sP, 1)
                assert cP.add(1) == peB_ev[t]

            def emit_C(t):
                hh = t - c * 64
                q = t % 2
                tensor.wait_ge(sS, scB_ev[t])
                tensor.matmul(pss[0:1, 0:128], onescol_bf[:],
                              yg2_sb[:, q * 128:q * 128 + 128],
                              start=(hh == 0), stop=(hh == 63)).then_inc(sP, 1)
                assert cP.add(1) == peC_ev[t]

            emit_A(c * 64)
            emit_A(c * 64 + 1)
            for hh in range(NH):
                t = c * 64 + hh
                emit_B(t)
                if hh < 62:
                    emit_A(t + 2)
                emit_C(t)
            tensor.wait_ge(sV, ev_recip[c])
            tensor.matmul(pss[:, 128:256], onesrow[:], rms_sb[0:1, :],
                          start=True, stop=True).then_inc(sP, 1)
            assert cP.add(1) == ev_rmsbmm[c]

        @block.scalar
        def _(scalar, c=c):
            for hh in range(NH):
                t = c * 64 + hh
                q = t % 2
                scalar.wait_ge(sP, peA_ev[t])
                scalar.activation(out=e_sb[:, q * 128:q * 128 + 128],
                                  in_=pA[q][:, 128:256], func=AF.Exp,
                                  bias=lndtm[:, c * 64 + hh:c * 64 + hh + 1]
                                  ).then_inc(sS, 1)
                assert cS.add(1) == scA_ev[t]
                scalar.wait_ge(sV, vB_ev[t])
                if t >= 2:
                    scalar.wait_ge(sP, peC_ev[t - 2])
                scalar.activation(out=yg2_sb[:, q * 128:q * 128 + 128],
                                  in_=yg_all[:, hh * 128:hh * 128 + 128],
                                  func=AF.Square).then_inc(sS, 1)
                assert cS.add(1) == scB_ev[t]
            scalar.wait_ge(sP, peC_ev[c * 64 + 63])
            scalar.activation(out=rms_sb[:], in_=pss[0:1, 0:128], func=AF.Sqrt,
                              scale=1.0 / DSSM, bias=cst[0:1, 0:1]).then_inc(sS, 1)
            assert cS.add(1) == ev_rms_sc[c]

        @block.vector
        def _(vector, c=c, qc=qc):
            def emit_v0(t):
                hh = t - c * 64
                q = t % 2
                if t >= 2:
                    vector.wait_ge(sP, peA_ev[t - 2])
                vector.tensor_scalar(out=lbt_sb[:, q * 128:q * 128 + 128],
                                     in0=maskT_sb[:],
                                     scalar1=ltm[:, c * 64 + hh:c * 64 + hh + 1],
                                     scalar2=None, op0=ALU.add)
                vector.tensor_scalar(out=cbt_sb[:, q * 128:q * 128 + 128],
                                     in0=maskT_sb[:], scalar1=0.0,
                                     scalar2=cumtm[:, c * 64 + hh:c * 64 + hh + 1],
                                     op0=ALU.mult, op1=ALU.add).then_inc(sV, 1)
                assert cV.add(1) == v0t_ev[t]

            emit_v0(c * 64)
            emit_v0(c * 64 + 1)
            for hh in range(NH):
                t = c * 64 + hh
                q = t % 2
                if hh < 62:
                    emit_v0(t + 2)
                vector.wait_ge(sS, scA_ev[t])
                if t >= 64:
                    vector.wait_ge(sV, vB_ev[t - 64])
                vector.tensor_copy(out=sbf_sb[:, q * 128:q * 128 + 128],
                                   in_=s_all[:, hh * 128:hh * 128 + 128])
                vector.tensor_scalar(
                    out=xw2_sb[:, q * 128:q * 128 + 128],
                    in0=xtmc[:, qc * DSSM + hh * 128:qc * DSSM + hh * 128 + 128],
                    scalar1=wctm[:, c * 64 + hh:c * 64 + hh + 1],
                    scalar2=None, op0=ALU.mult)
                vector.tensor_tensor(out=m1_sb[:, q * 128:q * 128 + 128],
                                     in0=e_sb[:, q * 128:q * 128 + 128],
                                     in1=pA[q][:, 0:128], op=ALU.mult)
                vector.tensor_tensor(out=mv_sb[:, q * 128:q * 128 + 128],
                                     in0=c_fm[:, c * 128:c * 128 + 128],
                                     in1=pA[q][:, 256:384], op=ALU.mult).then_inc(sV, 1)
                assert cV.add(1) == vA_ev[t]
                vector.wait_ge(sP, peB_ev[t])
                vector.scalar_tensor_tensor(
                    out=s_all[:, hh * 128:hh * 128 + 128],
                    in0=s_all[:, hh * 128:hh * 128 + 128],
                    scalar=dbc[:, c * 64 + hh:c * 64 + hh + 1],
                    in1=pA[q][:, 384:512], op0=ALU.mult, op1=ALU.add)
                vector.tensor_tensor(out=yg_all[:, hh * 128:hh * 128 + 128],
                                     in0=pY[q][:, 0:128],
                                     in1=szc[:, qc * DSSM + hh * 128:qc * DSSM + hh * 128 + 128],
                                     op=ALU.mult).then_inc(sV, 1)
                assert cV.add(1) == vB_ev[t]

        @block.vector
        def _(vector, c=c):
            vector.wait_ge(sS, ev_rms_sc[c])
            vector.reciprocal(out=rms_sb[:], in_=rms_sb[:]).then_inc(sV, 1)
            assert cV.add(1) == ev_recip[c]

        @block.vector
        def _(vector, c=c):
            vector.wait_ge(sP, ev_rmsbmm[c])
            vector.tensor_copy(out=rmsb_sb[:], in_=pss[:, 128:256]).then_inc(sV, 1)
            assert cV.add(1) == ev_rmsb[c]

        @block.vector
        def _(vector, c=c):
            for hh in range(NH):
                if hh == 0:
                    vector.wait_ge(sV, ev_rmsb[c])
                if hh >= 2:
                    vector.wait_ge(sNp[hh % 2], ev_yn_dma[(c, hh - 2)])
                elif c >= 1:
                    vector.wait_ge(sNp[hh % 2], ev_yn_dma[(c - 1, 62 + hh)])
                vector.scalar_tensor_tensor(
                    out=ynt_sb[:, (hh % 2) * 128:(hh % 2) * 128 + 128],
                    in0=yg_all[:, hh * 128:hh * 128 + 128],
                    scalar=gnormc[:, hh:hh + 1], in1=rmsb_sb[:],
                    op0=ALU.mult, op1=ALU.mult).then_inc(sV, 1)
                assert cV.add(1) == ev_yn_stt[(c, hh)]

        @block.scalar
        def _(scalar, c=c):
            for hh in range(NH):
                scalar.wait_ge(sV, ev_yn_stt[(c, hh)])
                scalar.dma_start(out=yn_st[hh][:, c * 128:c * 128 + 128],
                                 in_=ynt_sb[:, (hh % 2) * 128:(hh % 2) * 128 + 128]
                                 ).then_inc(sNp[hh % 2], 16)
                assert cNp[hh % 2].add(16) == ev_yn_dma[(c, hh)]

    tap("yg7", yg_all[:, :])
    tap("s_end", s_all[:, :])

    if phase_limit < 4:
        return _finish(nc, block, es, taps, sV, cV, sG, cG, [], out_t,
                       extra_ctx=[pyc, yctx, sctx], all_sems=all_dma_sems)

    yctx.close()
    pyc.close()
    sctx.close()
    msctx.close()
    wctx = ExitStack()

    def wsb(name, shape, dtype):
        return wctx.enter_context(nc.sbuf_tensor(name, shape, dtype))

    ynp = wsb("ynp", [128, NH * 512], BF16)
    wobf = wsb("wobf", [128, 2 * 4096], BF16)
    attn_f = wsb("attn_f", [128, 512], F32)
    xres_sb = wsb("xres_sb", [128, 2 * 128], F32)
    x1t_sb = wsb("x1t_sb", [128, 2 * 128], F32)

    pwc = ExitStack()
    pw = [pwc.enter_context(nc.psum_tensor(f"pw{i}", [128, 512], F32)) for i in range(4)]

    ev_wold = {}
    ev_wocast = {}
    ev_att_ev = {}
    ev_x1dma = {}
    last_yn0 = cNp[0].v
    last_yn1 = cNp[1].v

    for tg in range(2):
        @block.sync
        def _(sync, tg=tg):
            if tg == 0:
                sync.wait_ge(sNp[0], last_yn0)
                sync.wait_ge(sNp[1], last_yn1)
            else:
                sync.wait_ge(sV, cV.v)
            for hh in range(NH):
                sync.dma_start(out=ynp[:, hh * 512:(hh + 1) * 512],
                               in_=yn_st[hh][:, tg * 512:(tg + 1) * 512]
                               ).then_inc(sYp[tg], 16)
        cYp[tg].add(16 * NH)
        ev_ynp = (sYp[tg], cYp[tg].v)

        for ng in range(8):
            for kg in range(8):
                si = (tg * 8 + ng) * 8 + kg
                qw = si % 2

                @block.gpsimd
                def _(gpsimd, tg=tg, ng=ng, kg=kg, si=si, qw=qw):
                    if si < 2:
                        gpsimd.wait_ge(sNp[0], last_yn0)
                        gpsimd.wait_ge(sNp[1], last_yn1)
                    if si >= 2:
                        gpsimd.wait_ge(sP, _later(ev_att_ev, si - 2, cP))
                    gpsimd.dma_start(
                        out=wobf[:, qw * 4096:(qw + 1) * 4096],
                        in_=wo[ng, kg * 8:(kg + 1) * 8].rearrange(
                            "k p n -> p k n")).then_inc(sWp[qw], 16)
                cWp[qw].add(16)
                ev_wold[si] = (sWp[qw], cWp[qw].v)

                @block.tensor
                def _(tensor, tg=tg, ng=ng, kg=kg, si=si, qw=qw, ev_ynp=ev_ynp):
                    tensor.wait_ge(ev_wold[si][0], ev_wold[si][1])
                    if kg == 0:
                        tensor.wait_ge(ev_ynp[0], ev_ynp[1])
                        if not (tg == 0 and ng == 0):
                            tensor.wait_ge(sV, cV.v)
                    last = None
                    for kk8 in range(8):
                        kc = kg * 8 + kk8
                        for ncx in range(4):
                            last = tensor.matmul(
                                pw[ncx][:, 0:512],
                                wobf[:, qw * 4096 + kk8 * 512 + ncx * 128:
                                     qw * 4096 + kk8 * 512 + ncx * 128 + 128],
                                ynp[:, kc * 512:(kc + 1) * 512],
                                start=(kc == 0), stop=(kc == 63))
                    last.then_inc(sP, 1)
                cP.add(1)
                ev_att_ev[si] = cP.v

            for ncx in range(4):
                @block.vector
                def _(vector, ncx=ncx, w=ev_att_ev[(tg * 8 + ng) * 8 + 7]):
                    vector.wait_ge(sP, w)
                    vector.tensor_copy(out=attn_f[:], in_=pw[ncx][:, 0:512]).then_inc(sV, 1)
                cV.add(1)
                ev_evac = cV.v

                for tb in range(4):
                    ti = ncx * 4 + tb
                    qt = ti % 2

                    @block.tensor
                    def _(tensor, tb=tb, qt=qt, w=ev_evac, ti=ti, tg=tg, ng=ng):
                        tensor.wait_ge(sV, w)
                        if ti >= 2 or not (tg == 0 and ng == 0):
                            tensor.wait_ge(sV, cV.v)
                        dstp = pt if qt == 0 else ph
                        tensor.transpose(dstp[:, 0:128],
                                         attn_f[:, tb * 128:tb * 128 + 128],
                                         identf[:]).then_inc(sP, 1)
                    cP.add(1)
                    ev_tp = cP.v

                    @block.scalar
                    def _(scalar, tg=tg, ng=ng, ncx=ncx, tb=tb, qt=qt, wv=cV.v):
                        scalar.wait_ge(sV, wv)
                        r0 = 3 + tg * 512 + tb * 128
                        c0 = ng * 512 + ncx * 128
                        scalar.dma_start(out=xres_sb[:, qt * 128:qt * 128 + 128],
                                         in_=xs[r0:r0 + 128, c0:c0 + 128]
                                         ).then_inc(sXp[qt], 16)
                    cXp[qt].add(16)
                    ev_xr = (sXp[qt], cXp[qt].v)

                    @block.vector
                    def _(vector, qt=qt, w1=ev_tp, w2=ev_xr,
                          w3=ev_x1dma.get(len(ev_x1dma) - 2)):
                        vector.wait_ge(sP, w1)
                        vector.wait_ge(w2[0], w2[1])
                        if w3 is not None:
                            vector.wait_ge(w3[0], w3[1])
                        srcp = pt if qt == 0 else ph
                        vector.tensor_add(x1t_sb[:, qt * 128:qt * 128 + 128],
                                          srcp[:, 0:128],
                                          xres_sb[:, qt * 128:qt * 128 + 128])
                        vector.drain()
                        vector.nop().then_inc(sV, 1)
                    cV.add(1)

                    @block.sync
                    def _(sync, tg=tg, ng=ng, ncx=ncx, tb=tb, qt=qt, w=cV.v):
                        sync.wait_ge(sV, w)
                        r0 = tg * 512 + tb * 128
                        c0 = ng * 512 + ncx * 128
                        sync.dma_start(out=x1_st[r0:r0 + 128, c0:c0 + 128],
                                       in_=x1t_sb[:, qt * 128:qt * 128 + 128]
                                       ).then_inc(sTp[qt], 16)
                    cTp[qt].add(16)
                    ev_x1dma[len(ev_x1dma)] = (sTp[qt], cTp[qt].v)

    ev_x1_done = [(sTp[0], cTp[0].v), (sTp[1], cTp[1].v)]
    tap("x1st", x1_st[:, :])

    if phase_limit < 5:
        return _finish(nc, block, es, taps, sV, cV, sG, cG, [], out_t,
                       extra_ctx=[pwc, wctx], all_sems=all_dma_sems)

    wctx.close()
    pwc.close()
    fctx = ExitStack()

    def fsb(name, shape, dtype):
        return fctx.enter_context(nc.sbuf_tensor(name, shape, dtype))

    qxT = fsb("qxT", [128, 32 * 1024], BF16)
    x1row2 = fsb("x1row2", [128, 2 * 4096], F32)
    qxr = fsb("qxr", [128, 2 * 4096], F32)
    qs_f = fsb("qs_f", [128, 4096], F32)
    qi_i = fsb("qi_i", [128, 4096], I32)
    q256a = fsb("q256a", [128, 256], F32)
    q256b = fsb("q256b", [128, 256], F32)
    q256c = fsb("q256c", [128, 256], F32)
    q256d = fsb("q256d", [128, 256], F32)
    qwT_sb = fsb("qwT_sb", [128, 32 * 128], BF16)
    ffn_f = fsb("ffn_f", [128, 1024], F32)
    ost_sb = fsb("ost_sb", [128, 2 * 128], F32)
    x1r_sb = fsb("x1r_sb", [128, 2 * 128], F32)
    lnq_small = fsb("lnq_small", [128, 16], F32)

    pfc = ExitStack()
    pf = [pfc.enter_context(nc.psum_tensor(f"pf{i}", [128, 512], F32)) for i in range(2)]

    ev_q_src_done = {}
    ev_qr_done = {}
    ev_bfp_done = {}
    ev_ld = {}
    ev_outdma = {}
    ev_fev = {}
    ev_fmm = {}

    for tc in range(NCHUNK):
        i = tc
        qp = i % 2

        @block.scalar
        def _(scalar, tc=tc, i=i, qp=qp):
            if tc == 0:
                for s_, v_ in ev_x1_done:
                    scalar.wait_ge(s_, v_)
            if i >= 2:
                scalar.wait_ge(sV, ev_q_src_done[i - 2])
            scalar.dma_start(out=x1row2[:, qp * 4096:(qp + 1) * 4096],
                             in_=x1_st[tc * 128:(tc + 1) * 128, :]).then_inc(sXp[qp], 16)
        cXp[qp].add(16)
        ev_ld[i] = (sXp[qp], cXp[qp].v)

        @block.scalar
        def _(scalar, i=i, qp=qp, w=ev_ld[i]):
            scalar.wait_ge(w[0], w[1])
            if i >= 2:
                scalar.wait_ge(sP, ev_qr_done[i - 2])
            scalar.activation(out=qxr[:, qp * 4096:(qp + 1) * 4096],
                              in_=x1row2[:, qp * 4096:(qp + 1) * 4096],
                              func=AF.Square).then_inc(sS, 1)
        cS.add(1)

        @block.vector
        def _(vector, i=i, qp=qp, w=ev_ld[i], w2=cS.v):
            vector.wait_ge(w[0], w[1])
            vector.wait_ge(sS, w2)
            if i >= 2:
                vector.wait_ge(sP, ev_qr_done[i - 2])
            xt = x1row2[:, qp * 4096:(qp + 1) * 4096]
            vector.reduce_sum(lnq_small[:, 1:2], qxr[:, qp * 4096:(qp + 1) * 4096],
                              axis=AX)
            vector.reduce_sum(lnq_small[:, 0:1], xt, axis=AX)
            vector.drain()
            vector.tensor_scalar(out=lnq_small[:, 1:2], in0=lnq_small[:, 1:2],
                                 scalar1=1.0 / H, scalar2=None, op0=ALU.mult)
            vector.tensor_scalar(out=lnq_small[:, 0:1], in0=lnq_small[:, 0:1],
                                 scalar1=1.0 / H, scalar2=None, op0=ALU.mult)
            vector.drain()
            vector.tensor_tensor(out=lnq_small[:, 2:3], in0=lnq_small[:, 0:1],
                                 in1=lnq_small[:, 0:1], op=ALU.mult)
            vector.tensor_scalar(out=lnq_small[:, 5:6], in0=lnq_small[:, 0:1],
                                 scalar1=-1.0, scalar2=None, op0=ALU.mult)
            vector.drain()
            vector.tensor_sub(lnq_small[:, 2:3], lnq_small[:, 1:2], lnq_small[:, 2:3])
            vector.drain()
            vector.nop().then_inc(sV, 1)
        cV.add(1)

        @block.scalar
        def _(scalar, w=cV.v):
            scalar.wait_ge(sV, w)
            scalar.activation(out=lnq_small[:, 3:4], in_=lnq_small[:, 2:3],
                              func=AF.Sqrt, bias=cst[:, 0:1]).then_inc(sS, 1)
        cS.add(1)

        @block.vector
        def _(vector, i=i, qp=qp, w=cS.v):
            vector.wait_ge(sS, w)
            vector.reciprocal(out=lnq_small[:, 4:5], in_=lnq_small[:, 3:4])
            vector.drain()
            vector.tensor_scalar(out=qxr[:, qp * 4096:(qp + 1) * 4096],
                                 in0=x1row2[:, qp * 4096:(qp + 1) * 4096],
                                 scalar1=lnq_small[:, 5:6], scalar2=lnq_small[:, 4:5],
                                 op0=ALU.add, op1=ALU.mult)
            vector.drain()
            vector.nop().then_inc(sV, 1)
        cV.add(1)
        ev_q_src_done[i] = cV.v

        _emit_bfp(block, sV, cV, sS, cS, qxr[:, qp * 4096:(qp + 1) * 4096],
                  qs_f, qi_i, q256a, q256b, q256c, q256d,
                  qxr[:, qp * 4096:(qp + 1) * 4096], INVLN2, LN2C, cst)
        ev_bfp_done[i] = cV.v

        tev = []
        for fc in range(32):
            @block.tensor
            def _(tensor, tc=tc, fc=fc, qp=qp, w=cV.v):
                if fc == 0:
                    tensor.wait_ge(sV, w)
                if fc >= 2:
                    tensor.wait_ge(sV, tev[fc - 2])
                dstp = pt if fc % 2 == 0 else ph
                tensor.transpose(dstp[:, 0:128],
                                 qxr[:, qp * 4096 + fc * 128:qp * 4096 + fc * 128 + 128],
                                 identf[:]).then_inc(sP, 1)
            cP.add(1)

            @block.vector
            def _(vector, tc=tc, fc=fc, w=cP.v):
                vector.wait_ge(sP, w)
                srcp = pt if fc % 2 == 0 else ph
                vector.tensor_copy(out=qxT[:, fc * 1024 + tc * 128:fc * 1024 + tc * 128 + 128],
                                   in_=srcp[:, 0:128]).then_inc(sV, 1)
            cV.add(1)
            tev.append(cV.v)
        ev_qr_done[i] = cP.v
    ev_qx_done = cV.v

    def emit_outstage(ncx):
        for tb in range(8):
            qt = tb % 2

            @block.tensor
            def _(tensor, ncx=ncx, tb=tb, qt=qt, w=ev_fev[ncx]):
                tensor.wait_ge(sS, w)
                if tb >= 2:
                    tensor.wait_ge(sV, cV.v)
                dstp = pt if qt == 0 else ph
                tensor.transpose(dstp[:, 128:256], ffn_f[:, tb * 128:tb * 128 + 128],
                                 identf[:]).then_inc(sP, 1)
            cP.add(1)
            ev_otp = cP.v

            @block.scalar
            def _(scalar, ncx=ncx, tb=tb, qt=qt, wv=cV.v):
                scalar.wait_ge(sV, wv)
                scalar.dma_start(out=x1r_sb[:, qt * 128:qt * 128 + 128],
                                 in_=x1_st[tb * 128:(tb + 1) * 128,
                                           ncx * 128:(ncx + 1) * 128]
                                 ).then_inc(sXp[qt], 16)
            cXp[qt].add(16)
            ev_x1r = (sXp[qt], cXp[qt].v)

            @block.vector
            def _(vector, qt=qt, w1=ev_otp, w2=ev_x1r,
                  w3=ev_outdma.get(len(ev_outdma) - 2)):
                vector.wait_ge(sP, w1)
                vector.wait_ge(w2[0], w2[1])
                if w3 is not None:
                    vector.wait_ge(w3[0], w3[1])
                srcp = pt if qt == 0 else ph
                vector.tensor_add(ost_sb[:, qt * 128:qt * 128 + 128],
                                  srcp[:, 128:256], x1r_sb[:, qt * 128:qt * 128 + 128])
                vector.nop().then_inc(sV, 1)
            cV.add(1)

            @block.sync
            def _(sync, ncx=ncx, tb=tb, qt=qt, w=cV.v):
                sync.wait_ge(sV, w)
                sync.dma_start(out=out_t[tb * 128:(tb + 1) * 128,
                                         ncx * 128:(ncx + 1) * 128],
                               in_=ost_sb[:, qt * 128:qt * 128 + 128]
                               ).then_inc(sOp[qt], 16)
            cOp[qt].add(16)
            ev_outdma[len(ev_outdma)] = (sOp[qt], cOp[qt].v)

    def emit_ffn_tail(n):
        @block.scalar
        def _(scalar, ncx=n, w=ev_fmm[n], wv=cV.v):
            scalar.wait_ge(sP, w)
            if ncx >= 1:
                scalar.wait_ge(sV, wv)
            for tg in range(2):
                scalar.activation(out=ffn_f[:, tg * 512:(tg + 1) * 512],
                                  in_=pf[tg][:, 0:512], func=AF.Identity,
                                  bias=fbiasc[:, ncx:ncx + 1])
            scalar.drain()
            scalar.nop().then_inc(sS, 1)
        cS.add(1)
        ev_fev[n] = cS.v
        emit_outstage(n)

    for ncx in range(32):
        i = 8 + ncx
        qp = i % 2

        @block.scalar
        def _(scalar, ncx=ncx, i=i, qp=qp):
            if i >= 2:
                scalar.wait_ge(sV, ev_q_src_done[i - 2])
            scalar.dma_start(out=x1row2[:, qp * 4096:(qp + 1) * 4096],
                             in_=wf[ncx * 128:(ncx + 1) * 128, :]).then_inc(sXp[qp], 16)
        cXp[qp].add(16)
        ev_ld[i] = (sXp[qp], cXp[qp].v)

        @block.vector
        def _(vector, i=i, qp=qp, w=ev_ld[i]):
            vector.wait_ge(w[0], w[1])
            if i >= 2:
                vector.wait_ge(sP, ev_qr_done[i - 2])
            vector.nop().then_inc(sV, 1)
        cV.add(1)

        _emit_bfp(block, sV, cV, sS, cS, x1row2[:, qp * 4096:(qp + 1) * 4096],
                  qs_f, qi_i, q256a, q256b, q256c, q256d,
                  qxr[:, qp * 4096:(qp + 1) * 4096], INVLN2, LN2C, cst)
        ev_q_src_done[i] = cV.v

        if ncx >= 1:
            emit_ffn_tail(ncx - 1)

        tev = []
        for kk in range(32):
            @block.tensor
            def _(tensor, kk=kk, qp=qp, w=cV.v):
                if kk == 0:
                    tensor.wait_ge(sV, w)
                if kk >= 2:
                    tensor.wait_ge(sV, tev[kk - 2])
                dstp = pt if kk % 2 == 0 else ph
                tensor.transpose(dstp[:, 0:128],
                                 qxr[:, qp * 4096 + kk * 128:qp * 4096 + kk * 128 + 128],
                                 identf[:]).then_inc(sP, 1)
            cP.add(1)

            @block.vector
            def _(vector, kk=kk, w=cP.v):
                vector.wait_ge(sP, w)
                srcp = pt if kk % 2 == 0 else ph
                vector.tensor_copy(out=qwT_sb[:, kk * 128:kk * 128 + 128],
                                   in_=srcp[:, 0:128]).then_inc(sV, 1)
            cV.add(1)
            tev.append(cV.v)
        ev_qr_done[i] = cP.v
        ev_qwT = cV.v

        @block.tensor
        def _(tensor, ncx=ncx, w=ev_qwT):
            tensor.wait_ge(sV, w)
            if ncx >= 1:
                tensor.wait_ge(sS, ev_fev[ncx - 1])
            last = None
            for kk in range(32):
                for tg in range(2):
                    last = tensor.matmul(
                        pf[tg][:, 0:512], qwT_sb[:, kk * 128:kk * 128 + 128],
                        qxT[:, kk * 1024 + tg * 512:kk * 1024 + tg * 512 + 512],
                        start=(kk == 0), stop=(kk == 31))
            last.then_inc(sP, 1)
        cP.add(1)
        ev_fmm[ncx] = cP.v

    emit_ffn_tail(31)

    return _finish(nc, block, es, taps, sV, cV, sG, cG,
                   [(sOp[0], cOp[0].v), (sOp[1], cOp[1].v)], out_t,
                   extra_ctx=[pfc, fctx], all_sems=all_dma_sems)


def _later(d, t, c=None):
    return d.get(t, 0) if isinstance(d, dict) else d[t]


def _dt_math(block, L, sV, cV, sS, cS, sP, cP, upto=99):
    nc_l = L
    dt_raw = L["dt_raw"]; dtb_sb = L["dtb_sb"]; dt_sb = L["dt_sb"]
    lndt_sb = L["lndt_sb"]; alog_sb = L["alog_sb"]; lda_sb = L["lda_sb"]
    ll_sb = L["ll_sb"]; offs_sb = L["offs_sb"]
    ltot_sb = L["ltot_sb"]; wtm = L["wtm"]
    wctm = L["wctm"]; lndtm = L["lndtm"]; dbc = L["dbc"]
    etot_sb = L["etot_sb"]; cumtm = L["cumtm"]; ltm = L["ltm"]
    sm_a = L["sm_a"]; sm_b = L["sm_b"]; dslrow = L["dslrow"]
    identf = L["identf"]; onesrow = L["onesrow"]
    pt = L["pt"]; ph = L["ph"]
    evfree_dt = L["evfree"][0]

    @block.scalar
    def _(scalar):
        scalar.wait_ge(sV, evfree_dt[1])
        scalar.activation(out=dt_sb[:], in_=dt_raw[:], func=AF.Abs, bias=dtb_sb[:])
        scalar.drain()
        scalar.activation(out=dt_sb[:], in_=dt_sb[:], func=AF.Exp, scale=-1.0)
        scalar.drain()
        scalar.activation(out=dt_sb[:], in_=dt_sb[:], func=AF.Ln, bias=1.0)
        scalar.activation(out=lda_sb[:], in_=dt_raw[:], func=AF.Relu, bias=dtb_sb[:])
        scalar.activation(out=sm_a[:, 0:1], in_=alog_sb[:], func=AF.Exp)
        scalar.drain()
        scalar.drain()
        scalar.nop().then_inc(sS, 1)
    cS.add(1)

    if upto < 2:
        return (cV.v, cV.v)

    @block.vector
    def _(vector, w=cS.v):
        vector.wait_ge(sS, w)
        vector.tensor_add(dt_sb[:], dt_sb[:], lda_sb[:])
        vector.tensor_scalar(out=sm_a[:, 1:2], in0=sm_a[:, 0:1], scalar1=-1.0,
                             scalar2=None, op0=ALU.mult)
        vector.memset(offs_sb[:, 15:16], 0.0)
        vector.drain()
        vector.tensor_scalar(out=lda_sb[:], in0=dt_sb[:], scalar1=sm_a[:, 1:2],
                             scalar2=None, op0=ALU.mult)
        vector.drain()
        for c in range(NCHUNK):
            vector.tensor_tensor_scan(
                out=ll_sb[:, c * 128:(c + 1) * 128],
                data0=lda_sb[:, c * 128:(c + 1) * 128],
                data1=offs_sb[:, 15:16].broadcast_to([NH, 128]),
                initial=0.0, op0=ALU.add, op1=ALU.add)
        vector.drain()
        vector.tensor_copy(out=ltot_sb[:], in_=ll_sb[:, 127::128])
        vector.memset(offs_sb[:, 0:1], 0.0)
        vector.drain()
        vector.tensor_tensor_scan(
            out=offs_sb[:, 1:9], data0=ltot_sb[:],
            data1=offs_sb[:, 15:16].broadcast_to([NH, 8]),
            initial=0.0, op0=ALU.add, op1=ALU.add)
        vector.drain()
        vector.tensor_scalar(out=sm_b[:, 0:8],
                             in0=offs_sb[:, 0:8], scalar1=offs_sb[:, 8:9],
                             scalar2=-1.0, op0=ALU.subtract, op1=ALU.mult)
        vector.drain()
        vector.nop().then_inc(sV, 1)
    cV.add(1)

    if upto < 3:
        return (cV.v, cV.v)

    @block.scalar
    def _(scalar, w=cV.v):
        scalar.wait_ge(sV, w)
        scalar.activation(out=lndt_sb[:], in_=dt_sb[:], func=AF.Ln)
        scalar.activation(out=sm_a[:, 2:3], in_=offs_sb[:, 8:9], func=AF.Exp)
        scalar.drain()
        scalar.nop().then_inc(sS, 1)
    cS.add(1)
    ev_base = cS.v

    if upto < 4:
        return (cV.v, cV.v)

    ev_tr = []
    for c in range(NCHUNK):
        @block.scalar
        def _(scalar, c=c, w=ev_base):
            scalar.wait_ge(sS, w)
            if c >= 1:
                scalar.wait_ge(sV, ev_tr[c - 1])
            scalar.activation(out=sm_a[:, 0:128],
                              in_=ll_sb[:, c * 128:(c + 1) * 128], func=AF.Exp,
                              scale=-1.0, bias=sm_b[:, c:c + 1])
            scalar.drain()
            scalar.nop().then_inc(sS, 1)
        cS.add(1)

        @block.vector
        def _(vector, c=c, w=cS.v):
            vector.wait_ge(sS, w)
            vector.tensor_mul(sm_a[:, 0:128], sm_a[:, 0:128],
                              dt_sb[:, c * 128:(c + 1) * 128]).then_inc(sV, 1)
        cV.add(1)

        @block.tensor
        def _(tensor, c=c, w=cV.v):
            tensor.wait_ge(sV, w)
            tensor.transpose(pt[0:128, 0:64], sm_a[0:NH, 0:128],
                             identf[0:NH, 0:NH]).then_inc(sP, 1)
        cP.add(1)

        @block.vector
        def _(vector, c=c, w=cP.v):
            vector.wait_ge(sP, w)
            vector.tensor_copy(out=wtm[:, c * 64:(c + 1) * 64],
                               in_=pt[0:128, 0:64]).then_inc(sV, 1)
        cV.add(1)

        @block.scalar
        def _(scalar, c=c, w=cV.v):
            scalar.wait_ge(sV, w)
            scalar.activation(out=sm_a[:, 0:128],
                              in_=ll_sb[:, c * 128:(c + 1) * 128], func=AF.Exp,
                              scale=-1.0, bias=ltot_sb[:, c:c + 1])
            scalar.drain()
            scalar.nop().then_inc(sS, 1)
        cS.add(1)

        @block.vector
        def _(vector, c=c, w=cS.v):
            vector.wait_ge(sS, w)
            vector.tensor_mul(sm_a[:, 0:128], sm_a[:, 0:128],
                              dt_sb[:, c * 128:(c + 1) * 128])
            vector.tensor_sub(dt_raw[:, 0:128],
                              lndt_sb[:, c * 128:(c + 1) * 128],
                              ll_sb[:, c * 128:(c + 1) * 128])
            vector.drain()
            vector.nop().then_inc(sV, 1)
        cV.add(1)

        @block.tensor
        def _(tensor, c=c, w=cV.v):
            tensor.wait_ge(sV, w)
            tensor.transpose(pt[0:128, 64:128], sm_a[0:NH, 0:128],
                             identf[0:NH, 0:NH])
            tensor.transpose(ph[0:128, 0:64], dt_raw[0:NH, 0:128],
                             identf[0:NH, 0:NH])
            tensor.transpose(ph[0:128, 64:128], ll_sb[0:NH, c * 128:(c + 1) * 128],
                             identf[0:NH, 0:NH]).then_inc(sP, 1)
        cP.add(1)

        @block.vector
        def _(vector, c=c, w=cP.v):
            vector.wait_ge(sP, w)
            vector.tensor_copy(out=wctm[:, c * 64:(c + 1) * 64], in_=pt[0:128, 64:128])
            vector.tensor_copy(out=lndtm[:, c * 64:(c + 1) * 64],
                               in_=ph[0:128, 0:64])
            vector.tensor_copy(out=ltm[:, c * 64:(c + 1) * 64],
                               in_=ph[0:128, 64:128]).then_inc(sV, 1)
        cV.add(1)
        ev_tr.append(cV.v)

        @block.scalar
        def _(scalar, c=c, w=cV.v):
            scalar.wait_ge(sV, w)
            scalar.activation(out=cumtm[:, c * 64:(c + 1) * 64],
                              in_=ltm[:, c * 64:(c + 1) * 64],
                              func=AF.Exp).then_inc(sS, 1)
        cS.add(1)


    if upto < 5:
        return (cV.v, cV.v)

    @block.scalar
    def _(scalar, w=ev_tr[NCHUNK - 1]):
        scalar.wait_ge(sV, w)
        scalar.activation(out=etot_sb[:], in_=ltot_sb[:], func=AF.Exp).then_inc(sS, 1)
    cS.add(1)

    for c in range(NCHUNK):
        @block.vector
        def _(vector, c=c, w=cS.v):
            vector.wait_ge(sS, w)
            if c >= 1:
                vector.wait_ge(sP, cP.v)
            vector.drain()
            vector.tensor_scalar(out=sm_a[0:NH, 0:128], in0=sm_a[0:NH, 0:128],
                                 scalar1=0.0, scalar2=etot_sb[:, c:c + 1],
                                 op0=ALU.mult, op1=ALU.add)
            vector.drain()
            vector.drain()
            vector.nop().then_inc(sV, 1)
        cV.add(1)

        @block.tensor
        def _(tensor, c=c, w=cV.v, ws=cS.v):
            tensor.wait_ge(sV, w)
            tensor.wait_ge(sS, ws)
            tensor.transpose(ph[0:128, 128:192], sm_a[0:NH, 0:128],
                             identf[0:NH, 0:NH]).then_inc(sP, 1)
        cP.add(1)

        @block.vector
        def _(vector, c=c, w=cP.v):
            vector.wait_ge(sP, w)
            vector.tensor_copy(out=dbc[:, c * 64:(c + 1) * 64],
                               in_=ph[0:128, 128:192]).then_inc(sV, 1)
        cV.add(1)

    if upto < 6:
        return (cV.v, cV.v)

    pj0 = L["pj"][0]

    @block.tensor
    def _(tensor, w=cS.v):
        tensor.wait_ge(sS, w)
        tensor.transpose(pj0[0:1, 1280:1344], sm_a[0:NH, 2:3],
                         identf[0:NH, 0:NH]).then_inc(sP, 1)
    cP.add(1)

    @block.vector
    def _(vector, w=cP.v):
        vector.wait_ge(sP, w)
        vector.tensor_copy(out=dslrow[:], in_=pj0[0:1, 1280:1344]).then_inc(sV, 1)
    cV.add(1)
    ev_dsl = cV.v
    ev_smalls_done = cV.v
    return ev_smalls_done, ev_dsl


def _emit_bfp(block, sV, cV, sS, cS, srcf, qs_f, qi_i, q256a, q256b, q256c,
              q256d, dstf, invln2, ln2c, cstq=None):
    @block.vector
    def _(vector, w=cV.v):
        vector.wait_ge(sV, w)
        vector.tensor_reduce(
            out=q256a[:], in_=srcf.rearrange("p (g k) -> p g k", k=16),
            axis=AX, op=ALU.max, apply_absolute_value=True)
        vector.drain()
        vector.tensor_scalar(out=q256a[:], in0=q256a[:], scalar1=1e-38,
                             scalar2=None, op0=ALU.max)
        vector.drain()
        vector.drain()
        vector.nop().then_inc(sV, 1)
    cV.add(1)

    @block.scalar
    def _(scalar, w=cV.v):
        scalar.wait_ge(sV, w)
        scalar.activation(out=q256b[:], in_=q256a[:], func=AF.Ln).then_inc(sS, 1)
    cS.add(1)

    @block.vector
    def _(vector, w=cS.v):
        vector.wait_ge(sS, w)
        vector.tensor_scalar(out=q256b[:], in0=q256b[:], scalar1=invln2,
                             scalar2=None, op0=ALU.mult)
        vector.drain()
        vector.tensor_copy(out=qi_i[:, 0:256], in_=q256b[:])
        vector.drain()
        vector.tensor_copy(out=q256c[:], in_=qi_i[:, 0:256])
        vector.drain()
        vector.tensor_tensor(out=q256d[:], in0=q256c[:], in1=q256b[:], op=ALU.is_gt)
        vector.drain()
        vector.tensor_sub(q256c[:], q256c[:], q256d[:])
        vector.drain()
        vector.drain()
        vector.nop().then_inc(sV, 1)
    cV.add(1)

    @block.scalar
    def _(scalar, w=cV.v):
        scalar.wait_ge(sV, w)
        scalar.activation(out=q256a[:], in_=q256c[:], func=AF.Exp,
                          scale=ln2c, bias=cstq[:, 1:2])
        scalar.activation(out=q256b[:], in_=q256c[:], func=AF.Exp,
                          scale=-ln2c, bias=cstq[:, 2:3])
        scalar.drain()
        scalar.nop().then_inc(sS, 1)
    cS.add(1)

    @block.vector
    def _(vector, w=cS.v):
        vector.wait_ge(sS, w)
        vector.tensor_tensor(
            out=qs_f[:].rearrange("p (g k) -> p g k", k=16),
            in0=srcf.rearrange("p (g k) -> p g k", k=16),
            in1=q256b[:].unsqueeze(2).broadcast_to([128, 256, 16]), op=ALU.mult)
        vector.drain()
        vector.tensor_scalar(out=qs_f[:], in0=qs_f[:], scalar1=127.49999,
                             scalar2=-127.49999, op0=ALU.min, op1=ALU.max)
        vector.drain()
        vector.tensor_copy(out=qi_i[:], in_=qs_f[:])
        vector.drain()
        vector.tensor_tensor(
            out=dstf.rearrange("p (g k) -> p g k", k=16),
            in0=qi_i[:].rearrange("p (g k) -> p g k", k=16),
            in1=q256a[:].unsqueeze(2).broadcast_to([128, 256, 16]),
            op=ALU.mult).then_inc(sV, 1)
    cV.add(1)


def _finish(nc, block, es, taps, sV, cV, sG, cG, out_waits, out_t, extra_ctx=(),
            all_sems=()):
    @block.gpsimd
    def _(gpsimd):
        gpsimd.wait_ge(sV, cV.v)
        for d, src in taps:
            gpsimd.wait_ge(sG, cG.v)
            gpsimd.dma_start(out=d, in_=src).then_inc(sG, 16)
            cG.add(16)
        gpsimd.wait_ge(sG, cG.v)
        for s_, v_ in out_waits:
            gpsimd.wait_ge(s_, v_)
        for s_, c_ in all_sems:
            gpsimd.wait_ge(s_, c_.v)

    for ctx in extra_ctx:
        ctx.close()
    es.close()
    return nc


def _host_prep(inputs):
    x = np.ascontiguousarray(np.asarray(inputs["L0_attn_hidden_states"], np.float32))
    Wi = np.asarray(inputs["L0_attn_W_in"], np.float32)
    convw = np.asarray(inputs["L0_attn_conv_w"], np.float32)
    dtb = np.asarray(inputs["L0_attn_dt_bias"], np.float32)
    alog = np.asarray(inputs["L0_attn_A_log"], np.float32)
    dpar = np.asarray(inputs["L0_attn_D_param"], np.float32)
    gnorm = np.asarray(inputs["L0_attn_norm_weight"], np.float32)
    Wo = np.asarray(inputs["L0_attn_W_out"], np.float32)
    Wf = np.asarray(inputs["L0_ffn_weight"], np.float32)
    fb = np.asarray(inputs["L0_ffn_bias"], np.float32)

    wi_t = np.zeros((H, NCC * 128), np.float32)
    wi_t[:, :16704] = Wi
    wi_t = np.ascontiguousarray(
        wi_t.reshape(2, 16, 128, NCC, 128).transpose(3, 0, 2, 1, 4))
    wo_t = np.ascontiguousarray(
        Wo.reshape(64, 128, 8, 512).transpose(2, 0, 1, 3))
    convw_t = np.ascontiguousarray(convw.reshape(66, 128, 4).transpose(1, 0, 2)
                                   ).reshape(128, 264)
    dpar_bc = np.ascontiguousarray(np.broadcast_to(dpar[None, :], (128, NH)))
    gnorm_t = np.ascontiguousarray(gnorm.reshape(NH, 128).T)
    fbias_t = np.ascontiguousarray(fb.reshape(32, 128).T)
    j_idx = np.arange(128)
    maskbias = np.where(j_idx[:, None] <= j_idx[None, :], 0.0, -1e30).astype(np.float32)
    ident = np.eye(128, dtype=np.float32)

    shared = dict(wi=wi_t, wo=wo_t, wf=Wf, convw_t=convw_t,
                  dtb=dtb.reshape(NH, 1), alog=alog.reshape(NH, 1),
                  dpar=dpar_bc, gnorm=gnorm_t, fbias=fbias_t,
                  maskb=maskbias, maskbT=np.ascontiguousarray(maskbias.T),
                  identb=ident)

    in_maps = []
    for core in range(8):
        b, rk = core // 4, core % 4
        xsl = np.zeros((TOKX, H), np.float32)
        if rk > 0:
            xsl[0:3] = x[b, rk * 1024 - 3:rk * 1024]
        xsl[3:] = x[b, rk * 1024:(rk + 1) * 1024]
        cm = np.zeros((8, 16), np.float32)
        grp = range((core // 4) * 4, (core // 4) * 4 + 4)
        msel = np.zeros(8, np.float32)
        for j in grp:
            rj = j % 4
            if rj < rk:
                msel[j] = 1.0
                for k in grp:
                    if rj < (k % 4) < rk:
                        cm[j, k] = 1.0
        cm[:, 8:16] = 1.0 - cm[:, 0:8]
        cm[:, 15] = msel
        in_maps.append(dict(xs=np.ascontiguousarray(xsl), cmask=cm, **shared))
    return x, in_maps


_NC_CACHE = {}


def kernel(**inputs):
    x, in_maps = _host_prep(inputs)
    key = ("full",)
    if key not in _NC_CACHE:
        _NC_CACHE[key] = build()
    nc = _NC_CACHE[key]
    res = run_bass_kernel_spmd(nc, in_maps, core_ids=list(range(8)))
    out = np.zeros_like(x)
    for core in range(8):
        b, rk = core // 4, core % 4
        out[b, rk * 1024:(rk + 1) * 1024] = res.results[core]["out"]
    return out


# version 16 (final)
# speedup vs baseline: 1.2055x; 1.0412x over previous
import numpy as np
from contextlib import ExitStack

import concourse.bass as bass
import concourse.mybir as mybir
from concourse.bass_utils import run_bass_kernel_spmd

F32 = mybir.dt.float32
BF16 = mybir.dt.bfloat16
I32 = mybir.dt.int32
AF = mybir.ActivationFunctionType
ALU = mybir.AluOpType
AX = mybir.AxisListType.X

H = 4096
DSSM = 8192
NH = 64
EPS = 1e-5
CH = 128
NCHUNK = 8
TOK = 1024
TOKX = 1027
LN2C = float(np.log(2.0))
INVLN2 = float(1.0 / np.log(2.0))
NCC = 131
CC_B, CC_C, CC_DT = 128, 129, 130
SLOTS = [0, 1, 2, 4, 5, 6]


class Ct:
    __slots__ = ("v",)

    def __init__(self):
        self.v = 0

    def add(self, n=1):
        self.v += n
        return self.v


def build(phase_limit=99, want_taps=(), max_stripes=None, dbg_mode=99):
    want_taps = set(want_taps)
    nc = bass.Bass("TRN2", target_bir_lowering=False, debug=False)
    es = ExitStack()

    def sb(name, shape, dtype):
        return es.enter_context(nc.sbuf_tensor(name, shape, dtype))

    xs = nc.dram_tensor("xs", [TOKX, H], F32, kind="ExternalInput").ap()
    wi = nc.dram_tensor("wi", [NCC, 2, 128, 16, 128], F32, kind="ExternalInput").ap()
    wo = nc.dram_tensor("wo", [8, 64, 128, 512], F32, kind="ExternalInput").ap()
    wf = nc.dram_tensor("wf", [4096, 4096], F32, kind="ExternalInput").ap()
    convw_in = nc.dram_tensor("convw_t", [128, 264], F32, kind="ExternalInput").ap()
    dtb_in = nc.dram_tensor("dtb", [NH, 1], F32, kind="ExternalInput").ap()
    alog_in = nc.dram_tensor("alog", [NH, 1], F32, kind="ExternalInput").ap()
    dpar_in = nc.dram_tensor("dpar", [128, NH], F32, kind="ExternalInput").ap()
    gnorm_in = nc.dram_tensor("gnorm", [128, NH], F32, kind="ExternalInput").ap()
    fbias_in = nc.dram_tensor("fbias", [128, 32], F32, kind="ExternalInput").ap()
    mask_in = nc.dram_tensor("maskb", [128, 128], F32, kind="ExternalInput").ap()
    maskT_in = nc.dram_tensor("maskbT", [128, 128], F32, kind="ExternalInput").ap()
    ident_in = nc.dram_tensor("identb", [128, 128], F32, kind="ExternalInput").ap()
    cmask_in = nc.dram_tensor("cmask", [8, 16], F32, kind="ExternalInput").ap()
    out_t = nc.dram_tensor("out", [TOK, H], F32, kind="ExternalOutput").ap()

    xtm_st = nc.dram_tensor("xtm_st", [NCHUNK, NH, 128, 128], BF16).ap()
    sz_st = nc.dram_tensor("sz_st", [NCHUNK, NH, 128, 128], BF16).ap()
    yn_st = nc.dram_tensor("yn_st", [NH, 128, TOK], BF16).ap()
    x1_st = nc.dram_tensor("x1_st", [TOK, H], F32).ap()
    cc_in = nc.dram_tensor("cc_in", [128, DSSM], BF16)
    cc_out = nc.dram_tensor("cc_out", [1024, DSSM], BF16)
    ccd_in = nc.dram_tensor("ccd_in", [1, NH], F32)
    ccd_out = nc.dram_tensor("ccd_out", [8, NH], F32)

    taps = []

    def tap(name, src_ap):
        if name in want_taps:
            d = nc.dram_tensor("tap_" + name, list(src_ap.shape), src_ap.dtype,
                               kind="ExternalOutput").ap()
            taps.append((d, src_ap))

    sK = es.enter_context(nc.semaphore("sK"))
    sWp = [es.enter_context(nc.semaphore(f"sW{i}")) for i in range(2)]
    sXp = [es.enter_context(nc.semaphore(f"sX{i}")) for i in range(2)]
    sXc = es.enter_context(nc.semaphore("sXc"))
    sTp = [es.enter_context(nc.semaphore(f"sT{i}")) for i in range(4)]
    sZp = [es.enter_context(nc.semaphore(f"sZ{i}")) for i in range(2)]
    sNp = [es.enter_context(nc.semaphore(f"sN{i}")) for i in range(2)]
    sYp = [es.enter_context(nc.semaphore(f"sY{i}")) for i in range(2)]
    sOp = [es.enter_context(nc.semaphore(f"sO{i}")) for i in range(2)]
    sG = es.enter_context(nc.semaphore("sG"))
    sP = es.enter_context(nc.semaphore("sP"))
    sV = es.enter_context(nc.semaphore("sV"))
    sS = es.enter_context(nc.semaphore("sS"))
    sC = es.enter_context(nc.semaphore("sC"))
    cK = Ct()
    cWp = [Ct(), Ct()]
    cXp = [Ct(), Ct()]
    cXc = Ct()
    cTp = [Ct() for _ in range(4)]
    cZp = [Ct(), Ct()]
    cNp = [Ct(), Ct()]
    cYp = [Ct(), Ct()]
    cOp = [Ct(), Ct()]
    cG, cP, cV, cS, cC = (Ct() for _ in range(5))
    all_dma_sems = ([(sK, cK), (sXc, cXc), (sG, cG)] +
                    list(zip(sWp, cWp)) + list(zip(sXp, cXp)) +
                    list(zip(sTp, cTp)) + list(zip(sZp, cZp)) +
                    list(zip(sNp, cNp)) + list(zip(sYp, cYp)) +
                    list(zip(sOp, cOp)))

    convw_sb = sb("convw_sb", [128, 264], F32)
    dtb_sb = sb("dtb_sb", [NH, 1], F32)
    alog_sb = sb("alog_sb", [NH, 1], F32)
    dparc = sb("dparc", [128, NH], F32)
    gnormc = sb("gnormc", [128, NH], F32)
    fbiasc = sb("fbiasc", [128, 32], F32)
    identf = sb("identf_s", [128, 128], F32)
    identb = sb("identb_s", [128, 128], BF16)
    onesrow = sb("onesrow", [1, 128], F32)
    onescol_bf = sb("onescol_bf", [128, 1], BF16)
    cmask_sb = sb("cmask_sb", [8, 16], F32)
    cst = sb("cst", [128, 4], F32)

    msctx = ExitStack()
    b_fm = sb("b_fm", [128, TOK], BF16)
    c_fm = sb("c_fm", [128, TOK], BF16)
    b_tm = sb("b_tm", [128, NCHUNK * 128], BF16)
    dt_sb = msctx.enter_context(nc.sbuf_tensor("dt_sb", [NH, TOK], F32, side="right"))
    lda_sb = msctx.enter_context(nc.sbuf_tensor("lda_sb", [NH, TOK], F32, side="right"))
    lndt_sb = msctx.enter_context(nc.sbuf_tensor("lndt_sb", [NH, TOK], F32, side="right"))
    ll_sb = msctx.enter_context(nc.sbuf_tensor("ll_sb", [NH, TOK], F32, side="right"))
    offs_sb = sb("offs_sb", [NH, 16], F32)
    ltot_sb = sb("ltot_sb", [NH, 8], F32)
    etot_sb = sb("etot_sb", [NH, 8], F32)
    cumtm = msctx.enter_context(nc.sbuf_tensor("cumtm", [128, NCHUNK * 64], F32, side="right"))
    maskT_sb = sb("maskT_sb", [128, 128], F32)
    dgT_sb = sb("dgT_sb", [NH, 8], F32)
    coefT_sb = sb("coefT_sb", [NH, 8], F32)
    wtm = msctx.enter_context(nc.sbuf_tensor("wtm", [128, NCHUNK * 64], F32, side="right"))
    ltm = msctx.enter_context(nc.sbuf_tensor("ltm", [128, NCHUNK * 64], F32, side="right"))
    wctm = msctx.enter_context(nc.sbuf_tensor("wctm", [128, NCHUNK * 64], F32, side="right"))
    lndtm = msctx.enter_context(nc.sbuf_tensor("lndtm", [128, NCHUNK * 64], F32, side="right"))
    dbc = msctx.enter_context(nc.sbuf_tensor("dbc", [128, NCHUNK * 64], F32, side="right"))
    sm_a = sb("sm_a", [NH, 128], F32)
    sm_b = sb("sm_b", [NH, 128], F32)
    dslrow = sb("dslrow", [1, NH], F32)
    coef_sb = sb("coef_sb", [8, NH], F32)
    coefbc = sb("coefbc", [128, 8 * NH], F32)
    dg_sb = sb("dg_sb", [8, NH], F32)

    pt = es.enter_context(nc.psum_tensor("pt", [128, 512], F32))
    ph = es.enter_context(nc.psum_tensor("ph", [128, 512], F32))

    block = es.enter_context(nc.Block())

    lnxt_ctx = ExitStack()
    lnxt = lnxt_ctx.enter_context(nc.sbuf_tensor("lnxt", [128, 32 * 1028], BF16))
    ln_ctx = ExitStack()
    x_tile = ln_ctx.enter_context(nc.sbuf_tensor("x_tile", [128, 2 * H], F32))
    xc_f = ln_ctx.enter_context(nc.sbuf_tensor("xc_f", [128, 2 * H], F32))
    ln_small = ln_ctx.enter_context(nc.sbuf_tensor("ln_small", [128, 16], F32))
    rrow = ln_ctx.enter_context(nc.sbuf_tensor("rrow", [1, 128], F32))
    rbcast = ln_ctx.enter_context(nc.sbuf_tensor("rbcast", [128, 128], F32))
    sq_sink = ln_ctx.enter_context(nc.sbuf_tensor("sq_sink", [128, H], BF16))
    pjs = ExitStack()
    pj = [pjs.enter_context(nc.psum_tensor(f"pj{i}", [128, 1536], F32)) for i in range(2)]

    @block.sync
    def _(sync):
        for dst, src in ((convw_sb[:], convw_in), (dtb_sb[:], dtb_in),
                         (alog_sb[:], alog_in), (dparc[:], dpar_in),
                         (gnormc[:], gnorm_in), (fbiasc[:], fbias_in),
                         (maskT_sb[:], maskT_in), (identf[:], ident_in),
                         (cmask_sb[:], cmask_in)):
            sync.dma_start(out=dst, in_=src).then_inc(sK, 16)
            cK.add(16)

    @block.vector
    def _(vector):
        vector.wait_ge(sK, cK.v)
        vector.memset(onesrow[:], 1.0)
        vector.memset(onescol_bf[:], 1.0)
        vector.memset(cst[:, 0:1], EPS)
        vector.memset(cst[:, 1:2], -7.0 * LN2C)
        vector.memset(cst[:, 2:3], 7.0 * LN2C)
        vector.tensor_copy(out=identb[:], in_=identf[:])
        vector.drain()
        vector.drain()
        vector.nop().then_inc(sV, 1)
    cV.add(1)
    ev_consts = cV.v

    xrows = [(t * 128, 128) for t in range(8)] + [(1024, 3)]
    ln_v = [None] * 9
    ln_pe = [None] * 9
    ln_xc = [None] * 9

    for tt, (r0, nr) in enumerate(xrows):
        q = tt % 2

        @block.scalar
        def _(scalar, tt=tt, r0=r0, nr=nr, q=q):
            if tt >= 2:
                scalar.wait_ge(sP, ln_pe[tt - 2])
                scalar.wait_ge(sV, ln_xc[tt - 2])
            scalar.dma_start(out=x_tile[0:nr, q * H:q * H + H],
                             in_=xs[r0:r0 + nr, :]).then_inc(sXp[q], 16)
        cXp[q].add(16)
        ev_xload = (sXp[q], cXp[q].v)

        @block.scalar
        def _(scalar, tt=tt, nr=nr, q=q, ev_xload=ev_xload):
            scalar.wait_ge(ev_xload[0], ev_xload[1])
            if tt == 0:
                scalar.wait_ge(sV, ev_consts)
            scalar.activation(out=sq_sink[0:nr, :], in_=x_tile[0:nr, q * H:q * H + H],
                              func=AF.Square).then_inc(sS, 1)
        cS.add(1)
        ev_sq = cS.v

        @block.vector
        def _(vector, tt=tt, nr=nr, q=q, ev_xload=ev_xload, ev_sq=ev_sq):
            vector.wait_ge(ev_xload[0], ev_xload[1])
            vector.wait_ge(sS, ev_sq)
            xt = x_tile[0:nr, q * H:q * H + H]
            vector.reduce_sum(ln_small[0:nr, 1:2], sq_sink[0:nr, :], axis=AX)
            vector.reduce_sum(ln_small[0:nr, 0:1], xt, axis=AX)
            vector.drain()
            vector.tensor_scalar(out=ln_small[0:nr, 1:2], in0=ln_small[0:nr, 1:2],
                                 scalar1=1.0 / H, scalar2=None, op0=ALU.mult)
            vector.tensor_scalar(out=ln_small[0:nr, 0:1], in0=ln_small[0:nr, 0:1],
                                 scalar1=1.0 / H, scalar2=None, op0=ALU.mult)
            vector.drain()
            vector.tensor_tensor(out=ln_small[0:nr, 2:3], in0=ln_small[0:nr, 0:1],
                                 in1=ln_small[0:nr, 0:1], op=ALU.mult)
            vector.tensor_scalar(out=ln_small[0:nr, 5:6], in0=ln_small[0:nr, 0:1],
                                 scalar1=-1.0, scalar2=None, op0=ALU.mult)
            vector.drain()
            vector.tensor_sub(ln_small[0:nr, 2:3], ln_small[0:nr, 1:2],
                              ln_small[0:nr, 2:3])
            vector.drain()
            vector.drain()
            vector.nop().then_inc(sV, 1)
        cV.add(1)
        ev_stats = cV.v

        @block.scalar
        def _(scalar, nr=nr, ev_stats=ev_stats):
            scalar.wait_ge(sV, ev_stats)
            scalar.activation(out=ln_small[0:nr, 3:4], in_=ln_small[0:nr, 2:3],
                              func=AF.Sqrt, bias=cst[0:nr, 0:1]).then_inc(sS, 1)
        cS.add(1)
        ev_sqrt = cS.v

        @block.vector
        def _(vector, nr=nr, q=q, ev_sqrt=ev_sqrt):
            vector.wait_ge(sS, ev_sqrt)
            vector.reciprocal(out=ln_small[0:nr, 4:5], in_=ln_small[0:nr, 3:4])
            vector.drain()
            vector.tensor_scalar(out=xc_f[0:nr, q * H:q * H + H],
                                 in0=x_tile[0:nr, q * H:q * H + H],
                                 scalar1=ln_small[0:nr, 5:6], scalar2=None,
                                 op0=ALU.add)
            vector.drain()
            vector.drain()
            vector.nop().then_inc(sV, 1)
        cV.add(1)
        ln_xc[tt] = cV.v

        @block.tensor
        def _(tensor, tt=tt, nr=nr):
            tensor.wait_ge(sV, ln_xc[tt])
            if tt >= 1:
                tensor.wait_ge(sV, ln_v[tt - 1])
            tensor.transpose(ph[0:1, 0:nr], ln_small[0:nr, 4:5],
                             identf[0:nr, 0:nr]).then_inc(sP, 1)
        cP.add(1)

        @block.vector
        def _(vector, nr=nr, w=cP.v):
            vector.wait_ge(sP, w)
            vector.tensor_copy(out=rrow[0:1, 0:nr], in_=ph[0:1, 0:nr]).then_inc(sV, 1)
        cV.add(1)

        @block.tensor
        def _(tensor, nr=nr, w=cV.v):
            tensor.wait_ge(sV, w)
            tensor.matmul(ph[:, 256:256 + nr], onesrow[:], rrow[0:1, 0:nr],
                          start=True, stop=True).then_inc(sP, 1)
        cP.add(1)

        @block.vector
        def _(vector, nr=nr, w=cP.v):
            vector.wait_ge(sP, w)
            vector.tensor_copy(out=rbcast[:, 0:nr],
                               in_=ph[:, 256:256 + nr]).then_inc(sV, 1)
        cV.add(1)
        ev_rb = cV.v

        evac_ev = []
        for fc in range(32):
            @block.tensor
            def _(tensor, tt=tt, nr=nr, q=q, fc=fc):
                if fc == 0:
                    tensor.wait_ge(sV, ev_rb)
                if fc >= 2:
                    tensor.wait_ge(sV, evac_ev[fc - 2])
                dstp = pt if fc % 2 == 0 else ph
                tensor.transpose(dstp[:, 0:nr],
                                 xc_f[0:nr, q * H + fc * 128:q * H + fc * 128 + 128],
                                 identf[0:nr, 0:nr]).then_inc(sP, 1)
            cP.add(1)

            @block.vector
            def _(vector, r0=r0, nr=nr, fc=fc, w=cP.v):
                vector.wait_ge(sP, w)
                srcp = pt if fc % 2 == 0 else ph
                vector.tensor_tensor(
                    out=lnxt[:, fc * 1028 + r0:fc * 1028 + r0 + nr],
                    in0=srcp[:, 0:nr], in1=rbcast[:, 0:nr],
                    op=ALU.mult).then_inc(sV, 1)
            cV.add(1)
            evac_ev.append(cV.v)
        ln_v[tt] = cV.v
        ln_pe[tt] = cP.v

    tap("lnxt", lnxt[:, :])

    if phase_limit < 1:
        return _finish(nc, block, es, taps, sV, cV, sG, cG, [], out_t,
                       extra_ctx=[pjs, ln_ctx, lnxt_ctx], all_sems=all_dma_sems)

    ln_ctx.close()
    sctx = ExitStack()
    s_all = sctx.enter_context(nc.sbuf_tensor("s_all", [128, NH * 128], F32, side="right"))
    phA_ctx = ExitStack()
    exch = phA_ctx.enter_context(nc.sbuf_tensor("exch", [128, NH * 128], BF16))
    wbf = phA_ctx.enter_context(nc.sbuf_tensor("wbf", [128, 2 * 2048], BF16))
    conv_acc = phA_ctx.enter_context(nc.sbuf_tensor("conv_acc", [128, TOK], F32))
    xch_f = phA_ctx.enter_context(nc.sbuf_tensor("xch_f", [128, 2 * TOK], F32))
    bc_f = phA_ctx.enter_context(nc.sbuf_tensor("bc_f", [128, TOK], F32))
    dt_raw = phA_ctx.enter_context(nc.sbuf_tensor("dt_raw", [NH, TOK], F32))
    xw_sb = phA_ctx.enter_context(nc.sbuf_tensor("xw_sb", [128, 2 * 128], BF16))
    xtm_sb = phA_ctx.enter_context(nc.sbuf_tensor("xtm_sb", [128, 2 * 128], BF16))
    szh_bf = phA_ctx.enter_context(nc.sbuf_tensor("szh_bf", [128, 2 * TOK], BF16))
    slot_sb = phA_ctx.enter_context(nc.sbuf_tensor("slot_sb", [128, DSSM // 4], BF16))

    @block.vector
    def _(vector):
        vector.memset(conv_acc[:], 0.0).then_inc(sV, 1)
    cV.add(1)

    cc_order = [CC_DT, CC_B, CC_C] + list(range(64, 128)) + list(range(0, 64))
    if max_stripes is not None:
        cc_order = cc_order[:max_stripes]
    NSTRIPE = len(cc_order)
    evloadH = [None] * (2 * NSTRIPE)
    evcastH = [None] * (2 * NSTRIPE)
    evmmH = [None] * (2 * NSTRIPE)
    evmm = [None] * NSTRIPE
    evfree = [None] * NSTRIPE
    evsilu = [None] * NSTRIPE
    ev_hloc_ev = {}
    ev_xtm_dma = {}
    btm_ev = []
    ev_xtm_ev = {}
    ev_xtm_T = {}
    ev_smalls_done = None
    ev_dsl = None
    ev_sz_dma = {}

    GRP_FULL = [(0, 512), (512, 512), (1024, 3)]
    GRP_OWN = [(3, 0, 512), (515, 512, 512)]

    def xbc_ci(cc):
        if cc == CC_B:
            return 64
        if cc == CC_C:
            return 65
        return cc - 64

    last_ln = ln_v[8]

    def emit_tblock(ii, cc2):
        is_b2 = cc2 == CC_B
        is_xh2 = 64 <= cc2 < 128
        if is_b2:
            btm_ev.clear()
            for c in range(NCHUNK):
                @block.tensor
                def _(tensor, i=ii, c=c):
                    if c == 0:
                        tensor.wait_ge(sS, evsilu[i])
                        tensor.wait_ge(sV, ev_smalls_done)
                    if c >= 2:
                        tensor.wait_ge(sV, btm_ev[c - 2])
                    dstp = pt if c % 2 == 0 else ph
                    tensor.transpose(dstp[:, 0:128], bc_f[:, c * 128:c * 128 + 128],
                                     identf[:]).then_inc(sP, 1)
                cP.add(1)

                @block.vector
                def _(vector, c=c, w=cP.v):
                    vector.wait_ge(sP, w)
                    srcp = pt if c % 2 == 0 else ph
                    vector.tensor_copy(out=b_tm[:, c * 128:(c + 1) * 128],
                                       in_=srcp[:, 0:128]).then_inc(sV, 1)
                cV.add(1)
                btm_ev.append(cV.v)
        if is_xh2:
            h = cc2 - 64
            hq = ii % 2
            for c in range(NCHUNK):
                @block.tensor
                def _(tensor, i=ii, h=h, c=c, hq=hq):
                    if c == 0:
                        tensor.wait_ge(sS, evsilu[i])
                        tensor.wait_ge(sV, ev_smalls_done)
                        if h >= 1:
                            tensor.wait_ge(sV, ev_hloc_ev[h - 1])
                    if c >= 2:
                        tensor.wait_ge(sV, ev_xtm_ev[(h, c - 2)])
                    dstp = pt if c % 2 == 0 else ph
                    tensor.transpose(dstp[:, 0:128],
                                     xch_f[:, hq * TOK + c * 128:hq * TOK + c * 128 + 128],
                                     identf[:]).then_inc(sP, 1)
                cP.add(1)
                ev_xtm_T[(h, c)] = cP.v

                @block.vector
                def _(vector, h=h, c=c, w=cP.v):
                    vector.wait_ge(sP, w)
                    if (h, c - 2) in ev_xtm_dma:
                        vector.wait_ge(sTp[(c - 2) % 4], ev_xtm_dma[(h, c - 2)])
                    elif c < 2 and h >= 1 and (h - 1, 6 + c) in ev_xtm_dma:
                        vector.wait_ge(sTp[(6 + c) % 4], ev_xtm_dma[(h - 1, 6 + c)])
                    srcp = pt if c % 2 == 0 else ph
                    vector.tensor_copy(out=xtm_sb[:, (c % 2) * 128:(c % 2) * 128 + 128],
                                       in_=srcp[:, 0:128])
                    vector.tensor_scalar(
                        out=xw_sb[:, (c % 2) * 128:(c % 2) * 128 + 128],
                        in0=srcp[:, 0:128],
                        scalar1=wtm[:, c * 64 + h:c * 64 + h + 1], scalar2=None,
                        op0=ALU.mult).then_inc(sV, 1)
                cV.add(1)
                ev_xtm_ev[(h, c)] = cV.v

                @block.sync
                def _(sync, h=h, c=c, w=cV.v):
                    sync.wait_ge(sV, w)
                    sync.dma_start(out=xtm_st[c, h],
                                   in_=xtm_sb[:, (c % 2) * 128:(c % 2) * 128 + 128]
                                   ).then_inc(sTp[c % 4], 16)
                cTp[c % 4].add(16)
                ev_xtm_dma[(h, c)] = cTp[c % 4].v

                @block.tensor
                def _(tensor, h=h, c=c, ii=ii, w=cV.v):
                    q2 = ii % 2
                    tensor.wait_ge(sV, w)
                    tensor.matmul(pj[q2][:, 1152:1280],
                                  b_tm[:, c * 128:(c + 1) * 128],
                                  xw_sb[:, (c % 2) * 128:(c % 2) * 128 + 128],
                                  start=(c == 0), stop=(c == NCHUNK - 1)).then_inc(sP, 1)
                cP.add(1)
            ev_hloc_mm = cP.v

            @block.vector
            def _(vector, h=h, ii=ii, w=ev_hloc_mm):
                q2 = ii % 2
                vector.wait_ge(sP, w)
                vector.tensor_copy(out=exch[:, h * 128:(h + 1) * 128],
                                   in_=pj[q2][:, 1152:1280]).then_inc(sV, 1)
            cV.add(1)
            ev_hloc_ev[h] = cV.v
            evfree[ii] = (sV, cV.v)

            if h == 63:
                @block.gpsimd
                def _(gpsimd):
                    gpsimd.wait_ge(sV, ev_hloc_ev[63])
                    gpsimd.wait_ge(sV, ev_dsl)
                    gpsimd.dma_start(out=cc_in.ap(), in_=exch[:]).then_inc(sG, 16)
                    gpsimd.dma_start(out=ccd_in.ap(), in_=dslrow[:]).then_inc(sG, 16)
                    gpsimd.wait_ge(sG, cG.v + 32)
                    gpsimd.collective_compute(
                        "AllGather", ALU.bypass, replica_groups=[list(range(8))],
                        ins=[cc_in.ap().opt()], outs=[cc_out.ap().opt()]).then_inc(sC, 1)
                    gpsimd.collective_compute(
                        "AllGather", ALU.bypass, replica_groups=[list(range(8))],
                        ins=[ccd_in.ap().opt()], outs=[ccd_out.ap().opt()]).then_inc(sC, 1)
                cG.add(32)
                cC.add(2)


    for i, cc in enumerate(cc_order):
        q = i % 2
        is_dt = cc == CC_DT
        is_b = cc == CC_B
        is_c = cc == CC_C
        is_xh = 64 <= cc < 128
        is_z = cc < 64
        h = cc - 64 if is_xh else None

        for hf in range(2):
            j = 2 * i + hf
            qj = j % 2

            @block.gpsimd
            def _(gpsimd, i=i, cc=cc, hf=hf, j=j, qj=qj):
                if j < 2:
                    gpsimd.wait_ge(sV, last_ln)
                if j >= 2:
                    gpsimd.wait_ge(sP, evmmH[j - 2])
                gpsimd.dma_start(out=wbf[:, qj * 2048:(qj + 1) * 2048],
                                 in_=wi[cc, hf]).then_inc(sWp[qj], 16)
            cWp[qj].add(16)
            evloadH[j] = (sWp[qj], cWp[qj].v)

            @block.tensor
            def _(tensor, i=i, q=q, is_z=is_z, hf=hf, j=j, qj=qj):
                if dbg_mode < 2:
                    tensor.nop().then_inc(sP, 1)
                    return
                tensor.wait_ge(evloadH[j][0], evloadH[j][1])
                if hf == 0:
                    if i == 0:
                        tensor.wait_ge(sV, last_ln)
                    if i >= 2:
                        s, v = evfree[i - 2]
                        tensor.wait_ge(s, v)
                last = None
                for kk16 in range(16):
                    kk = hf * 16 + kk16
                    a = wbf[:, qj * 2048 + kk16 * 128:qj * 2048 + kk16 * 128 + 128]
                    if is_z:
                        for (ro, po, n) in GRP_OWN:
                            last = tensor.matmul(
                                pj[q][:, po:po + n], a,
                                lnxt[:, kk * 1028 + ro:kk * 1028 + ro + n],
                                start=(kk == 0), stop=(kk == 31))
                    else:
                        for (o, n) in GRP_FULL:
                            last = tensor.matmul(
                                pj[q][:, o:o + n], a,
                                lnxt[:, kk * 1028 + o:kk * 1028 + o + n],
                                start=(kk == 0), stop=(kk == 31))
                last.then_inc(sP, 1)
            cP.add(1)
            evmmH[j] = cP.v
        evmm[i] = evmmH[2 * i + 1]

        if i >= 1:
            emit_tblock(i - 1, cc_order[i - 1])
        if i == NSTRIPE - 1:
            _defer_last = True

        if is_dt:
            @block.vector
            def _(vector, i=i, q=q):
                vector.wait_ge(sP, evmm[i])
                if dbg_mode < 3:
                    vector.memset(dt_raw[:, :], 0.1)
                    vector.drain()
                    vector.nop().then_inc(sV, 1)
                    return
                vector.tensor_copy(out=dt_raw[:, :],
                                   in_=pj[q][0:NH, 3:TOKX]).then_inc(sV, 1)
            cV.add(1)
            evfree[i] = (sV, cV.v)
        elif is_z:
            zi = cc

            @block.scalar
            def _(scalar, i=i, q=q, zi=zi):
                scalar.wait_ge(sP, evmm[i])
                if zi >= 2:
                    scalar.wait_ge(sZp[zi % 2], ev_sz_dma[zi - 2])
                scalar.activation(out=szh_bf[:, (zi % 2) * TOK:(zi % 2 + 1) * TOK],
                                  in_=pj[q][:, 0:TOK],
                                  func=AF.Silu).then_inc(sS, 1)
            cS.add(1)
            evfree[i] = (sS, cS.v)
            evsilu[i] = cS.v

            @block.sync
            def _(sync, zi=zi, w=cS.v):
                sync.wait_ge(sS, w)
                for c in range(NCHUNK):
                    sync.dma_start(
                        out=sz_st[c, zi],
                        in_=szh_bf[:, (zi % 2) * TOK + c * 128:
                                   (zi % 2) * TOK + c * 128 + 128]
                    ).then_inc(sZp[zi % 2], 16)
            cZp[zi % 2].add(16 * NCHUNK)
            ev_sz_dma[zi] = cZp[zi % 2].v
        else:
            ci = xbc_ci(cc)

            @block.vector
            def _(vector, i=i, q=q, ci=ci):
                vector.wait_ge(sP, evmm[i])
                if i >= 1 and evsilu[i - 1] is not None:
                    vector.wait_ge(sS, evsilu[i - 1])
                ca = conv_acc[:, 0:TOK]
                vector.scalar_tensor_tensor(
                    out=ca, in0=pj[q][:, 0:1024],
                    scalar=convw_sb[:, ci * 4:ci * 4 + 1], in1=ca,
                    op0=ALU.mult, op1=ALU.bypass)
                for j in (1, 2, 3):
                    vector.drain()
                    vector.scalar_tensor_tensor(
                        out=ca, in0=pj[q][:, j:1024 + j],
                        scalar=convw_sb[:, ci * 4 + j:ci * 4 + j + 1], in1=ca,
                        op0=ALU.mult, op1=ALU.add)
                vector.drain()
                vector.drain()
                vector.nop().then_inc(sV, 1)
            cV.add(1)
            evfree[i] = (sV, cV.v)

            @block.scalar
            def _(scalar, i=i, cc=cc, w=cV.v):
                scalar.wait_ge(sV, w)
                if cc == CC_C:
                    scalar.wait_ge(sV, btm_ev[NCHUNK - 1])
                if 64 <= cc < 128 and (cc - 64) >= 2:
                    scalar.wait_ge(sP, ev_xtm_T[(cc - 64 - 2, NCHUNK - 1)])
                dst = (xch_f[:, (i % 2) * TOK:(i % 2 + 1) * TOK] if 64 <= cc < 128
                       else bc_f[:])
                scalar.activation(out=dst, in_=conv_acc[:, 0:TOK],
                                  func=AF.Silu).then_inc(sS, 1)
            cS.add(1)
            evsilu[i] = cS.v

            if cc in (CC_B, CC_C):
                @block.vector
                def _(vector, cc=cc, w=cS.v):
                    vector.wait_ge(sS, w)
                    vector.tensor_copy(out=(b_fm[:] if cc == CC_B else c_fm[:]),
                                       in_=bc_f[:]).then_inc(sV, 1)
                cV.add(1)
                evsilu[i] = (cS.v, cV.v)[1] if False else evsilu[i]
                ev_bcf = cV.v

        if is_dt and dbg_mode >= 1:
            ev_smalls_done, ev_dsl = _dt_math(
                block, locals(), sV, cV, sS, cS, sP, cP,
                upto=(dbg_mode - 10 if 10 < dbg_mode < 20 else 99))
        elif is_dt:
            ev_smalls_done, ev_dsl = (cV.v, cV.v)

    tap("dt", dt_sb[:, :])
    tap("ll", ll_sb[:, :])
    tap("cum", cumtm[:, :])
    tap("wtm", wtm[:, :])
    tap("b_fm", b_fm[:, :])
    tap("c_fm", c_fm[:, :])
    tap("exch", exch[:, :])

    if phase_limit < 2:
        return _finish(nc, block, es, taps, sV, cV, sG, cG, [], out_t,
                       extra_ctx=[pjs, phA_ctx, lnxt_ctx, sctx], all_sems=all_dma_sems)

    @block.gpsimd
    def _(gpsimd):
        gpsimd.wait_ge(sC, 2)
        gpsimd.dma_start(out=dg_sb[:], in_=ccd_out.ap()).then_inc(sG, 16)
    cG.add(16)
    ev_dg = cG.v

    @block.vector
    def _(vector):
        vector.wait_ge(sG, ev_dg)
        vector.memset(s_all[:], 0.0)
        vector.memset(coef_sb[:], 1.0)
        vector.drain()
        vector.drain()
        vector.nop().then_inc(sV, 1)
    cV.add(1)

    @block.tensor
    def _(tensor, w=cV.v):
        tensor.wait_ge(sV, w)
        tensor.transpose(pt[0:NH, 0:8], dg_sb[:], identf[0:8, 0:8]).then_inc(sP, 1)
    cP.add(1)

    @block.vector
    def _(vector, w=cP.v):
        vector.wait_ge(sP, w)
        vector.tensor_copy(out=dgT_sb[:], in_=pt[0:NH, 0:8]).then_inc(sV, 1)
    cV.add(1)

    for k in (1, 2, 5, 6):
        @block.vector
        def _(vector, k=k, w=cV.v):
            vector.wait_ge(sV, w)
            vector.wait_ge(sP, cP.v)
            vector.drain()
            vector.tensor_scalar(out=sm_a[0:NH, 0:8], in0=sm_a[0:NH, 0:8],
                                 scalar1=0.0, scalar2=dgT_sb[:, k:k + 1],
                                 op0=ALU.mult, op1=ALU.add)
            vector.drain()
            vector.drain()
            vector.nop().then_inc(sV, 1)
        cV.add(1)

        @block.tensor
        def _(tensor, k=k, w=cV.v):
            tensor.wait_ge(sV, w)
            tensor.transpose(pt[0:8, 64:128], sm_a[0:NH, 0:8],
                             identf[0:NH, 0:NH]).then_inc(sP, 1)
        cP.add(1)

        @block.vector
        def _(vector, k=k, w=cP.v):
            vector.wait_ge(sP, w)
            vector.tensor_scalar(out=sm_b[0:8, 0:NH], in0=pt[0:8, 64:128],
                                 scalar1=cmask_sb[:, k:k + 1],
                                 scalar2=cmask_sb[:, 8 + k:9 + k],
                                 op0=ALU.mult, op1=ALU.add)
            vector.drain()
            vector.tensor_mul(coef_sb[:], coef_sb[:], sm_b[0:8, 0:NH])
            vector.drain()
            vector.drain()
            vector.nop().then_inc(sV, 1)
        cV.add(1)

    @block.vector
    def _(vector):
        vector.tensor_scalar(out=coef_sb[:], in0=coef_sb[:],
                             scalar1=cmask_sb[:, 15:16], scalar2=None,
                             op0=ALU.mult)
        vector.drain()
        vector.drain()
        vector.nop().then_inc(sV, 1)
    cV.add(1)
    ev_coef = cV.v

    @block.tensor
    def _(tensor, w=cV.v):
        tensor.wait_ge(sV, w)
        tensor.transpose(pt[0:NH, 8:16], coef_sb[:], identf[0:8, 0:8]).then_inc(sP, 1)
    cP.add(1)

    @block.vector
    def _(vector, w=cP.v):
        vector.wait_ge(sP, w)
        vector.tensor_copy(out=coefT_sb[:], in_=pt[0:NH, 8:16]).then_inc(sV, 1)
    cV.add(1)

    for ji, j in enumerate(SLOTS):
        @block.vector
        def _(vector, j=j, w=cV.v):
            vector.wait_ge(sV, w)
            vector.wait_ge(sP, cP.v)
            vector.drain()
            vector.tensor_scalar(out=sm_a[0:NH, 0:128], in0=sm_a[0:NH, 0:128],
                                 scalar1=0.0, scalar2=coefT_sb[:, j:j + 1],
                                 op0=ALU.mult, op1=ALU.add)
            vector.drain()
            vector.drain()
            vector.nop().then_inc(sV, 1)
        cV.add(1)

        @block.tensor
        def _(tensor, j=j, w=cV.v):
            tensor.wait_ge(sV, w)
            tensor.transpose(ph[0:128, 192:256], sm_a[0:NH, 0:128],
                             identf[0:NH, 0:NH]).then_inc(sP, 1)
        cP.add(1)

        @block.vector
        def _(vector, j=j, w=cP.v):
            vector.wait_ge(sP, w)
            vector.tensor_copy(out=coefbc[:, j * NH:(j + 1) * NH],
                               in_=ph[0:128, 192:256]).then_inc(sV, 1)
        cV.add(1)
    ev_coefbc = cV.v

    for ji, j in enumerate(SLOTS):
        for qtr in range(4):
            @block.gpsimd
            def _(gpsimd, j=j, ji=ji, qtr=qtr):
                if ji > 0 or qtr > 0:
                    gpsimd.wait_ge(sV, cV.v)
                gpsimd.dma_start(
                    out=slot_sb[:],
                    in_=cc_out.ap()[j * 128:(j + 1) * 128,
                                    qtr * (DSSM // 4):(qtr + 1) * (DSSM // 4)]
                ).then_inc(sG, 16)
            cG.add(16)

            @block.vector
            def _(vector, j=j, qtr=qtr, w=cG.v):
                vector.wait_ge(sG, w)
                for hh in range(NH // 4):
                    hg = qtr * (NH // 4) + hh
                    vector.scalar_tensor_tensor(
                        out=s_all[:, hg * 128:(hg + 1) * 128],
                        in0=slot_sb[:, hh * 128:(hh + 1) * 128],
                        scalar=coefbc[:, j * NH + hg:j * NH + hg + 1],
                        in1=s_all[:, hg * 128:(hg + 1) * 128],
                        op0=ALU.mult, op1=ALU.add)
                vector.drain()
                vector.nop().then_inc(sV, 1)
            cV.add(1)
    ev_hin = cV.v
    tap("h_in", s_all[:, :])

    if phase_limit < 3:
        return _finish(nc, block, es, taps, sV, cV, sG, cG, [], out_t,
                       extra_ctx=[pjs, phA_ctx, lnxt_ctx, sctx], all_sems=all_dma_sems)

    pjs.close()
    phA_ctx.close()
    lnxt_ctx.close()
    yctx = ExitStack()

    def ysb(name, shape, dtype):
        return yctx.enter_context(nc.sbuf_tensor(name, shape, dtype))

    xtmc = ysb("xtmc", [128, 2 * DSSM], BF16)
    szc = ysb("szc", [128, 2 * DSSM], BF16)
    yg_all = ysb("yg_all", [128, DSSM], BF16)
    e_sb = ysb("e_sb", [128, 2 * 128], BF16)
    m1_sb = ysb("m1_sb", [128, 2 * 128], BF16)
    identd = ysb("identd", [128, NH * 128], BF16)
    mv_sb = ysb("mv_sb", [128, 2 * 128], BF16)
    sbf_sb = ysb("sbf_sb", [128, 2 * 128], BF16)
    xw2_sb = ysb("xw2_sb", [128, 2 * 128], BF16)
    yg2_sb = ysb("yg2_sb", [128, 2 * 128], BF16)
    ynt_sb = ysb("ynt_sb", [128, 2 * 128], BF16)
    rms_sb = ysb("rms_sb", [1, 128], F32)
    lbt_sb = ysb("lbt_sb", [128, 2 * 128], F32)
    cbt_sb = ysb("cbt_sb", [128, 2 * 128], F32)
    rmsb_sb = ysb("rmsb_sb", [128, 128], F32)

    pyc = ExitStack()
    pA = [pyc.enter_context(nc.psum_tensor(f"pA{i}", [128, 512], F32)) for i in range(2)]
    pY = [pyc.enter_context(nc.psum_tensor(f"pY{i}", [128, 512], F32)) for i in range(2)]
    pss = pyc.enter_context(nc.psum_tensor("pss", [128, 512], F32))

    v0, s0, p0 = cV.v + 1, cS.v, cP.v
    n0 = [cNp[0].v, cNp[1].v]
    y0 = [cYp[0].v, cYp[1].v]
    tot_sT = [c_.v for c_ in cTp]
    peA_ev, peB_ev, peC_ev = {}, {}, {}
    v0t_ev, vA_ev, vB_ev, scA_ev, scB_ev = {}, {}, {}, {}, {}
    g1_ev, g2_ev = {}, {}
    ev_cdma, ev_rms_sc, ev_recip, ev_rmsbmm, ev_rmsb = {}, {}, {}, {}, {}
    ev_yn_stt, ev_yn_dma = {}, {}

    pe_seq = []
    for c in range(NCHUNK):
        pe_seq += [("A", c * 64), ("A", c * 64 + 1)]
        for hh in range(NH):
            t = c * 64 + hh
            pe_seq.append(("B", t))
            if hh < 62:
                pe_seq.append(("A", t + 2))
            pe_seq.append(("C", t))
        pe_seq.append(("R", c))
    pv = p0
    for kind, t in pe_seq:
        pv += 1
        if kind == "A":
            peA_ev[t] = pv
        elif kind == "B":
            peB_ev[t] = pv
        elif kind == "C":
            peC_ev[t] = pv
        else:
            ev_rmsbmm[t] = pv
    for c in range(NCHUNK):
        base = v0 + c * 258
        v0t_ev[c * 64 + 0] = base + 1
        v0t_ev[c * 64 + 1] = base + 2
        for hh in range(NH):
            t = c * 64 + hh
            if hh < 62:
                v0t_ev[t + 2] = base + 2 + hh * 3 + 1
                vA_ev[t] = base + 2 + hh * 3 + 2
                vB_ev[t] = base + 2 + hh * 3 + 3
            else:
                vA_ev[t] = base + 2 + 62 * 3 + (hh - 62) * 2 + 1
                vB_ev[t] = base + 2 + 62 * 3 + (hh - 62) * 2 + 2
            scA_ev[t] = s0 + c * 129 + hh * 2 + 1
            scB_ev[t] = s0 + c * 129 + hh * 2 + 2
            ev_yn_stt[(c, hh)] = base + 194 + hh + 1
            ev_yn_dma[(c, hh)] = n0[hh % 2] + (c * 32 + hh // 2 + 1) * 16
        ev_cdma[c] = y0[c % 2] + (c // 2 + 1) * 32
        ev_rms_sc[c] = s0 + c * 129 + 129
        ev_recip[c] = base + 193
        ev_rmsb[c] = base + 194

    @block.vector
    def _(vector):
        for hh in range(NH):
            vector.tensor_scalar(out=identd[:, hh * 128:(hh + 1) * 128],
                                 in0=identb[:], scalar1=dparc[:, hh:hh + 1],
                                 scalar2=None, op0=ALU.mult)
        vector.drain()
        vector.nop().then_inc(sV, 1)
    cV.add(1)
    assert cV.v == v0

    for c in range(NCHUNK):
        qc = c % 2

        @block.sync
        def _(sync, c=c, qc=qc):
            if c == 0:
                sync.wait_ge(sV, ev_hin)
                sync.wait_ge(sZp[0], ev_sz_dma[62])
                sync.wait_ge(sZp[1], ev_sz_dma[63])
                for ii in range(4):
                    sync.wait_ge(sTp[ii], tot_sT[ii])
            if c >= 2:
                sync.wait_ge(sV, vB_ev[(c - 2) * 64 + 63])
            sync.dma_start(
                out=xtmc[:, qc * DSSM:(qc + 1) * DSSM],
                in_=xtm_st[c].rearrange("h p t -> p h t")).then_inc(sYp[qc], 16)
            sync.dma_start(
                out=szc[:, qc * DSSM:(qc + 1) * DSSM],
                in_=sz_st[c].rearrange("h p t -> p h t")).then_inc(sYp[qc], 16)
        cYp[qc].add(32)
        assert cYp[qc].v == ev_cdma[c]

        @block.tensor
        def _(tensor, c=c, qc=qc):
            def emit_A(t):
                hh = t - c * 64
                q = t % 2
                if hh == 0:
                    tensor.wait_ge(sYp[qc], ev_cdma[c])
                    if c >= 1:
                        tensor.wait_ge(sV, ev_rmsb[c - 1])
                tensor.wait_ge(sV, v0t_ev[t])
                if t >= 2:
                    tensor.wait_ge(sV, vB_ev[t - 2])
                    tensor.wait_ge(sS, scA_ev[t - 2])
                tensor.matmul(pA[q][:, 0:128], b_fm[:, c * 128:c * 128 + 128],
                              c_fm[:, c * 128:c * 128 + 128], start=True, stop=True)
                tensor.transpose(pA[q][:, 128:256],
                                 lbt_sb[:, q * 128:q * 128 + 128], identf[:])
                tensor.transpose(pA[q][:, 256:384],
                                 cbt_sb[:, q * 128:q * 128 + 128],
                                 identf[:]).then_inc(sP, 1)
                assert cP.add(1) == peA_ev[t]

            def emit_B(t):
                hh = t - c * 64
                q = t % 2
                tensor.wait_ge(sV, vA_ev[t])
                tensor.matmul(pY[q][:, 0:128],
                              xtmc[:, qc * DSSM + hh * 128:qc * DSSM + hh * 128 + 128],
                              m1_sb[:, q * 128:q * 128 + 128], start=True, stop=False)
                tensor.matmul(pY[q][:, 0:128],
                              xtmc[:, qc * DSSM + hh * 128:qc * DSSM + hh * 128 + 128],
                              identd[:, hh * 128:(hh + 1) * 128], start=False, stop=False)
                tensor.matmul(pY[q][:, 0:128], sbf_sb[:, q * 128:q * 128 + 128],
                              mv_sb[:, q * 128:q * 128 + 128], start=False, stop=True)
                tensor.matmul(pA[q][:, 384:512], b_tm[:, c * 128:(c + 1) * 128],
                              xw2_sb[:, q * 128:q * 128 + 128],
                              start=True, stop=True).then_inc(sP, 1)
                assert cP.add(1) == peB_ev[t]

            def emit_C(t):
                hh = t - c * 64
                q = t % 2
                tensor.wait_ge(sS, scB_ev[t])
                tensor.matmul(pss[0:1, 0:128], onescol_bf[:],
                              yg2_sb[:, q * 128:q * 128 + 128],
                              start=(hh == 0), stop=(hh == 63)).then_inc(sP, 1)
                assert cP.add(1) == peC_ev[t]

            emit_A(c * 64)
            emit_A(c * 64 + 1)
            for hh in range(NH):
                t = c * 64 + hh
                emit_B(t)
                if hh < 62:
                    emit_A(t + 2)
                emit_C(t)
            tensor.wait_ge(sV, ev_recip[c])
            tensor.matmul(pss[:, 128:256], onesrow[:], rms_sb[0:1, :],
                          start=True, stop=True).then_inc(sP, 1)
            assert cP.add(1) == ev_rmsbmm[c]

        @block.scalar
        def _(scalar, c=c):
            for hh in range(NH):
                t = c * 64 + hh
                q = t % 2
                scalar.wait_ge(sP, peA_ev[t])
                scalar.activation(out=e_sb[:, q * 128:q * 128 + 128],
                                  in_=pA[q][:, 128:256], func=AF.Exp,
                                  bias=lndtm[:, c * 64 + hh:c * 64 + hh + 1]
                                  ).then_inc(sS, 1)
                assert cS.add(1) == scA_ev[t]
                scalar.wait_ge(sV, vB_ev[t])
                if t >= 2:
                    scalar.wait_ge(sP, peC_ev[t - 2])
                scalar.activation(out=yg2_sb[:, q * 128:q * 128 + 128],
                                  in_=yg_all[:, hh * 128:hh * 128 + 128],
                                  func=AF.Square).then_inc(sS, 1)
                assert cS.add(1) == scB_ev[t]
            scalar.wait_ge(sP, peC_ev[c * 64 + 63])
            scalar.activation(out=rms_sb[:], in_=pss[0:1, 0:128], func=AF.Sqrt,
                              scale=1.0 / DSSM, bias=cst[0:1, 0:1]).then_inc(sS, 1)
            assert cS.add(1) == ev_rms_sc[c]

        @block.vector
        def _(vector, c=c, qc=qc):
            def emit_v0(t):
                hh = t - c * 64
                q = t % 2
                if t >= 2:
                    vector.wait_ge(sP, peA_ev[t - 2])
                vector.tensor_scalar(out=lbt_sb[:, q * 128:q * 128 + 128],
                                     in0=maskT_sb[:],
                                     scalar1=ltm[:, c * 64 + hh:c * 64 + hh + 1],
                                     scalar2=None, op0=ALU.add)
                vector.tensor_scalar(out=cbt_sb[:, q * 128:q * 128 + 128],
                                     in0=maskT_sb[:], scalar1=0.0,
                                     scalar2=cumtm[:, c * 64 + hh:c * 64 + hh + 1],
                                     op0=ALU.mult, op1=ALU.add).then_inc(sV, 1)
                assert cV.add(1) == v0t_ev[t]

            emit_v0(c * 64)
            emit_v0(c * 64 + 1)
            for hh in range(NH):
                t = c * 64 + hh
                q = t % 2
                if hh < 62:
                    emit_v0(t + 2)
                vector.wait_ge(sS, scA_ev[t])
                if t >= 64:
                    vector.wait_ge(sV, vB_ev[t - 64])
                vector.tensor_copy(out=sbf_sb[:, q * 128:q * 128 + 128],
                                   in_=s_all[:, hh * 128:hh * 128 + 128])
                vector.tensor_scalar(
                    out=xw2_sb[:, q * 128:q * 128 + 128],
                    in0=xtmc[:, qc * DSSM + hh * 128:qc * DSSM + hh * 128 + 128],
                    scalar1=wctm[:, c * 64 + hh:c * 64 + hh + 1],
                    scalar2=None, op0=ALU.mult)
                vector.tensor_tensor(out=m1_sb[:, q * 128:q * 128 + 128],
                                     in0=e_sb[:, q * 128:q * 128 + 128],
                                     in1=pA[q][:, 0:128], op=ALU.mult)
                vector.tensor_tensor(out=mv_sb[:, q * 128:q * 128 + 128],
                                     in0=c_fm[:, c * 128:c * 128 + 128],
                                     in1=pA[q][:, 256:384], op=ALU.mult).then_inc(sV, 1)
                assert cV.add(1) == vA_ev[t]
                vector.wait_ge(sP, peB_ev[t])
                vector.scalar_tensor_tensor(
                    out=s_all[:, hh * 128:hh * 128 + 128],
                    in0=s_all[:, hh * 128:hh * 128 + 128],
                    scalar=dbc[:, c * 64 + hh:c * 64 + hh + 1],
                    in1=pA[q][:, 384:512], op0=ALU.mult, op1=ALU.add)
                vector.tensor_tensor(out=yg_all[:, hh * 128:hh * 128 + 128],
                                     in0=pY[q][:, 0:128],
                                     in1=szc[:, qc * DSSM + hh * 128:qc * DSSM + hh * 128 + 128],
                                     op=ALU.mult).then_inc(sV, 1)
                assert cV.add(1) == vB_ev[t]

        @block.vector
        def _(vector, c=c):
            vector.wait_ge(sS, ev_rms_sc[c])
            vector.reciprocal(out=rms_sb[:], in_=rms_sb[:]).then_inc(sV, 1)
            assert cV.add(1) == ev_recip[c]

        @block.vector
        def _(vector, c=c):
            vector.wait_ge(sP, ev_rmsbmm[c])
            vector.tensor_copy(out=rmsb_sb[:], in_=pss[:, 128:256]).then_inc(sV, 1)
            assert cV.add(1) == ev_rmsb[c]

        @block.vector
        def _(vector, c=c):
            for hh in range(NH):
                if hh == 0:
                    vector.wait_ge(sV, ev_rmsb[c])
                if hh >= 2:
                    vector.wait_ge(sNp[hh % 2], ev_yn_dma[(c, hh - 2)])
                elif c >= 1:
                    vector.wait_ge(sNp[hh % 2], ev_yn_dma[(c - 1, 62 + hh)])
                vector.scalar_tensor_tensor(
                    out=ynt_sb[:, (hh % 2) * 128:(hh % 2) * 128 + 128],
                    in0=yg_all[:, hh * 128:hh * 128 + 128],
                    scalar=gnormc[:, hh:hh + 1], in1=rmsb_sb[:],
                    op0=ALU.mult, op1=ALU.mult).then_inc(sV, 1)
                assert cV.add(1) == ev_yn_stt[(c, hh)]

        @block.scalar
        def _(scalar, c=c):
            for hh in range(NH):
                scalar.wait_ge(sV, ev_yn_stt[(c, hh)])
                scalar.dma_start(out=yn_st[hh][:, c * 128:c * 128 + 128],
                                 in_=ynt_sb[:, (hh % 2) * 128:(hh % 2) * 128 + 128]
                                 ).then_inc(sNp[hh % 2], 16)
                assert cNp[hh % 2].add(16) == ev_yn_dma[(c, hh)]

    tap("yg7", yg_all[:, :])
    tap("s_end", s_all[:, :])

    if phase_limit < 4:
        return _finish(nc, block, es, taps, sV, cV, sG, cG, [], out_t,
                       extra_ctx=[pyc, yctx, sctx], all_sems=all_dma_sems)

    yctx.close()
    pyc.close()
    sctx.close()
    msctx.close()
    wctx = ExitStack()

    def wsb(name, shape, dtype):
        return wctx.enter_context(nc.sbuf_tensor(name, shape, dtype))

    ynp = wsb("ynp", [128, NH * 512], BF16)
    wobf = wsb("wobf", [128, 2 * 4096], BF16)
    attn4 = wsb("attn4", [128, 2 * 2048], F32)
    xres_sb = wsb("xres_sb", [128, 2 * 128], F32)
    x1t_sb = wsb("x1t_sb", [128, 2 * 128], F32)

    pwc = ExitStack()
    pw = [pwc.enter_context(nc.psum_tensor(f"pw{i}", [128, 512], F32)) for i in range(4)]
    pz2 = [pwc.enter_context(nc.psum_tensor(f"pz{i}", [128, 512], F32)) for i in range(2)]

    ev_wold = {}
    ev_wocast = {}
    ev_att_ev = {}
    ev_x1dma = {}
    ev_pw_free = {}
    ev_unitsT = {}
    last_yn0 = cNp[0].v
    last_yn1 = cNp[1].v

    wslots = [(pt, 0), (pt, 128), (ph, 0), (ph, 128)]
    for pzb in pz2:
        for j in range(4):
            wslots.append((pzb, j * 128))
    NSLOT = len(wslots)
    add_evs = []

    def emit_wout_tail(n):
        tgn, ngn = n // 8, n % 8
        parn = n % 2
        for ncx in range(4):
            for tb in range(4):
                ti = ncx * 4 + tb
                u = n * 16 + ti
                qt = ti % 2
                pb, off = wslots[u % NSLOT]

                @block.tensor
                def _(tensor, n=n, ncx=ncx, tb=tb, ti=ti, u=u, pb=pb, off=off,
                      parn=parn):
                    if ti == 0:
                        tensor.wait_ge(sS, ev_pw_free[n])
                    if u >= NSLOT:
                        tensor.wait_ge(sV, add_evs[u - NSLOT])
                    tensor.transpose(pb[:, off:off + 128],
                                     attn4[:, parn * 2048 + ncx * 512 + tb * 128:
                                           parn * 2048 + ncx * 512 + tb * 128 + 128],
                                     identf[:]).then_inc(sP, 1)
                cP.add(1)
                ev_tp = cP.v

                @block.scalar
                def _(scalar, tgn=tgn, ngn=ngn, ncx=ncx, tb=tb, qt=qt, wv=cV.v):
                    scalar.wait_ge(sV, wv)
                    r0 = 3 + tgn * 512 + tb * 128
                    c0 = ngn * 512 + ncx * 128
                    scalar.dma_start(out=xres_sb[:, qt * 128:qt * 128 + 128],
                                     in_=xs[r0:r0 + 128, c0:c0 + 128]
                                     ).then_inc(sXp[qt], 16)
                cXp[qt].add(16)
                ev_xr = (sXp[qt], cXp[qt].v)

                @block.vector
                def _(vector, qt=qt, w1=ev_tp, w2=ev_xr, pb=pb, off=off,
                      w3=ev_x1dma.get(len(ev_x1dma) - 2)):
                    vector.wait_ge(sP, w1)
                    vector.wait_ge(w2[0], w2[1])
                    if w3 is not None:
                        vector.wait_ge(w3[0], w3[1])
                    vector.tensor_add(x1t_sb[:, qt * 128:qt * 128 + 128],
                                      pb[:, off:off + 128],
                                      xres_sb[:, qt * 128:qt * 128 + 128])
                    vector.drain()
                    vector.nop().then_inc(sV, 1)
                cV.add(1)
                add_evs.append(cV.v)

                @block.sync
                def _(sync, tgn=tgn, ngn=ngn, ncx=ncx, tb=tb, qt=qt, w=cV.v):
                    sync.wait_ge(sV, w)
                    r0 = tgn * 512 + tb * 128
                    c0 = ngn * 512 + ncx * 128
                    sync.dma_start(out=x1_st[r0:r0 + 128, c0:c0 + 128],
                                   in_=x1t_sb[:, qt * 128:qt * 128 + 128]
                                   ).then_inc(sTp[qt], 16)
                cTp[qt].add(16)
                ev_x1dma[len(ev_x1dma)] = (sTp[qt], cTp[qt].v)
        ev_unitsT[n] = cP.v

    for tg in range(2):
        @block.sync
        def _(sync, tg=tg):
            if tg == 0:
                sync.wait_ge(sNp[0], last_yn0)
                sync.wait_ge(sNp[1], last_yn1)
            else:
                sync.wait_ge(sS, ev_pw_free[7])
            for hh in range(NH):
                sync.dma_start(out=ynp[:, hh * 512:(hh + 1) * 512],
                               in_=yn_st[hh][:, tg * 512:(tg + 1) * 512]
                               ).then_inc(sYp[tg], 16)
        cYp[tg].add(16 * NH)
        ev_ynp = (sYp[tg], cYp[tg].v)

        for ng in range(8):
            for kg in range(8):
                si = (tg * 8 + ng) * 8 + kg
                qw = si % 2

                @block.gpsimd
                def _(gpsimd, tg=tg, ng=ng, kg=kg, si=si, qw=qw):
                    if si < 2:
                        gpsimd.wait_ge(sNp[0], last_yn0)
                        gpsimd.wait_ge(sNp[1], last_yn1)
                    if si >= 2:
                        gpsimd.wait_ge(sP, _later(ev_att_ev, si - 2, cP))
                    gpsimd.dma_start(
                        out=wobf[:, qw * 4096:(qw + 1) * 4096],
                        in_=wo[ng, kg * 8:(kg + 1) * 8].rearrange(
                            "k p n -> p k n")).then_inc(sWp[qw], 16)
                cWp[qw].add(16)
                ev_wold[si] = (sWp[qw], cWp[qw].v)

                @block.tensor
                def _(tensor, tg=tg, ng=ng, kg=kg, si=si, qw=qw, ev_ynp=ev_ynp):
                    tensor.wait_ge(ev_wold[si][0], ev_wold[si][1])
                    if kg == 0:
                        tensor.wait_ge(ev_ynp[0], ev_ynp[1])
                        if not (tg == 0 and ng == 0):
                            tensor.wait_ge(sS, ev_pw_free[tg * 8 + ng - 1])
                    last = None
                    for kk8 in range(8):
                        kc = kg * 8 + kk8
                        for ncx in range(4):
                            last = tensor.matmul(
                                pw[ncx][:, 0:512],
                                wobf[:, qw * 4096 + kk8 * 512 + ncx * 128:
                                     qw * 4096 + kk8 * 512 + ncx * 128 + 128],
                                ynp[:, kc * 512:(kc + 1) * 512],
                                start=(kc == 0), stop=(kc == 63))
                    last.then_inc(sP, 1)
                cP.add(1)
                ev_att_ev[si] = cP.v

            bi = tg * 8 + ng
            par = bi % 2

            @block.scalar
            def _(scalar, bi=bi, par=par, wmm=ev_att_ev[(tg * 8 + ng) * 8 + 7]):
                scalar.wait_ge(sP, wmm)
                if bi >= 2:
                    scalar.wait_ge(sP, ev_unitsT[bi - 2])
                for ncx in range(4):
                    scalar.activation(out=attn4[:, par * 2048 + ncx * 512:
                                                par * 2048 + ncx * 512 + 512],
                                      in_=pw[ncx][:, 0:512], func=AF.Identity)
                scalar.drain()
                scalar.nop().then_inc(sS, 1)
            cS.add(1)
            ev_pw_free[bi] = cS.v

            if bi >= 1:
                emit_wout_tail(bi - 1)

    emit_wout_tail(15)

    ev_x1_done = [(sTp[0], cTp[0].v), (sTp[1], cTp[1].v)]
    tap("x1st", x1_st[:, :])

    if phase_limit < 5:
        return _finish(nc, block, es, taps, sV, cV, sG, cG, [], out_t,
                       extra_ctx=[pwc, wctx], all_sems=all_dma_sems)

    wctx.close()
    pwc.close()
    fctx = ExitStack()

    def fsb(name, shape, dtype):
        return fctx.enter_context(nc.sbuf_tensor(name, shape, dtype))

    qxT = fsb("qxT", [128, 32 * 1024], BF16)
    x1row2 = fsb("x1row2", [128, 2 * 4096], F32)
    qxr = fsb("qxr", [128, 2 * 4096], F32)
    qs_f = fsb("qs_f", [128, 4096], F32)
    qi_i = fsb("qi_i", [128, 4096], I32)
    q256a = fsb("q256a", [128, 256], F32)
    q256b = fsb("q256b", [128, 256], F32)
    q256c = fsb("q256c", [128, 256], F32)
    q256d = fsb("q256d", [128, 256], F32)
    qwT_sb = fsb("qwT_sb", [128, 32 * 128], BF16)
    ffn_f = fsb("ffn_f", [128, 1024], F32)
    ost_sb = fsb("ost_sb", [128, 2 * 128], F32)
    x1r_sb = fsb("x1r_sb", [128, 2 * 128], F32)
    lnq_small = fsb("lnq_small", [128, 16], F32)

    pfc = ExitStack()
    pf = [pfc.enter_context(nc.psum_tensor(f"pf{i}", [128, 512], F32)) for i in range(2)]

    ev_q_src_done = {}
    ev_qr_done = {}
    ev_bfp_done = {}
    ev_ld = {}
    ev_outdma = {}
    ev_fev = {}
    ev_fmm = {}

    for tc in range(NCHUNK):
        i = tc
        qp = i % 2

        @block.scalar
        def _(scalar, tc=tc, i=i, qp=qp):
            if tc == 0:
                for s_, v_ in ev_x1_done:
                    scalar.wait_ge(s_, v_)
            if i >= 2:
                scalar.wait_ge(sV, ev_q_src_done[i - 2])
            scalar.dma_start(out=x1row2[:, qp * 4096:(qp + 1) * 4096],
                             in_=x1_st[tc * 128:(tc + 1) * 128, :]).then_inc(sXp[qp], 16)
        cXp[qp].add(16)
        ev_ld[i] = (sXp[qp], cXp[qp].v)

        @block.scalar
        def _(scalar, i=i, qp=qp, w=ev_ld[i]):
            scalar.wait_ge(w[0], w[1])
            if i >= 2:
                scalar.wait_ge(sP, ev_qr_done[i - 2])
            scalar.activation(out=qxr[:, qp * 4096:(qp + 1) * 4096],
                              in_=x1row2[:, qp * 4096:(qp + 1) * 4096],
                              func=AF.Square).then_inc(sS, 1)
        cS.add(1)

        @block.vector
        def _(vector, i=i, qp=qp, w=ev_ld[i], w2=cS.v):
            vector.wait_ge(w[0], w[1])
            vector.wait_ge(sS, w2)
            if i >= 2:
                vector.wait_ge(sP, ev_qr_done[i - 2])
            xt = x1row2[:, qp * 4096:(qp + 1) * 4096]
            vector.reduce_sum(lnq_small[:, 1:2], qxr[:, qp * 4096:(qp + 1) * 4096],
                              axis=AX)
            vector.reduce_sum(lnq_small[:, 0:1], xt, axis=AX)
            vector.drain()
            vector.tensor_scalar(out=lnq_small[:, 1:2], in0=lnq_small[:, 1:2],
                                 scalar1=1.0 / H, scalar2=None, op0=ALU.mult)
            vector.tensor_scalar(out=lnq_small[:, 0:1], in0=lnq_small[:, 0:1],
                                 scalar1=1.0 / H, scalar2=None, op0=ALU.mult)
            vector.drain()
            vector.tensor_tensor(out=lnq_small[:, 2:3], in0=lnq_small[:, 0:1],
                                 in1=lnq_small[:, 0:1], op=ALU.mult)
            vector.tensor_scalar(out=lnq_small[:, 5:6], in0=lnq_small[:, 0:1],
                                 scalar1=-1.0, scalar2=None, op0=ALU.mult)
            vector.drain()
            vector.tensor_sub(lnq_small[:, 2:3], lnq_small[:, 1:2], lnq_small[:, 2:3])
            vector.drain()
            vector.nop().then_inc(sV, 1)
        cV.add(1)

        @block.scalar
        def _(scalar, w=cV.v):
            scalar.wait_ge(sV, w)
            scalar.activation(out=lnq_small[:, 3:4], in_=lnq_small[:, 2:3],
                              func=AF.Sqrt, bias=cst[:, 0:1]).then_inc(sS, 1)
        cS.add(1)

        @block.vector
        def _(vector, i=i, qp=qp, w=cS.v):
            vector.wait_ge(sS, w)
            vector.reciprocal(out=lnq_small[:, 4:5], in_=lnq_small[:, 3:4])
            vector.drain()
            vector.tensor_scalar(out=qxr[:, qp * 4096:(qp + 1) * 4096],
                                 in0=x1row2[:, qp * 4096:(qp + 1) * 4096],
                                 scalar1=lnq_small[:, 5:6], scalar2=lnq_small[:, 4:5],
                                 op0=ALU.add, op1=ALU.mult)
            vector.drain()
            vector.nop().then_inc(sV, 1)
        cV.add(1)
        ev_q_src_done[i] = cV.v

        _emit_bfp(block, sV, cV, sS, cS, qxr[:, qp * 4096:(qp + 1) * 4096],
                  qs_f, qi_i, q256a, q256b, q256c, q256d,
                  qxr[:, qp * 4096:(qp + 1) * 4096], INVLN2, LN2C, cst)
        ev_bfp_done[i] = cV.v

        tev = []
        for fc in range(32):
            @block.tensor
            def _(tensor, tc=tc, fc=fc, qp=qp, w=cV.v):
                if fc == 0:
                    tensor.wait_ge(sV, w)
                if fc >= 2:
                    tensor.wait_ge(sV, tev[fc - 2])
                dstp = pt if fc % 2 == 0 else ph
                tensor.transpose(dstp[:, 0:128],
                                 qxr[:, qp * 4096 + fc * 128:qp * 4096 + fc * 128 + 128],
                                 identf[:]).then_inc(sP, 1)
            cP.add(1)

            @block.vector
            def _(vector, tc=tc, fc=fc, w=cP.v):
                vector.wait_ge(sP, w)
                srcp = pt if fc % 2 == 0 else ph
                vector.tensor_copy(out=qxT[:, fc * 1024 + tc * 128:fc * 1024 + tc * 128 + 128],
                                   in_=srcp[:, 0:128]).then_inc(sV, 1)
            cV.add(1)
            tev.append(cV.v)
        ev_qr_done[i] = cP.v
    ev_qx_done = cV.v

    def emit_outstage(ncx):
        for tb in range(8):
            qt = tb % 2

            @block.tensor
            def _(tensor, ncx=ncx, tb=tb, qt=qt, w=ev_fev[ncx]):
                tensor.wait_ge(sS, w)
                if tb >= 2:
                    tensor.wait_ge(sV, cV.v)
                dstp = pt if qt == 0 else ph
                tensor.transpose(dstp[:, 128:256], ffn_f[:, tb * 128:tb * 128 + 128],
                                 identf[:]).then_inc(sP, 1)
            cP.add(1)
            ev_otp = cP.v

            @block.scalar
            def _(scalar, ncx=ncx, tb=tb, qt=qt, wv=cV.v):
                scalar.wait_ge(sV, wv)
                scalar.dma_start(out=x1r_sb[:, qt * 128:qt * 128 + 128],
                                 in_=x1_st[tb * 128:(tb + 1) * 128,
                                           ncx * 128:(ncx + 1) * 128]
                                 ).then_inc(sXp[qt], 16)
            cXp[qt].add(16)
            ev_x1r = (sXp[qt], cXp[qt].v)

            @block.vector
            def _(vector, qt=qt, w1=ev_otp, w2=ev_x1r,
                  w3=ev_outdma.get(len(ev_outdma) - 2)):
                vector.wait_ge(sP, w1)
                vector.wait_ge(w2[0], w2[1])
                if w3 is not None:
                    vector.wait_ge(w3[0], w3[1])
                srcp = pt if qt == 0 else ph
                vector.tensor_add(ost_sb[:, qt * 128:qt * 128 + 128],
                                  srcp[:, 128:256], x1r_sb[:, qt * 128:qt * 128 + 128])
                vector.nop().then_inc(sV, 1)
            cV.add(1)

            @block.sync
            def _(sync, ncx=ncx, tb=tb, qt=qt, w=cV.v):
                sync.wait_ge(sV, w)
                sync.dma_start(out=out_t[tb * 128:(tb + 1) * 128,
                                         ncx * 128:(ncx + 1) * 128],
                               in_=ost_sb[:, qt * 128:qt * 128 + 128]
                               ).then_inc(sOp[qt], 16)
            cOp[qt].add(16)
            ev_outdma[len(ev_outdma)] = (sOp[qt], cOp[qt].v)

    def emit_ffn_tail(n):
        @block.scalar
        def _(scalar, ncx=n, w=ev_fmm[n], wv=cV.v):
            scalar.wait_ge(sP, w)
            if ncx >= 1:
                scalar.wait_ge(sV, wv)
            for tg in range(2):
                scalar.activation(out=ffn_f[:, tg * 512:(tg + 1) * 512],
                                  in_=pf[tg][:, 0:512], func=AF.Identity,
                                  bias=fbiasc[:, ncx:ncx + 1])
            scalar.drain()
            scalar.nop().then_inc(sS, 1)
        cS.add(1)
        ev_fev[n] = cS.v
        emit_outstage(n)

    for ncx in range(32):
        i = 8 + ncx
        qp = i % 2

        @block.scalar
        def _(scalar, ncx=ncx, i=i, qp=qp):
            if i >= 2:
                scalar.wait_ge(sV, ev_q_src_done[i - 2])
            scalar.dma_start(out=x1row2[:, qp * 4096:(qp + 1) * 4096],
                             in_=wf[ncx * 128:(ncx + 1) * 128, :]).then_inc(sXp[qp], 16)
        cXp[qp].add(16)
        ev_ld[i] = (sXp[qp], cXp[qp].v)

        @block.vector
        def _(vector, i=i, qp=qp, w=ev_ld[i]):
            vector.wait_ge(w[0], w[1])
            if i >= 2:
                vector.wait_ge(sP, ev_qr_done[i - 2])
            vector.nop().then_inc(sV, 1)
        cV.add(1)

        _emit_bfp(block, sV, cV, sS, cS, x1row2[:, qp * 4096:(qp + 1) * 4096],
                  qs_f, qi_i, q256a, q256b, q256c, q256d,
                  qxr[:, qp * 4096:(qp + 1) * 4096], INVLN2, LN2C, cst)
        ev_q_src_done[i] = cV.v

        if ncx >= 1:
            emit_ffn_tail(ncx - 1)

        tev = []
        for kk in range(32):
            @block.tensor
            def _(tensor, kk=kk, qp=qp, w=cV.v):
                if kk == 0:
                    tensor.wait_ge(sV, w)
                if kk >= 2:
                    tensor.wait_ge(sV, tev[kk - 2])
                dstp = pt if kk % 2 == 0 else ph
                tensor.transpose(dstp[:, 0:128],
                                 qxr[:, qp * 4096 + kk * 128:qp * 4096 + kk * 128 + 128],
                                 identf[:]).then_inc(sP, 1)
            cP.add(1)

            @block.vector
            def _(vector, kk=kk, w=cP.v):
                vector.wait_ge(sP, w)
                srcp = pt if kk % 2 == 0 else ph
                vector.tensor_copy(out=qwT_sb[:, kk * 128:kk * 128 + 128],
                                   in_=srcp[:, 0:128]).then_inc(sV, 1)
            cV.add(1)
            tev.append(cV.v)
        ev_qr_done[i] = cP.v
        ev_qwT = cV.v

        @block.tensor
        def _(tensor, ncx=ncx, w=ev_qwT):
            tensor.wait_ge(sV, w)
            if ncx >= 1:
                tensor.wait_ge(sS, ev_fev[ncx - 1])
            last = None
            for kk in range(32):
                for tg in range(2):
                    last = tensor.matmul(
                        pf[tg][:, 0:512], qwT_sb[:, kk * 128:kk * 128 + 128],
                        qxT[:, kk * 1024 + tg * 512:kk * 1024 + tg * 512 + 512],
                        start=(kk == 0), stop=(kk == 31))
            last.then_inc(sP, 1)
        cP.add(1)
        ev_fmm[ncx] = cP.v

    emit_ffn_tail(31)

    return _finish(nc, block, es, taps, sV, cV, sG, cG,
                   [(sOp[0], cOp[0].v), (sOp[1], cOp[1].v)], out_t,
                   extra_ctx=[pfc, fctx], all_sems=all_dma_sems)


def _later(d, t, c=None):
    return d.get(t, 0) if isinstance(d, dict) else d[t]


def _dt_math(block, L, sV, cV, sS, cS, sP, cP, upto=99):
    nc_l = L
    dt_raw = L["dt_raw"]; dtb_sb = L["dtb_sb"]; dt_sb = L["dt_sb"]
    lndt_sb = L["lndt_sb"]; alog_sb = L["alog_sb"]; lda_sb = L["lda_sb"]
    ll_sb = L["ll_sb"]; offs_sb = L["offs_sb"]
    ltot_sb = L["ltot_sb"]; wtm = L["wtm"]
    wctm = L["wctm"]; lndtm = L["lndtm"]; dbc = L["dbc"]
    etot_sb = L["etot_sb"]; cumtm = L["cumtm"]; ltm = L["ltm"]
    sm_a = L["sm_a"]; sm_b = L["sm_b"]; dslrow = L["dslrow"]
    identf = L["identf"]; onesrow = L["onesrow"]
    pt = L["pt"]; ph = L["ph"]
    evfree_dt = L["evfree"][0]

    @block.scalar
    def _(scalar):
        scalar.wait_ge(sV, evfree_dt[1])
        scalar.activation(out=dt_sb[:], in_=dt_raw[:], func=AF.Abs, bias=dtb_sb[:])
        scalar.drain()
        scalar.activation(out=dt_sb[:], in_=dt_sb[:], func=AF.Exp, scale=-1.0)
        scalar.drain()
        scalar.activation(out=dt_sb[:], in_=dt_sb[:], func=AF.Ln, bias=1.0)
        scalar.activation(out=lda_sb[:], in_=dt_raw[:], func=AF.Relu, bias=dtb_sb[:])
        scalar.activation(out=sm_a[:, 0:1], in_=alog_sb[:], func=AF.Exp)
        scalar.drain()
        scalar.drain()
        scalar.nop().then_inc(sS, 1)
    cS.add(1)

    if upto < 2:
        return (cV.v, cV.v)

    @block.vector
    def _(vector, w=cS.v):
        vector.wait_ge(sS, w)
        vector.tensor_add(dt_sb[:], dt_sb[:], lda_sb[:])
        vector.tensor_scalar(out=sm_a[:, 1:2], in0=sm_a[:, 0:1], scalar1=-1.0,
                             scalar2=None, op0=ALU.mult)
        vector.memset(offs_sb[:, 15:16], 0.0)
        vector.drain()
        vector.tensor_scalar(out=lda_sb[:], in0=dt_sb[:], scalar1=sm_a[:, 1:2],
                             scalar2=None, op0=ALU.mult)
        vector.drain()
        for c in range(NCHUNK):
            vector.tensor_tensor_scan(
                out=ll_sb[:, c * 128:(c + 1) * 128],
                data0=lda_sb[:, c * 128:(c + 1) * 128],
                data1=offs_sb[:, 15:16].broadcast_to([NH, 128]),
                initial=0.0, op0=ALU.add, op1=ALU.add)
        vector.drain()
        vector.tensor_copy(out=ltot_sb[:], in_=ll_sb[:, 127::128])
        vector.memset(offs_sb[:, 0:1], 0.0)
        vector.drain()
        vector.tensor_tensor_scan(
            out=offs_sb[:, 1:9], data0=ltot_sb[:],
            data1=offs_sb[:, 15:16].broadcast_to([NH, 8]),
            initial=0.0, op0=ALU.add, op1=ALU.add)
        vector.drain()
        vector.tensor_scalar(out=sm_b[:, 0:8],
                             in0=offs_sb[:, 0:8], scalar1=offs_sb[:, 8:9],
                             scalar2=-1.0, op0=ALU.subtract, op1=ALU.mult)
        vector.drain()
        vector.nop().then_inc(sV, 1)
    cV.add(1)

    if upto < 3:
        return (cV.v, cV.v)

    @block.scalar
    def _(scalar, w=cV.v):
        scalar.wait_ge(sV, w)
        scalar.activation(out=lndt_sb[:], in_=dt_sb[:], func=AF.Ln)
        scalar.activation(out=sm_a[:, 2:3], in_=offs_sb[:, 8:9], func=AF.Exp)
        scalar.drain()
        scalar.nop().then_inc(sS, 1)
    cS.add(1)
    ev_base = cS.v

    if upto < 4:
        return (cV.v, cV.v)

    ev_tr = []
    for c in range(NCHUNK):
        @block.scalar
        def _(scalar, c=c, w=ev_base):
            scalar.wait_ge(sS, w)
            if c >= 1:
                scalar.wait_ge(sV, ev_tr[c - 1])
            scalar.activation(out=sm_a[:, 0:128],
                              in_=ll_sb[:, c * 128:(c + 1) * 128], func=AF.Exp,
                              scale=-1.0, bias=sm_b[:, c:c + 1])
            scalar.drain()
            scalar.nop().then_inc(sS, 1)
        cS.add(1)

        @block.vector
        def _(vector, c=c, w=cS.v):
            vector.wait_ge(sS, w)
            vector.tensor_mul(sm_a[:, 0:128], sm_a[:, 0:128],
                              dt_sb[:, c * 128:(c + 1) * 128]).then_inc(sV, 1)
        cV.add(1)

        @block.tensor
        def _(tensor, c=c, w=cV.v):
            tensor.wait_ge(sV, w)
            tensor.transpose(pt[0:128, 0:64], sm_a[0:NH, 0:128],
                             identf[0:NH, 0:NH]).then_inc(sP, 1)
        cP.add(1)

        @block.vector
        def _(vector, c=c, w=cP.v):
            vector.wait_ge(sP, w)
            vector.tensor_copy(out=wtm[:, c * 64:(c + 1) * 64],
                               in_=pt[0:128, 0:64]).then_inc(sV, 1)
        cV.add(1)

        @block.scalar
        def _(scalar, c=c, w=cV.v):
            scalar.wait_ge(sV, w)
            scalar.activation(out=sm_a[:, 0:128],
                              in_=ll_sb[:, c * 128:(c + 1) * 128], func=AF.Exp,
                              scale=-1.0, bias=ltot_sb[:, c:c + 1])
            scalar.drain()
            scalar.nop().then_inc(sS, 1)
        cS.add(1)

        @block.vector
        def _(vector, c=c, w=cS.v):
            vector.wait_ge(sS, w)
            vector.tensor_mul(sm_a[:, 0:128], sm_a[:, 0:128],
                              dt_sb[:, c * 128:(c + 1) * 128])
            vector.tensor_sub(dt_raw[:, 0:128],
                              lndt_sb[:, c * 128:(c + 1) * 128],
                              ll_sb[:, c * 128:(c + 1) * 128])
            vector.drain()
            vector.nop().then_inc(sV, 1)
        cV.add(1)

        @block.tensor
        def _(tensor, c=c, w=cV.v):
            tensor.wait_ge(sV, w)
            tensor.transpose(pt[0:128, 64:128], sm_a[0:NH, 0:128],
                             identf[0:NH, 0:NH])
            tensor.transpose(ph[0:128, 0:64], dt_raw[0:NH, 0:128],
                             identf[0:NH, 0:NH])
            tensor.transpose(ph[0:128, 64:128], ll_sb[0:NH, c * 128:(c + 1) * 128],
                             identf[0:NH, 0:NH]).then_inc(sP, 1)
        cP.add(1)

        @block.vector
        def _(vector, c=c, w=cP.v):
            vector.wait_ge(sP, w)
            vector.tensor_copy(out=wctm[:, c * 64:(c + 1) * 64], in_=pt[0:128, 64:128])
            vector.tensor_copy(out=lndtm[:, c * 64:(c + 1) * 64],
                               in_=ph[0:128, 0:64])
            vector.tensor_copy(out=ltm[:, c * 64:(c + 1) * 64],
                               in_=ph[0:128, 64:128]).then_inc(sV, 1)
        cV.add(1)
        ev_tr.append(cV.v)

        @block.scalar
        def _(scalar, c=c, w=cV.v):
            scalar.wait_ge(sV, w)
            scalar.activation(out=cumtm[:, c * 64:(c + 1) * 64],
                              in_=ltm[:, c * 64:(c + 1) * 64],
                              func=AF.Exp).then_inc(sS, 1)
        cS.add(1)


    if upto < 5:
        return (cV.v, cV.v)

    @block.scalar
    def _(scalar, w=ev_tr[NCHUNK - 1]):
        scalar.wait_ge(sV, w)
        scalar.activation(out=etot_sb[:], in_=ltot_sb[:], func=AF.Exp).then_inc(sS, 1)
    cS.add(1)

    for c in range(NCHUNK):
        @block.vector
        def _(vector, c=c, w=cS.v):
            vector.wait_ge(sS, w)
            if c >= 1:
                vector.wait_ge(sP, cP.v)
            vector.drain()
            vector.tensor_scalar(out=sm_a[0:NH, 0:128], in0=sm_a[0:NH, 0:128],
                                 scalar1=0.0, scalar2=etot_sb[:, c:c + 1],
                                 op0=ALU.mult, op1=ALU.add)
            vector.drain()
            vector.drain()
            vector.nop().then_inc(sV, 1)
        cV.add(1)

        @block.tensor
        def _(tensor, c=c, w=cV.v, ws=cS.v):
            tensor.wait_ge(sV, w)
            tensor.wait_ge(sS, ws)
            tensor.transpose(ph[0:128, 128:192], sm_a[0:NH, 0:128],
                             identf[0:NH, 0:NH]).then_inc(sP, 1)
        cP.add(1)

        @block.vector
        def _(vector, c=c, w=cP.v):
            vector.wait_ge(sP, w)
            vector.tensor_copy(out=dbc[:, c * 64:(c + 1) * 64],
                               in_=ph[0:128, 128:192]).then_inc(sV, 1)
        cV.add(1)

    if upto < 6:
        return (cV.v, cV.v)

    pj0 = L["pj"][0]

    @block.tensor
    def _(tensor, w=cS.v):
        tensor.wait_ge(sS, w)
        tensor.transpose(pj0[0:1, 1280:1344], sm_a[0:NH, 2:3],
                         identf[0:NH, 0:NH]).then_inc(sP, 1)
    cP.add(1)

    @block.vector
    def _(vector, w=cP.v):
        vector.wait_ge(sP, w)
        vector.tensor_copy(out=dslrow[:], in_=pj0[0:1, 1280:1344]).then_inc(sV, 1)
    cV.add(1)
    ev_dsl = cV.v
    ev_smalls_done = cV.v
    return ev_smalls_done, ev_dsl


def _emit_bfp(block, sV, cV, sS, cS, srcf, qs_f, qi_i, q256a, q256b, q256c,
              q256d, dstf, invln2, ln2c, cstq=None):
    @block.vector
    def _(vector, w=cV.v):
        vector.wait_ge(sV, w)
        vector.tensor_reduce(
            out=q256a[:], in_=srcf.rearrange("p (g k) -> p g k", k=16),
            axis=AX, op=ALU.max, apply_absolute_value=True)
        vector.drain()
        vector.tensor_scalar(out=q256a[:], in0=q256a[:], scalar1=1e-38,
                             scalar2=None, op0=ALU.max)
        vector.drain()
        vector.drain()
        vector.nop().then_inc(sV, 1)
    cV.add(1)

    @block.scalar
    def _(scalar, w=cV.v):
        scalar.wait_ge(sV, w)
        scalar.activation(out=q256b[:], in_=q256a[:], func=AF.Ln).then_inc(sS, 1)
    cS.add(1)

    @block.vector
    def _(vector, w=cS.v):
        vector.wait_ge(sS, w)
        vector.tensor_scalar(out=q256b[:], in0=q256b[:], scalar1=invln2,
                             scalar2=None, op0=ALU.mult)
        vector.drain()
        vector.tensor_copy(out=qi_i[:, 0:256], in_=q256b[:])
        vector.drain()
        vector.tensor_copy(out=q256c[:], in_=qi_i[:, 0:256])
        vector.drain()
        vector.tensor_tensor(out=q256d[:], in0=q256c[:], in1=q256b[:], op=ALU.is_gt)
        vector.drain()
        vector.tensor_sub(q256c[:], q256c[:], q256d[:])
        vector.drain()
        vector.drain()
        vector.nop().then_inc(sV, 1)
    cV.add(1)

    @block.scalar
    def _(scalar, w=cV.v):
        scalar.wait_ge(sV, w)
        scalar.activation(out=q256a[:], in_=q256c[:], func=AF.Exp,
                          scale=ln2c, bias=cstq[:, 1:2])
        scalar.activation(out=q256b[:], in_=q256c[:], func=AF.Exp,
                          scale=-ln2c, bias=cstq[:, 2:3])
        scalar.drain()
        scalar.nop().then_inc(sS, 1)
    cS.add(1)

    @block.vector
    def _(vector, w=cS.v):
        vector.wait_ge(sS, w)
        vector.tensor_tensor(
            out=qs_f[:].rearrange("p (g k) -> p g k", k=16),
            in0=srcf.rearrange("p (g k) -> p g k", k=16),
            in1=q256b[:].unsqueeze(2).broadcast_to([128, 256, 16]), op=ALU.mult)
        vector.drain()
        vector.tensor_scalar(out=qs_f[:], in0=qs_f[:], scalar1=127.49999,
                             scalar2=-127.49999, op0=ALU.min, op1=ALU.max)
        vector.drain()
        vector.tensor_copy(out=qi_i[:], in_=qs_f[:])
        vector.drain()
        vector.tensor_tensor(
            out=dstf.rearrange("p (g k) -> p g k", k=16),
            in0=qi_i[:].rearrange("p (g k) -> p g k", k=16),
            in1=q256a[:].unsqueeze(2).broadcast_to([128, 256, 16]),
            op=ALU.mult).then_inc(sV, 1)
    cV.add(1)


def _finish(nc, block, es, taps, sV, cV, sG, cG, out_waits, out_t, extra_ctx=(),
            all_sems=()):
    @block.gpsimd
    def _(gpsimd):
        gpsimd.wait_ge(sV, cV.v)
        for d, src in taps:
            gpsimd.wait_ge(sG, cG.v)
            gpsimd.dma_start(out=d, in_=src).then_inc(sG, 16)
            cG.add(16)
        gpsimd.wait_ge(sG, cG.v)
        for s_, v_ in out_waits:
            gpsimd.wait_ge(s_, v_)
        for s_, c_ in all_sems:
            gpsimd.wait_ge(s_, c_.v)

    for ctx in extra_ctx:
        ctx.close()
    es.close()
    return nc


def _host_prep(inputs):
    x = np.ascontiguousarray(np.asarray(inputs["L0_attn_hidden_states"], np.float32))
    Wi = np.asarray(inputs["L0_attn_W_in"], np.float32)
    convw = np.asarray(inputs["L0_attn_conv_w"], np.float32)
    dtb = np.asarray(inputs["L0_attn_dt_bias"], np.float32)
    alog = np.asarray(inputs["L0_attn_A_log"], np.float32)
    dpar = np.asarray(inputs["L0_attn_D_param"], np.float32)
    gnorm = np.asarray(inputs["L0_attn_norm_weight"], np.float32)
    Wo = np.asarray(inputs["L0_attn_W_out"], np.float32)
    Wf = np.asarray(inputs["L0_ffn_weight"], np.float32)
    fb = np.asarray(inputs["L0_ffn_bias"], np.float32)

    wi_t = np.zeros((H, NCC * 128), np.float32)
    wi_t[:, :16704] = Wi
    wi_t = np.ascontiguousarray(
        wi_t.reshape(2, 16, 128, NCC, 128).transpose(3, 0, 2, 1, 4))
    wo_t = np.ascontiguousarray(
        Wo.reshape(64, 128, 8, 512).transpose(2, 0, 1, 3))
    convw_t = np.ascontiguousarray(convw.reshape(66, 128, 4).transpose(1, 0, 2)
                                   ).reshape(128, 264)
    dpar_bc = np.ascontiguousarray(np.broadcast_to(dpar[None, :], (128, NH)))
    gnorm_t = np.ascontiguousarray(gnorm.reshape(NH, 128).T)
    fbias_t = np.ascontiguousarray(fb.reshape(32, 128).T)
    j_idx = np.arange(128)
    maskbias = np.where(j_idx[:, None] <= j_idx[None, :], 0.0, -1e30).astype(np.float32)
    ident = np.eye(128, dtype=np.float32)

    shared = dict(wi=wi_t, wo=wo_t, wf=Wf, convw_t=convw_t,
                  dtb=dtb.reshape(NH, 1), alog=alog.reshape(NH, 1),
                  dpar=dpar_bc, gnorm=gnorm_t, fbias=fbias_t,
                  maskb=maskbias, maskbT=np.ascontiguousarray(maskbias.T),
                  identb=ident)

    in_maps = []
    for core in range(8):
        b, rk = core // 4, core % 4
        xsl = np.zeros((TOKX, H), np.float32)
        if rk > 0:
            xsl[0:3] = x[b, rk * 1024 - 3:rk * 1024]
        xsl[3:] = x[b, rk * 1024:(rk + 1) * 1024]
        cm = np.zeros((8, 16), np.float32)
        grp = range((core // 4) * 4, (core // 4) * 4 + 4)
        msel = np.zeros(8, np.float32)
        for j in grp:
            rj = j % 4
            if rj < rk:
                msel[j] = 1.0
                for k in grp:
                    if rj < (k % 4) < rk:
                        cm[j, k] = 1.0
        cm[:, 8:16] = 1.0 - cm[:, 0:8]
        cm[:, 15] = msel
        in_maps.append(dict(xs=np.ascontiguousarray(xsl), cmask=cm, **shared))
    return x, in_maps


_NC_CACHE = {}


def kernel(**inputs):
    x, in_maps = _host_prep(inputs)
    key = ("full",)
    if key not in _NC_CACHE:
        _NC_CACHE[key] = build()
    nc = _NC_CACHE[key]
    res = run_bass_kernel_spmd(nc, in_maps, core_ids=list(range(8)))
    out = np.zeros_like(x)
    for core in range(8):
        b, rk = core // 4, core % 4
        out[b, rk * 1024:(rk + 1) * 1024] = res.results[core]["out"]
    return out


# version 17
# speedup vs baseline: 1.2207x; 1.0127x over previous
import numpy as np
from contextlib import ExitStack

import concourse.bass as bass
import concourse.mybir as mybir
from concourse.bass_utils import run_bass_kernel_spmd

F32 = mybir.dt.float32
BF16 = mybir.dt.bfloat16
I32 = mybir.dt.int32
AF = mybir.ActivationFunctionType
ALU = mybir.AluOpType
AX = mybir.AxisListType.X

H = 4096
DSSM = 8192
NH = 64
EPS = 1e-5
CH = 128
NCHUNK = 8
TOK = 1024
TOKX = 1027
LN2C = float(np.log(2.0))
INVLN2 = float(1.0 / np.log(2.0))
NCC = 131
CC_B, CC_C, CC_DT = 128, 129, 130
SLOTS = [0, 1, 2, 4, 5, 6]


class Ct:
    __slots__ = ("v",)

    def __init__(self):
        self.v = 0

    def add(self, n=1):
        self.v += n
        return self.v


def build(phase_limit=99, want_taps=(), max_stripes=None, dbg_mode=99):
    want_taps = set(want_taps)
    nc = bass.Bass("TRN2", target_bir_lowering=False, debug=False)
    es = ExitStack()

    def sb(name, shape, dtype):
        return es.enter_context(nc.sbuf_tensor(name, shape, dtype))

    xs = nc.dram_tensor("xs", [TOKX, H], F32, kind="ExternalInput").ap()
    wi = nc.dram_tensor("wi", [NCC, 2, 128, 16, 128], F32, kind="ExternalInput").ap()
    wo = nc.dram_tensor("wo", [8, 64, 128, 512], F32, kind="ExternalInput").ap()
    wf = nc.dram_tensor("wf", [4096, 4096], F32, kind="ExternalInput").ap()
    convw_in = nc.dram_tensor("convw_t", [128, 264], F32, kind="ExternalInput").ap()
    dtb_in = nc.dram_tensor("dtb", [NH, 1], F32, kind="ExternalInput").ap()
    alog_in = nc.dram_tensor("alog", [NH, 1], F32, kind="ExternalInput").ap()
    dpar_in = nc.dram_tensor("dpar", [128, NH], F32, kind="ExternalInput").ap()
    gnorm_in = nc.dram_tensor("gnorm", [128, NH], F32, kind="ExternalInput").ap()
    fbias_in = nc.dram_tensor("fbias", [128, 32], F32, kind="ExternalInput").ap()
    mask_in = nc.dram_tensor("maskb", [128, 128], F32, kind="ExternalInput").ap()
    maskT_in = nc.dram_tensor("maskbT", [128, 128], F32, kind="ExternalInput").ap()
    ident_in = nc.dram_tensor("identb", [128, 128], F32, kind="ExternalInput").ap()
    cmask_in = nc.dram_tensor("cmask", [8, 16], F32, kind="ExternalInput").ap()
    out_t = nc.dram_tensor("out", [TOK, H], F32, kind="ExternalOutput").ap()

    xtm_st = nc.dram_tensor("xtm_st", [NCHUNK, NH, 128, 128], BF16).ap()
    sz_st = nc.dram_tensor("sz_st", [NCHUNK, NH, 128, 128], BF16).ap()
    yn_st = nc.dram_tensor("yn_st", [NH, 128, TOK], BF16).ap()
    x1_st = nc.dram_tensor("x1_st", [TOK, H], F32).ap()
    cc_in = nc.dram_tensor("cc_in", [128, DSSM], BF16)
    cc_out = nc.dram_tensor("cc_out", [1024, DSSM], BF16)
    ccd_in = nc.dram_tensor("ccd_in", [1, NH], F32)
    ccd_out = nc.dram_tensor("ccd_out", [8, NH], F32)

    taps = []

    def tap(name, src_ap):
        if name in want_taps:
            d = nc.dram_tensor("tap_" + name, list(src_ap.shape), src_ap.dtype,
                               kind="ExternalOutput").ap()
            taps.append((d, src_ap))

    sK = es.enter_context(nc.semaphore("sK"))
    sWp = [es.enter_context(nc.semaphore(f"sW{i}")) for i in range(2)]
    sWo = [es.enter_context(nc.semaphore(f"sWo{i}")) for i in range(2)]
    sXp = [es.enter_context(nc.semaphore(f"sX{i}")) for i in range(2)]
    sXc = es.enter_context(nc.semaphore("sXc"))
    sTp = [es.enter_context(nc.semaphore(f"sT{i}")) for i in range(4)]
    sZp = [es.enter_context(nc.semaphore(f"sZ{i}")) for i in range(2)]
    sNp = [es.enter_context(nc.semaphore(f"sN{i}")) for i in range(2)]
    sYp = [es.enter_context(nc.semaphore(f"sY{i}")) for i in range(2)]
    sOp = [es.enter_context(nc.semaphore(f"sO{i}")) for i in range(2)]
    sG = es.enter_context(nc.semaphore("sG"))
    sP = es.enter_context(nc.semaphore("sP"))
    sV = es.enter_context(nc.semaphore("sV"))
    sS = es.enter_context(nc.semaphore("sS"))
    sC = es.enter_context(nc.semaphore("sC"))
    cK = Ct()
    cWp = [Ct(), Ct()]
    cWo = [Ct(), Ct()]
    cXp = [Ct(), Ct()]
    cXc = Ct()
    cTp = [Ct() for _ in range(4)]
    cZp = [Ct(), Ct()]
    cNp = [Ct(), Ct()]
    cYp = [Ct(), Ct()]
    cOp = [Ct(), Ct()]
    cG, cP, cV, cS, cC = (Ct() for _ in range(5))
    all_dma_sems = ([(sK, cK), (sXc, cXc), (sG, cG)] +
                    list(zip(sWp, cWp)) + list(zip(sWo, cWo)) +
                    list(zip(sXp, cXp)) +
                    list(zip(sTp, cTp)) + list(zip(sZp, cZp)) +
                    list(zip(sNp, cNp)) + list(zip(sYp, cYp)) +
                    list(zip(sOp, cOp)))

    convw_sb = sb("convw_sb", [128, 264], F32)
    dtb_sb = sb("dtb_sb", [NH, 1], F32)
    alog_sb = sb("alog_sb", [NH, 1], F32)
    dparc = sb("dparc", [128, NH], F32)
    gnormc = sb("gnormc", [128, NH], F32)
    fbiasc = sb("fbiasc", [128, 32], F32)
    identf = sb("identf_s", [128, 128], F32)
    identb = sb("identb_s", [128, 128], BF16)
    onesrow = sb("onesrow", [1, 128], F32)
    onescol_bf = sb("onescol_bf", [128, 1], BF16)
    cmask_sb = sb("cmask_sb", [8, 16], F32)
    cst = sb("cst", [128, 4], F32)

    msctx = ExitStack()
    b_fm = sb("b_fm", [128, TOK], BF16)
    c_fm = sb("c_fm", [128, TOK], BF16)
    b_tm = sb("b_tm", [128, NCHUNK * 128], BF16)
    dt_sb = msctx.enter_context(nc.sbuf_tensor("dt_sb", [NH, TOK], F32, side="right"))
    lda_sb = msctx.enter_context(nc.sbuf_tensor("lda_sb", [NH, TOK], F32, side="right"))
    lndt_sb = msctx.enter_context(nc.sbuf_tensor("lndt_sb", [NH, TOK], F32, side="right"))
    ll_sb = msctx.enter_context(nc.sbuf_tensor("ll_sb", [NH, TOK], F32, side="right"))
    offs_sb = sb("offs_sb", [NH, 16], F32)
    ltot_sb = sb("ltot_sb", [NH, 8], F32)
    etot_sb = sb("etot_sb", [NH, 8], F32)
    cumtm = msctx.enter_context(nc.sbuf_tensor("cumtm", [128, NCHUNK * 64], F32, side="right"))
    maskT_sb = sb("maskT_sb", [128, 128], F32)
    dgT_sb = sb("dgT_sb", [NH, 8], F32)
    coefT_sb = sb("coefT_sb", [NH, 8], F32)
    wtm = msctx.enter_context(nc.sbuf_tensor("wtm", [128, NCHUNK * 64], F32, side="right"))
    ltm = msctx.enter_context(nc.sbuf_tensor("ltm", [128, NCHUNK * 64], F32, side="right"))
    wctm = msctx.enter_context(nc.sbuf_tensor("wctm", [128, NCHUNK * 64], F32, side="right"))
    lndtm = msctx.enter_context(nc.sbuf_tensor("lndtm", [128, NCHUNK * 64], F32, side="right"))
    dbc = msctx.enter_context(nc.sbuf_tensor("dbc", [128, NCHUNK * 64], F32, side="right"))
    sm_a = sb("sm_a", [NH, 128], F32)
    sm_b = sb("sm_b", [NH, 128], F32)
    dslrow = sb("dslrow", [1, NH], F32)
    coef_sb = sb("coef_sb", [8, NH], F32)
    coefbc = sb("coefbc", [128, 8 * NH], F32)
    dg_sb = sb("dg_sb", [8, NH], F32)

    pt = es.enter_context(nc.psum_tensor("pt", [128, 512], F32))
    ph = es.enter_context(nc.psum_tensor("ph", [128, 512], F32))

    block = es.enter_context(nc.Block())

    lnxt_ctx = ExitStack()
    lnxt = lnxt_ctx.enter_context(nc.sbuf_tensor("lnxt", [128, 32 * 1028], BF16))
    ln_ctx = ExitStack()
    x_tile = ln_ctx.enter_context(nc.sbuf_tensor("x_tile", [128, 2 * H], F32))
    xc_f = ln_ctx.enter_context(nc.sbuf_tensor("xc_f", [128, 2 * H], F32))
    ln_small = ln_ctx.enter_context(nc.sbuf_tensor("ln_small", [128, 16], F32))
    rrow = ln_ctx.enter_context(nc.sbuf_tensor("rrow", [1, 128], F32))
    rbcast = ln_ctx.enter_context(nc.sbuf_tensor("rbcast", [128, 128], F32))
    sq_sink = ln_ctx.enter_context(nc.sbuf_tensor("sq_sink", [128, H], BF16))
    pjs = ExitStack()
    pj = [pjs.enter_context(nc.psum_tensor(f"pj{i}", [128, 1536], F32)) for i in range(2)]

    @block.sync
    def _(sync):
        for dst, src in ((convw_sb[:], convw_in), (dtb_sb[:], dtb_in),
                         (alog_sb[:], alog_in), (dparc[:], dpar_in),
                         (gnormc[:], gnorm_in), (fbiasc[:], fbias_in),
                         (maskT_sb[:], maskT_in), (identf[:], ident_in),
                         (cmask_sb[:], cmask_in)):
            sync.dma_start(out=dst, in_=src).then_inc(sK, 16)
            cK.add(16)

    @block.vector
    def _(vector):
        vector.wait_ge(sK, cK.v)
        vector.memset(onesrow[:], 1.0)
        vector.memset(onescol_bf[:], 1.0)
        vector.memset(cst[:, 0:1], EPS)
        vector.memset(cst[:, 1:2], -7.0 * LN2C)
        vector.memset(cst[:, 2:3], 7.0 * LN2C)
        vector.tensor_copy(out=identb[:], in_=identf[:])
        vector.drain()
        vector.drain()
        vector.nop().then_inc(sV, 1)
    cV.add(1)
    ev_consts = cV.v

    xrows = [(t * 128, 128) for t in range(8)] + [(1024, 3)]
    ln_v = [None] * 9
    ln_pe = [None] * 9
    ln_xc = [None] * 9

    for tt, (r0, nr) in enumerate(xrows):
        q = tt % 2

        @block.scalar
        def _(scalar, tt=tt, r0=r0, nr=nr, q=q):
            if tt >= 2:
                scalar.wait_ge(sP, ln_pe[tt - 2])
                scalar.wait_ge(sV, ln_xc[tt - 2])
            scalar.dma_start(out=x_tile[0:nr, q * H:q * H + H],
                             in_=xs[r0:r0 + nr, :]).then_inc(sXp[q], 16)
        cXp[q].add(16)
        ev_xload = (sXp[q], cXp[q].v)

        @block.scalar
        def _(scalar, tt=tt, nr=nr, q=q, ev_xload=ev_xload):
            scalar.wait_ge(ev_xload[0], ev_xload[1])
            if tt == 0:
                scalar.wait_ge(sV, ev_consts)
            scalar.activation(out=sq_sink[0:nr, :], in_=x_tile[0:nr, q * H:q * H + H],
                              func=AF.Square).then_inc(sS, 1)
        cS.add(1)
        ev_sq = cS.v

        @block.vector
        def _(vector, tt=tt, nr=nr, q=q, ev_xload=ev_xload, ev_sq=ev_sq):
            vector.wait_ge(ev_xload[0], ev_xload[1])
            vector.wait_ge(sS, ev_sq)
            xt = x_tile[0:nr, q * H:q * H + H]
            vector.reduce_sum(ln_small[0:nr, 1:2], sq_sink[0:nr, :], axis=AX)
            vector.reduce_sum(ln_small[0:nr, 0:1], xt, axis=AX)
            vector.drain()
            vector.tensor_scalar(out=ln_small[0:nr, 1:2], in0=ln_small[0:nr, 1:2],
                                 scalar1=1.0 / H, scalar2=None, op0=ALU.mult)
            vector.tensor_scalar(out=ln_small[0:nr, 0:1], in0=ln_small[0:nr, 0:1],
                                 scalar1=1.0 / H, scalar2=None, op0=ALU.mult)
            vector.drain()
            vector.tensor_tensor(out=ln_small[0:nr, 2:3], in0=ln_small[0:nr, 0:1],
                                 in1=ln_small[0:nr, 0:1], op=ALU.mult)
            vector.tensor_scalar(out=ln_small[0:nr, 5:6], in0=ln_small[0:nr, 0:1],
                                 scalar1=-1.0, scalar2=None, op0=ALU.mult)
            vector.drain()
            vector.tensor_sub(ln_small[0:nr, 2:3], ln_small[0:nr, 1:2],
                              ln_small[0:nr, 2:3])
            vector.drain()
            vector.drain()
            vector.nop().then_inc(sV, 1)
        cV.add(1)
        ev_stats = cV.v

        @block.scalar
        def _(scalar, nr=nr, ev_stats=ev_stats):
            scalar.wait_ge(sV, ev_stats)
            scalar.activation(out=ln_small[0:nr, 3:4], in_=ln_small[0:nr, 2:3],
                              func=AF.Sqrt, bias=cst[0:nr, 0:1]).then_inc(sS, 1)
        cS.add(1)
        ev_sqrt = cS.v

        @block.vector
        def _(vector, nr=nr, q=q, ev_sqrt=ev_sqrt):
            vector.wait_ge(sS, ev_sqrt)
            vector.reciprocal(out=ln_small[0:nr, 4:5], in_=ln_small[0:nr, 3:4])
            vector.drain()
            vector.tensor_scalar(out=xc_f[0:nr, q * H:q * H + H],
                                 in0=x_tile[0:nr, q * H:q * H + H],
                                 scalar1=ln_small[0:nr, 5:6], scalar2=None,
                                 op0=ALU.add)
            vector.drain()
            vector.drain()
            vector.nop().then_inc(sV, 1)
        cV.add(1)
        ln_xc[tt] = cV.v

        @block.tensor
        def _(tensor, tt=tt, nr=nr):
            tensor.wait_ge(sV, ln_xc[tt])
            if tt >= 1:
                tensor.wait_ge(sV, ln_v[tt - 1])
            tensor.transpose(ph[0:1, 0:nr], ln_small[0:nr, 4:5],
                             identf[0:nr, 0:nr]).then_inc(sP, 1)
        cP.add(1)

        @block.vector
        def _(vector, nr=nr, w=cP.v):
            vector.wait_ge(sP, w)
            vector.tensor_copy(out=rrow[0:1, 0:nr], in_=ph[0:1, 0:nr]).then_inc(sV, 1)
        cV.add(1)

        @block.tensor
        def _(tensor, nr=nr, w=cV.v):
            tensor.wait_ge(sV, w)
            tensor.matmul(ph[:, 256:256 + nr], onesrow[:], rrow[0:1, 0:nr],
                          start=True, stop=True).then_inc(sP, 1)
        cP.add(1)

        @block.vector
        def _(vector, nr=nr, w=cP.v):
            vector.wait_ge(sP, w)
            vector.tensor_copy(out=rbcast[:, 0:nr],
                               in_=ph[:, 256:256 + nr]).then_inc(sV, 1)
        cV.add(1)
        ev_rb = cV.v

        evac_ev = []
        for fc in range(32):
            @block.tensor
            def _(tensor, tt=tt, nr=nr, q=q, fc=fc):
                if fc == 0:
                    tensor.wait_ge(sV, ev_rb)
                if fc >= 2:
                    tensor.wait_ge(sV, evac_ev[fc - 2])
                dstp = pt if fc % 2 == 0 else ph
                tensor.transpose(dstp[:, 0:nr],
                                 xc_f[0:nr, q * H + fc * 128:q * H + fc * 128 + 128],
                                 identf[0:nr, 0:nr]).then_inc(sP, 1)
            cP.add(1)

            @block.vector
            def _(vector, r0=r0, nr=nr, fc=fc, w=cP.v):
                vector.wait_ge(sP, w)
                srcp = pt if fc % 2 == 0 else ph
                vector.tensor_tensor(
                    out=lnxt[:, fc * 1028 + r0:fc * 1028 + r0 + nr],
                    in0=srcp[:, 0:nr], in1=rbcast[:, 0:nr],
                    op=ALU.mult).then_inc(sV, 1)
            cV.add(1)
            evac_ev.append(cV.v)
        ln_v[tt] = cV.v
        ln_pe[tt] = cP.v

    tap("lnxt", lnxt[:, :])

    if phase_limit < 1:
        return _finish(nc, block, es, taps, sV, cV, sG, cG, [], out_t,
                       extra_ctx=[pjs, ln_ctx, lnxt_ctx], all_sems=all_dma_sems)

    ln_ctx.close()
    sctx = ExitStack()
    s_all = sctx.enter_context(nc.sbuf_tensor("s_all", [128, NH * 128], F32, side="right"))
    phA_ctx = ExitStack()
    exch = phA_ctx.enter_context(nc.sbuf_tensor("exch", [128, NH * 128], BF16))
    wbf = phA_ctx.enter_context(nc.sbuf_tensor("wbf", [128, 2 * 2048], BF16))
    conv_acc = phA_ctx.enter_context(nc.sbuf_tensor("conv_acc", [128, TOK], F32))
    xch_f = phA_ctx.enter_context(nc.sbuf_tensor("xch_f", [128, 2 * TOK], F32))
    bc_f = phA_ctx.enter_context(nc.sbuf_tensor("bc_f", [128, TOK], F32))
    dt_raw = phA_ctx.enter_context(nc.sbuf_tensor("dt_raw", [NH, TOK], F32))
    xw_sb = phA_ctx.enter_context(nc.sbuf_tensor("xw_sb", [128, 2 * 128], BF16))
    xtm_sb = phA_ctx.enter_context(nc.sbuf_tensor("xtm_sb", [128, 2 * 128], BF16))
    szh_bf = phA_ctx.enter_context(nc.sbuf_tensor("szh_bf", [128, 2 * TOK], BF16))
    slot_sb = phA_ctx.enter_context(nc.sbuf_tensor("slot_sb", [128, DSSM // 4], BF16))

    @block.vector
    def _(vector):
        vector.memset(conv_acc[:], 0.0).then_inc(sV, 1)
    cV.add(1)

    cc_order = [CC_DT, CC_B, CC_C] + list(range(64, 128)) + list(range(0, 64))
    if max_stripes is not None:
        cc_order = cc_order[:max_stripes]
    NSTRIPE = len(cc_order)
    evloadH = [None] * (2 * NSTRIPE)
    evcastH = [None] * (2 * NSTRIPE)
    evmmH = [None] * (2 * NSTRIPE)
    evmm = [None] * NSTRIPE
    evfree = [None] * NSTRIPE
    evsilu = [None] * NSTRIPE
    ev_hloc_ev = {}
    ev_xtm_dma = {}
    btm_ev = []
    ev_xtm_ev = {}
    ev_xtm_T = {}
    ev_smalls_done = None
    ev_dsl = None
    ev_sz_dma = {}

    GRP_FULL = [(0, 512), (512, 512), (1024, 3)]
    GRP_OWN = [(3, 0, 512), (515, 512, 512)]

    def xbc_ci(cc):
        if cc == CC_B:
            return 64
        if cc == CC_C:
            return 65
        return cc - 64

    last_ln = ln_v[8]

    def emit_tblock(ii, cc2):
        is_b2 = cc2 == CC_B
        is_xh2 = 64 <= cc2 < 128
        if is_b2:
            btm_ev.clear()
            for c in range(NCHUNK):
                @block.tensor
                def _(tensor, i=ii, c=c):
                    if c == 0:
                        tensor.wait_ge(sS, evsilu[i])
                        tensor.wait_ge(sV, ev_smalls_done)
                    if c >= 2:
                        tensor.wait_ge(sV, btm_ev[c - 2])
                    dstp = pt if c % 2 == 0 else ph
                    tensor.transpose(dstp[:, 0:128], bc_f[:, c * 128:c * 128 + 128],
                                     identf[:]).then_inc(sP, 1)
                cP.add(1)

                @block.vector
                def _(vector, c=c, w=cP.v):
                    vector.wait_ge(sP, w)
                    srcp = pt if c % 2 == 0 else ph
                    vector.tensor_copy(out=b_tm[:, c * 128:(c + 1) * 128],
                                       in_=srcp[:, 0:128]).then_inc(sV, 1)
                cV.add(1)
                btm_ev.append(cV.v)
        if is_xh2:
            h = cc2 - 64
            hq = ii % 2
            for c in range(NCHUNK):
                @block.tensor
                def _(tensor, i=ii, h=h, c=c, hq=hq):
                    if c == 0:
                        tensor.wait_ge(sS, evsilu[i])
                        tensor.wait_ge(sV, ev_smalls_done)
                        if h >= 1:
                            tensor.wait_ge(sV, ev_hloc_ev[h - 1])
                    if c >= 2:
                        tensor.wait_ge(sV, ev_xtm_ev[(h, c - 2)])
                    dstp = pt if c % 2 == 0 else ph
                    tensor.transpose(dstp[:, 0:128],
                                     xch_f[:, hq * TOK + c * 128:hq * TOK + c * 128 + 128],
                                     identf[:]).then_inc(sP, 1)
                cP.add(1)
                ev_xtm_T[(h, c)] = cP.v

                @block.vector
                def _(vector, h=h, c=c, w=cP.v):
                    vector.wait_ge(sP, w)
                    if (h, c - 2) in ev_xtm_dma:
                        vector.wait_ge(sTp[(c - 2) % 4], ev_xtm_dma[(h, c - 2)])
                    elif c < 2 and h >= 1 and (h - 1, 6 + c) in ev_xtm_dma:
                        vector.wait_ge(sTp[(6 + c) % 4], ev_xtm_dma[(h - 1, 6 + c)])
                    srcp = pt if c % 2 == 0 else ph
                    vector.tensor_copy(out=xtm_sb[:, (c % 2) * 128:(c % 2) * 128 + 128],
                                       in_=srcp[:, 0:128])
                    vector.tensor_scalar(
                        out=xw_sb[:, (c % 2) * 128:(c % 2) * 128 + 128],
                        in0=srcp[:, 0:128],
                        scalar1=wtm[:, c * 64 + h:c * 64 + h + 1], scalar2=None,
                        op0=ALU.mult).then_inc(sV, 1)
                cV.add(1)
                ev_xtm_ev[(h, c)] = cV.v

                @block.sync
                def _(sync, h=h, c=c, w=cV.v):
                    sync.wait_ge(sV, w)
                    sync.dma_start(out=xtm_st[c, h],
                                   in_=xtm_sb[:, (c % 2) * 128:(c % 2) * 128 + 128]
                                   ).then_inc(sTp[c % 4], 16)
                cTp[c % 4].add(16)
                ev_xtm_dma[(h, c)] = cTp[c % 4].v

                @block.tensor
                def _(tensor, h=h, c=c, ii=ii, w=cV.v):
                    q2 = ii % 2
                    tensor.wait_ge(sV, w)
                    tensor.matmul(pj[q2][:, 1152:1280],
                                  b_tm[:, c * 128:(c + 1) * 128],
                                  xw_sb[:, (c % 2) * 128:(c % 2) * 128 + 128],
                                  start=(c == 0), stop=(c == NCHUNK - 1)).then_inc(sP, 1)
                cP.add(1)
            ev_hloc_mm = cP.v

            @block.vector
            def _(vector, h=h, ii=ii, w=ev_hloc_mm):
                q2 = ii % 2
                vector.wait_ge(sP, w)
                vector.tensor_copy(out=exch[:, h * 128:(h + 1) * 128],
                                   in_=pj[q2][:, 1152:1280]).then_inc(sV, 1)
            cV.add(1)
            ev_hloc_ev[h] = cV.v
            evfree[ii] = (sV, cV.v)

            if h == 63:
                @block.gpsimd
                def _(gpsimd):
                    gpsimd.wait_ge(sV, ev_hloc_ev[63])
                    gpsimd.wait_ge(sV, ev_dsl)
                    gpsimd.dma_start(out=cc_in.ap(), in_=exch[:]).then_inc(sG, 16)
                    gpsimd.dma_start(out=ccd_in.ap(), in_=dslrow[:]).then_inc(sG, 16)
                    gpsimd.wait_ge(sG, cG.v + 32)
                    gpsimd.collective_compute(
                        "AllGather", ALU.bypass, replica_groups=[list(range(8))],
                        ins=[cc_in.ap().opt()], outs=[cc_out.ap().opt()]).then_inc(sC, 1)
                    gpsimd.collective_compute(
                        "AllGather", ALU.bypass, replica_groups=[list(range(8))],
                        ins=[ccd_in.ap().opt()], outs=[ccd_out.ap().opt()]).then_inc(sC, 1)
                cG.add(32)
                cC.add(2)


    for i, cc in enumerate(cc_order):
        q = i % 2
        is_dt = cc == CC_DT
        is_b = cc == CC_B
        is_c = cc == CC_C
        is_xh = 64 <= cc < 128
        is_z = cc < 64
        h = cc - 64 if is_xh else None

        for hf in range(2):
            j = 2 * i + hf
            qj = j % 2

            @block.gpsimd
            def _(gpsimd, i=i, cc=cc, hf=hf, j=j, qj=qj):
                if j < 2:
                    gpsimd.wait_ge(sV, last_ln)
                if j >= 2:
                    gpsimd.wait_ge(sP, evmmH[j - 2])
                gpsimd.dma_start(out=wbf[:, qj * 2048:(qj + 1) * 2048],
                                 in_=wi[cc, hf]).then_inc(sWp[qj], 16)
            cWp[qj].add(16)
            evloadH[j] = (sWp[qj], cWp[qj].v)

            @block.tensor
            def _(tensor, i=i, q=q, is_z=is_z, hf=hf, j=j, qj=qj):
                if dbg_mode < 2:
                    tensor.nop().then_inc(sP, 1)
                    return
                tensor.wait_ge(evloadH[j][0], evloadH[j][1])
                if hf == 0:
                    if i == 0:
                        tensor.wait_ge(sV, last_ln)
                    if i >= 2:
                        s, v = evfree[i - 2]
                        tensor.wait_ge(s, v)
                last = None
                for kk16 in range(16):
                    kk = hf * 16 + kk16
                    a = wbf[:, qj * 2048 + kk16 * 128:qj * 2048 + kk16 * 128 + 128]
                    if is_z:
                        for (ro, po, n) in GRP_OWN:
                            last = tensor.matmul(
                                pj[q][:, po:po + n], a,
                                lnxt[:, kk * 1028 + ro:kk * 1028 + ro + n],
                                start=(kk == 0), stop=(kk == 31))
                    else:
                        for (o, n) in GRP_FULL:
                            last = tensor.matmul(
                                pj[q][:, o:o + n], a,
                                lnxt[:, kk * 1028 + o:kk * 1028 + o + n],
                                start=(kk == 0), stop=(kk == 31))
                last.then_inc(sP, 1)
            cP.add(1)
            evmmH[j] = cP.v
        evmm[i] = evmmH[2 * i + 1]

        if i >= 1:
            emit_tblock(i - 1, cc_order[i - 1])
        if i == NSTRIPE - 1:
            _defer_last = True

        if is_dt:
            @block.vector
            def _(vector, i=i, q=q):
                vector.wait_ge(sP, evmm[i])
                if dbg_mode < 3:
                    vector.memset(dt_raw[:, :], 0.1)
                    vector.drain()
                    vector.nop().then_inc(sV, 1)
                    return
                vector.tensor_copy(out=dt_raw[:, :],
                                   in_=pj[q][0:NH, 3:TOKX]).then_inc(sV, 1)
            cV.add(1)
            evfree[i] = (sV, cV.v)
        elif is_z:
            zi = cc

            @block.scalar
            def _(scalar, i=i, q=q, zi=zi):
                scalar.wait_ge(sP, evmm[i])
                if zi >= 2:
                    scalar.wait_ge(sZp[zi % 2], ev_sz_dma[zi - 2])
                scalar.activation(out=szh_bf[:, (zi % 2) * TOK:(zi % 2 + 1) * TOK],
                                  in_=pj[q][:, 0:TOK],
                                  func=AF.Silu).then_inc(sS, 1)
            cS.add(1)
            evfree[i] = (sS, cS.v)
            evsilu[i] = cS.v

            @block.sync
            def _(sync, zi=zi, w=cS.v):
                sync.wait_ge(sS, w)
                for c in range(NCHUNK):
                    sync.dma_start(
                        out=sz_st[c, zi],
                        in_=szh_bf[:, (zi % 2) * TOK + c * 128:
                                   (zi % 2) * TOK + c * 128 + 128]
                    ).then_inc(sZp[zi % 2], 16)
            cZp[zi % 2].add(16 * NCHUNK)
            ev_sz_dma[zi] = cZp[zi % 2].v
        else:
            ci = xbc_ci(cc)

            @block.vector
            def _(vector, i=i, q=q, ci=ci):
                vector.wait_ge(sP, evmm[i])
                if i >= 1 and evsilu[i - 1] is not None:
                    vector.wait_ge(sS, evsilu[i - 1])
                ca = conv_acc[:, 0:TOK]
                vector.scalar_tensor_tensor(
                    out=ca, in0=pj[q][:, 0:1024],
                    scalar=convw_sb[:, ci * 4:ci * 4 + 1], in1=ca,
                    op0=ALU.mult, op1=ALU.bypass)
                for j in (1, 2, 3):
                    vector.drain()
                    vector.scalar_tensor_tensor(
                        out=ca, in0=pj[q][:, j:1024 + j],
                        scalar=convw_sb[:, ci * 4 + j:ci * 4 + j + 1], in1=ca,
                        op0=ALU.mult, op1=ALU.add)
                vector.drain()
                vector.drain()
                vector.nop().then_inc(sV, 1)
            cV.add(1)
            evfree[i] = (sV, cV.v)

            @block.scalar
            def _(scalar, i=i, cc=cc, w=cV.v):
                scalar.wait_ge(sV, w)
                if cc == CC_C:
                    scalar.wait_ge(sV, btm_ev[NCHUNK - 1])
                if 64 <= cc < 128 and (cc - 64) >= 2:
                    scalar.wait_ge(sP, ev_xtm_T[(cc - 64 - 2, NCHUNK - 1)])
                dst = (xch_f[:, (i % 2) * TOK:(i % 2 + 1) * TOK] if 64 <= cc < 128
                       else bc_f[:])
                scalar.activation(out=dst, in_=conv_acc[:, 0:TOK],
                                  func=AF.Silu).then_inc(sS, 1)
            cS.add(1)
            evsilu[i] = cS.v

            if cc in (CC_B, CC_C):
                @block.vector
                def _(vector, cc=cc, w=cS.v):
                    vector.wait_ge(sS, w)
                    vector.tensor_copy(out=(b_fm[:] if cc == CC_B else c_fm[:]),
                                       in_=bc_f[:]).then_inc(sV, 1)
                cV.add(1)
                evsilu[i] = (cS.v, cV.v)[1] if False else evsilu[i]
                ev_bcf = cV.v

        if is_dt and dbg_mode >= 1:
            ev_smalls_done, ev_dsl = _dt_math(
                block, locals(), sV, cV, sS, cS, sP, cP,
                upto=(dbg_mode - 10 if 10 < dbg_mode < 20 else 99))
        elif is_dt:
            ev_smalls_done, ev_dsl = (cV.v, cV.v)

    tap("dt", dt_sb[:, :])
    tap("ll", ll_sb[:, :])
    tap("cum", cumtm[:, :])
    tap("wtm", wtm[:, :])
    tap("b_fm", b_fm[:, :])
    tap("c_fm", c_fm[:, :])
    tap("exch", exch[:, :])

    if phase_limit < 2:
        return _finish(nc, block, es, taps, sV, cV, sG, cG, [], out_t,
                       extra_ctx=[pjs, phA_ctx, lnxt_ctx, sctx], all_sems=all_dma_sems)

    @block.gpsimd
    def _(gpsimd):
        gpsimd.wait_ge(sC, 2)
        gpsimd.dma_start(out=dg_sb[:], in_=ccd_out.ap()).then_inc(sG, 16)
    cG.add(16)
    ev_dg = cG.v

    @block.vector
    def _(vector):
        vector.wait_ge(sG, ev_dg)
        vector.memset(s_all[:], 0.0)
        vector.memset(coef_sb[:], 1.0)
        vector.drain()
        vector.drain()
        vector.nop().then_inc(sV, 1)
    cV.add(1)

    @block.tensor
    def _(tensor, w=cV.v):
        tensor.wait_ge(sV, w)
        tensor.transpose(pt[0:NH, 0:8], dg_sb[:], identf[0:8, 0:8]).then_inc(sP, 1)
    cP.add(1)

    @block.vector
    def _(vector, w=cP.v):
        vector.wait_ge(sP, w)
        vector.tensor_copy(out=dgT_sb[:], in_=pt[0:NH, 0:8]).then_inc(sV, 1)
    cV.add(1)

    for k in (1, 2, 5, 6):
        @block.vector
        def _(vector, k=k, w=cV.v):
            vector.wait_ge(sV, w)
            vector.wait_ge(sP, cP.v)
            vector.drain()
            vector.tensor_scalar(out=sm_a[0:NH, 0:8], in0=sm_a[0:NH, 0:8],
                                 scalar1=0.0, scalar2=dgT_sb[:, k:k + 1],
                                 op0=ALU.mult, op1=ALU.add)
            vector.drain()
            vector.drain()
            vector.nop().then_inc(sV, 1)
        cV.add(1)

        @block.tensor
        def _(tensor, k=k, w=cV.v):
            tensor.wait_ge(sV, w)
            tensor.transpose(pt[0:8, 64:128], sm_a[0:NH, 0:8],
                             identf[0:NH, 0:NH]).then_inc(sP, 1)
        cP.add(1)

        @block.vector
        def _(vector, k=k, w=cP.v):
            vector.wait_ge(sP, w)
            vector.tensor_scalar(out=sm_b[0:8, 0:NH], in0=pt[0:8, 64:128],
                                 scalar1=cmask_sb[:, k:k + 1],
                                 scalar2=cmask_sb[:, 8 + k:9 + k],
                                 op0=ALU.mult, op1=ALU.add)
            vector.drain()
            vector.tensor_mul(coef_sb[:], coef_sb[:], sm_b[0:8, 0:NH])
            vector.drain()
            vector.drain()
            vector.nop().then_inc(sV, 1)
        cV.add(1)

    @block.vector
    def _(vector):
        vector.tensor_scalar(out=coef_sb[:], in0=coef_sb[:],
                             scalar1=cmask_sb[:, 15:16], scalar2=None,
                             op0=ALU.mult)
        vector.drain()
        vector.drain()
        vector.nop().then_inc(sV, 1)
    cV.add(1)
    ev_coef = cV.v

    @block.tensor
    def _(tensor, w=cV.v):
        tensor.wait_ge(sV, w)
        tensor.transpose(pt[0:NH, 8:16], coef_sb[:], identf[0:8, 0:8]).then_inc(sP, 1)
    cP.add(1)

    @block.vector
    def _(vector, w=cP.v):
        vector.wait_ge(sP, w)
        vector.tensor_copy(out=coefT_sb[:], in_=pt[0:NH, 8:16]).then_inc(sV, 1)
    cV.add(1)

    for ji, j in enumerate(SLOTS):
        @block.vector
        def _(vector, j=j, w=cV.v):
            vector.wait_ge(sV, w)
            vector.wait_ge(sP, cP.v)
            vector.drain()
            vector.tensor_scalar(out=sm_a[0:NH, 0:128], in0=sm_a[0:NH, 0:128],
                                 scalar1=0.0, scalar2=coefT_sb[:, j:j + 1],
                                 op0=ALU.mult, op1=ALU.add)
            vector.drain()
            vector.drain()
            vector.nop().then_inc(sV, 1)
        cV.add(1)

        @block.tensor
        def _(tensor, j=j, w=cV.v):
            tensor.wait_ge(sV, w)
            tensor.transpose(ph[0:128, 192:256], sm_a[0:NH, 0:128],
                             identf[0:NH, 0:NH]).then_inc(sP, 1)
        cP.add(1)

        @block.vector
        def _(vector, j=j, w=cP.v):
            vector.wait_ge(sP, w)
            vector.tensor_copy(out=coefbc[:, j * NH:(j + 1) * NH],
                               in_=ph[0:128, 192:256]).then_inc(sV, 1)
        cV.add(1)
    ev_coefbc = cV.v

    for ji, j in enumerate(SLOTS):
        for qtr in range(4):
            @block.gpsimd
            def _(gpsimd, j=j, ji=ji, qtr=qtr):
                if ji > 0 or qtr > 0:
                    gpsimd.wait_ge(sV, cV.v)
                gpsimd.dma_start(
                    out=slot_sb[:],
                    in_=cc_out.ap()[j * 128:(j + 1) * 128,
                                    qtr * (DSSM // 4):(qtr + 1) * (DSSM // 4)]
                ).then_inc(sG, 16)
            cG.add(16)

            @block.vector
            def _(vector, j=j, qtr=qtr, w=cG.v):
                vector.wait_ge(sG, w)
                for hh in range(NH // 4):
                    hg = qtr * (NH // 4) + hh
                    vector.scalar_tensor_tensor(
                        out=s_all[:, hg * 128:(hg + 1) * 128],
                        in0=slot_sb[:, hh * 128:(hh + 1) * 128],
                        scalar=coefbc[:, j * NH + hg:j * NH + hg + 1],
                        in1=s_all[:, hg * 128:(hg + 1) * 128],
                        op0=ALU.mult, op1=ALU.add)
                vector.drain()
                vector.nop().then_inc(sV, 1)
            cV.add(1)
    ev_hin = cV.v
    tap("h_in", s_all[:, :])

    if phase_limit < 3:
        return _finish(nc, block, es, taps, sV, cV, sG, cG, [], out_t,
                       extra_ctx=[pjs, phA_ctx, lnxt_ctx, sctx], all_sems=all_dma_sems)

    pjs.close()
    phA_ctx.close()
    lnxt_ctx.close()
    yctx = ExitStack()

    def ysb(name, shape, dtype):
        return yctx.enter_context(nc.sbuf_tensor(name, shape, dtype))

    xtmc = ysb("xtmc", [128, 2 * DSSM], BF16)
    szc = ysb("szc", [128, 2 * DSSM], BF16)
    yg_all = ysb("yg_all", [128, DSSM], BF16)
    e_sb = ysb("e_sb", [128, 2 * 128], BF16)
    m1_sb = ysb("m1_sb", [128, 2 * 128], BF16)
    identd = ysb("identd", [128, NH * 128], BF16)
    mv_sb = ysb("mv_sb", [128, 2 * 128], BF16)
    sbf_sb = ysb("sbf_sb", [128, 2 * 128], BF16)
    xw2_sb = ysb("xw2_sb", [128, 2 * 128], BF16)
    yg2_sb = ysb("yg2_sb", [128, 2 * 128], BF16)
    ynt_sb = ysb("ynt_sb", [128, 2 * 128], BF16)
    rms_sb = ysb("rms_sb", [1, 128], F32)
    lbt_sb = ysb("lbt_sb", [128, 2 * 128], F32)
    cbt_sb = ysb("cbt_sb", [128, 2 * 128], F32)
    rmsb_sb = ysb("rmsb_sb", [128, 128], F32)

    pyc = ExitStack()
    pA = [pyc.enter_context(nc.psum_tensor(f"pA{i}", [128, 512], F32)) for i in range(2)]
    pY = [pyc.enter_context(nc.psum_tensor(f"pY{i}", [128, 512], F32)) for i in range(2)]
    pss = pyc.enter_context(nc.psum_tensor("pss", [128, 512], F32))

    v0, s0, p0 = cV.v + 1, cS.v, cP.v
    n0 = [cNp[0].v, cNp[1].v]
    y0 = [cYp[0].v, cYp[1].v]
    tot_sT = [c_.v for c_ in cTp]
    peA_ev, peB_ev, peC_ev = {}, {}, {}
    v0t_ev, vA_ev, vB_ev, scA_ev, scB_ev = {}, {}, {}, {}, {}
    g1_ev, g2_ev = {}, {}
    ev_cdma, ev_rms_sc, ev_recip, ev_rmsbmm, ev_rmsb = {}, {}, {}, {}, {}
    ev_yn_stt, ev_yn_dma = {}, {}

    pe_seq = []
    for c in range(NCHUNK):
        pe_seq += [("A", c * 64), ("A", c * 64 + 1)]
        for hh in range(NH):
            t = c * 64 + hh
            pe_seq.append(("B", t))
            if hh < 62:
                pe_seq.append(("A", t + 2))
            pe_seq.append(("C", t))
        pe_seq.append(("R", c))
    pv = p0
    for kind, t in pe_seq:
        pv += 1
        if kind == "A":
            peA_ev[t] = pv
        elif kind == "B":
            peB_ev[t] = pv
        elif kind == "C":
            peC_ev[t] = pv
        else:
            ev_rmsbmm[t] = pv
    for c in range(NCHUNK):
        base = v0 + c * 258
        v0t_ev[c * 64 + 0] = base + 1
        v0t_ev[c * 64 + 1] = base + 2
        for hh in range(NH):
            t = c * 64 + hh
            if hh < 62:
                v0t_ev[t + 2] = base + 2 + hh * 3 + 1
                vA_ev[t] = base + 2 + hh * 3 + 2
                vB_ev[t] = base + 2 + hh * 3 + 3
            else:
                vA_ev[t] = base + 2 + 62 * 3 + (hh - 62) * 2 + 1
                vB_ev[t] = base + 2 + 62 * 3 + (hh - 62) * 2 + 2
            scA_ev[t] = s0 + c * 129 + hh * 2 + 1
            scB_ev[t] = s0 + c * 129 + hh * 2 + 2
            ev_yn_stt[(c, hh)] = base + 194 + hh + 1
            ev_yn_dma[(c, hh)] = n0[hh % 2] + (c * 32 + hh // 2 + 1) * 16
        ev_cdma[c] = y0[c % 2] + (c // 2 + 1) * 32
        ev_rms_sc[c] = s0 + c * 129 + 129
        ev_recip[c] = base + 193
        ev_rmsb[c] = base + 194

    @block.vector
    def _(vector):
        for hh in range(NH):
            vector.tensor_scalar(out=identd[:, hh * 128:(hh + 1) * 128],
                                 in0=identb[:], scalar1=dparc[:, hh:hh + 1],
                                 scalar2=None, op0=ALU.mult)
        vector.drain()
        vector.nop().then_inc(sV, 1)
    cV.add(1)
    assert cV.v == v0

    for c in range(NCHUNK):
        qc = c % 2

        @block.sync
        def _(sync, c=c, qc=qc):
            if c == 0:
                sync.wait_ge(sV, ev_hin)
                sync.wait_ge(sZp[0], ev_sz_dma[62])
                sync.wait_ge(sZp[1], ev_sz_dma[63])
                for ii in range(4):
                    sync.wait_ge(sTp[ii], tot_sT[ii])
            if c >= 2:
                sync.wait_ge(sV, vB_ev[(c - 2) * 64 + 63])
            sync.dma_start(
                out=xtmc[:, qc * DSSM:(qc + 1) * DSSM],
                in_=xtm_st[c].rearrange("h p t -> p h t")).then_inc(sYp[qc], 16)
            sync.dma_start(
                out=szc[:, qc * DSSM:(qc + 1) * DSSM],
                in_=sz_st[c].rearrange("h p t -> p h t")).then_inc(sYp[qc], 16)
        cYp[qc].add(32)
        assert cYp[qc].v == ev_cdma[c]

        @block.tensor
        def _(tensor, c=c, qc=qc):
            def emit_A(t):
                hh = t - c * 64
                q = t % 2
                if hh == 0:
                    tensor.wait_ge(sYp[qc], ev_cdma[c])
                    if c >= 1:
                        tensor.wait_ge(sV, ev_rmsb[c - 1])
                tensor.wait_ge(sV, v0t_ev[t])
                if t >= 2:
                    tensor.wait_ge(sV, vB_ev[t - 2])
                    tensor.wait_ge(sS, scA_ev[t - 2])
                tensor.matmul(pA[q][:, 0:128], b_fm[:, c * 128:c * 128 + 128],
                              c_fm[:, c * 128:c * 128 + 128], start=True, stop=True)
                tensor.transpose(pA[q][:, 128:256],
                                 lbt_sb[:, q * 128:q * 128 + 128], identf[:])
                tensor.transpose(pA[q][:, 256:384],
                                 cbt_sb[:, q * 128:q * 128 + 128],
                                 identf[:]).then_inc(sP, 1)
                assert cP.add(1) == peA_ev[t]

            def emit_B(t):
                hh = t - c * 64
                q = t % 2
                tensor.wait_ge(sV, vA_ev[t])
                tensor.matmul(pY[q][:, 0:128],
                              xtmc[:, qc * DSSM + hh * 128:qc * DSSM + hh * 128 + 128],
                              m1_sb[:, q * 128:q * 128 + 128], start=True, stop=False)
                tensor.matmul(pY[q][:, 0:128],
                              xtmc[:, qc * DSSM + hh * 128:qc * DSSM + hh * 128 + 128],
                              identd[:, hh * 128:(hh + 1) * 128], start=False, stop=False)
                tensor.matmul(pY[q][:, 0:128], sbf_sb[:, q * 128:q * 128 + 128],
                              mv_sb[:, q * 128:q * 128 + 128], start=False, stop=True)
                tensor.matmul(pA[q][:, 384:512], b_tm[:, c * 128:(c + 1) * 128],
                              xw2_sb[:, q * 128:q * 128 + 128],
                              start=True, stop=True).then_inc(sP, 1)
                assert cP.add(1) == peB_ev[t]

            def emit_C(t):
                hh = t - c * 64
                q = t % 2
                tensor.wait_ge(sS, scB_ev[t])
                tensor.matmul(pss[0:1, 0:128], onescol_bf[:],
                              yg2_sb[:, q * 128:q * 128 + 128],
                              start=(hh == 0), stop=(hh == 63)).then_inc(sP, 1)
                assert cP.add(1) == peC_ev[t]

            emit_A(c * 64)
            emit_A(c * 64 + 1)
            for hh in range(NH):
                t = c * 64 + hh
                emit_B(t)
                if hh < 62:
                    emit_A(t + 2)
                emit_C(t)
            tensor.wait_ge(sV, ev_recip[c])
            tensor.matmul(pss[:, 128:256], onesrow[:], rms_sb[0:1, :],
                          start=True, stop=True).then_inc(sP, 1)
            assert cP.add(1) == ev_rmsbmm[c]

        @block.scalar
        def _(scalar, c=c):
            for hh in range(NH):
                t = c * 64 + hh
                q = t % 2
                scalar.wait_ge(sP, peA_ev[t])
                scalar.activation(out=e_sb[:, q * 128:q * 128 + 128],
                                  in_=pA[q][:, 128:256], func=AF.Exp,
                                  bias=lndtm[:, c * 64 + hh:c * 64 + hh + 1]
                                  ).then_inc(sS, 1)
                assert cS.add(1) == scA_ev[t]
                scalar.wait_ge(sV, vB_ev[t])
                if t >= 2:
                    scalar.wait_ge(sP, peC_ev[t - 2])
                scalar.activation(out=yg2_sb[:, q * 128:q * 128 + 128],
                                  in_=yg_all[:, hh * 128:hh * 128 + 128],
                                  func=AF.Square).then_inc(sS, 1)
                assert cS.add(1) == scB_ev[t]
            scalar.wait_ge(sP, peC_ev[c * 64 + 63])
            scalar.activation(out=rms_sb[:], in_=pss[0:1, 0:128], func=AF.Sqrt,
                              scale=1.0 / DSSM, bias=cst[0:1, 0:1]).then_inc(sS, 1)
            assert cS.add(1) == ev_rms_sc[c]

        @block.vector
        def _(vector, c=c, qc=qc):
            def emit_v0(t):
                hh = t - c * 64
                q = t % 2
                if t >= 2:
                    vector.wait_ge(sP, peA_ev[t - 2])
                vector.tensor_scalar(out=lbt_sb[:, q * 128:q * 128 + 128],
                                     in0=maskT_sb[:],
                                     scalar1=ltm[:, c * 64 + hh:c * 64 + hh + 1],
                                     scalar2=None, op0=ALU.add)
                vector.tensor_scalar(out=cbt_sb[:, q * 128:q * 128 + 128],
                                     in0=maskT_sb[:], scalar1=0.0,
                                     scalar2=cumtm[:, c * 64 + hh:c * 64 + hh + 1],
                                     op0=ALU.mult, op1=ALU.add).then_inc(sV, 1)
                assert cV.add(1) == v0t_ev[t]

            emit_v0(c * 64)
            emit_v0(c * 64 + 1)
            for hh in range(NH):
                t = c * 64 + hh
                q = t % 2
                if hh < 62:
                    emit_v0(t + 2)
                vector.wait_ge(sS, scA_ev[t])
                if t >= 64:
                    vector.wait_ge(sV, vB_ev[t - 64])
                vector.tensor_copy(out=sbf_sb[:, q * 128:q * 128 + 128],
                                   in_=s_all[:, hh * 128:hh * 128 + 128])
                vector.tensor_scalar(
                    out=xw2_sb[:, q * 128:q * 128 + 128],
                    in0=xtmc[:, qc * DSSM + hh * 128:qc * DSSM + hh * 128 + 128],
                    scalar1=wctm[:, c * 64 + hh:c * 64 + hh + 1],
                    scalar2=None, op0=ALU.mult)
                vector.tensor_tensor(out=m1_sb[:, q * 128:q * 128 + 128],
                                     in0=e_sb[:, q * 128:q * 128 + 128],
                                     in1=pA[q][:, 0:128], op=ALU.mult)
                vector.tensor_tensor(out=mv_sb[:, q * 128:q * 128 + 128],
                                     in0=c_fm[:, c * 128:c * 128 + 128],
                                     in1=pA[q][:, 256:384], op=ALU.mult).then_inc(sV, 1)
                assert cV.add(1) == vA_ev[t]
                vector.wait_ge(sP, peB_ev[t])
                vector.scalar_tensor_tensor(
                    out=s_all[:, hh * 128:hh * 128 + 128],
                    in0=s_all[:, hh * 128:hh * 128 + 128],
                    scalar=dbc[:, c * 64 + hh:c * 64 + hh + 1],
                    in1=pA[q][:, 384:512], op0=ALU.mult, op1=ALU.add)
                vector.tensor_tensor(out=yg_all[:, hh * 128:hh * 128 + 128],
                                     in0=pY[q][:, 0:128],
                                     in1=szc[:, qc * DSSM + hh * 128:qc * DSSM + hh * 128 + 128],
                                     op=ALU.mult).then_inc(sV, 1)
                assert cV.add(1) == vB_ev[t]

        @block.vector
        def _(vector, c=c):
            vector.wait_ge(sS, ev_rms_sc[c])
            vector.reciprocal(out=rms_sb[:], in_=rms_sb[:]).then_inc(sV, 1)
            assert cV.add(1) == ev_recip[c]

        @block.vector
        def _(vector, c=c):
            vector.wait_ge(sP, ev_rmsbmm[c])
            vector.tensor_copy(out=rmsb_sb[:], in_=pss[:, 128:256]).then_inc(sV, 1)
            assert cV.add(1) == ev_rmsb[c]

        @block.vector
        def _(vector, c=c):
            for hh in range(NH):
                if hh == 0:
                    vector.wait_ge(sV, ev_rmsb[c])
                if hh >= 2:
                    vector.wait_ge(sNp[hh % 2], ev_yn_dma[(c, hh - 2)])
                elif c >= 1:
                    vector.wait_ge(sNp[hh % 2], ev_yn_dma[(c - 1, 62 + hh)])
                vector.scalar_tensor_tensor(
                    out=ynt_sb[:, (hh % 2) * 128:(hh % 2) * 128 + 128],
                    in0=yg_all[:, hh * 128:hh * 128 + 128],
                    scalar=gnormc[:, hh:hh + 1], in1=rmsb_sb[:],
                    op0=ALU.mult, op1=ALU.mult).then_inc(sV, 1)
                assert cV.add(1) == ev_yn_stt[(c, hh)]

        @block.scalar
        def _(scalar, c=c):
            for hh in range(NH):
                scalar.wait_ge(sV, ev_yn_stt[(c, hh)])
                scalar.dma_start(out=yn_st[hh][:, c * 128:c * 128 + 128],
                                 in_=ynt_sb[:, (hh % 2) * 128:(hh % 2) * 128 + 128]
                                 ).then_inc(sNp[hh % 2], 16)
                assert cNp[hh % 2].add(16) == ev_yn_dma[(c, hh)]

    tap("yg7", yg_all[:, :])
    tap("s_end", s_all[:, :])

    if phase_limit < 4:
        return _finish(nc, block, es, taps, sV, cV, sG, cG, [], out_t,
                       extra_ctx=[pyc, yctx, sctx], all_sems=all_dma_sems)

    yctx.close()
    pyc.close()
    sctx.close()
    msctx.close()
    wctx = ExitStack()

    def wsb(name, shape, dtype):
        return wctx.enter_context(nc.sbuf_tensor(name, shape, dtype))

    ynp = wsb("ynp", [128, NH * 512], BF16)
    wobf = wsb("wobf", [128, 4 * 4096], BF16)
    wsem = sWp + sWo
    wcnt = cWp + cWo
    attn4 = wsb("attn4", [128, 2 * 2048], F32)
    xres_sb = wsb("xres_sb", [128, 2 * 128], F32)
    x1t_sb = wsb("x1t_sb", [128, 2 * 128], F32)

    pwc = ExitStack()
    pw = [pwc.enter_context(nc.psum_tensor(f"pw{i}", [128, 512], F32)) for i in range(4)]
    pz2 = [pwc.enter_context(nc.psum_tensor(f"pz{i}", [128, 512], F32)) for i in range(2)]

    ev_wold = {}
    ev_wocast = {}
    ev_att_ev = {}
    ev_x1dma = {}
    ev_pw_free = {}
    ev_unitsT = {}
    last_yn0 = cNp[0].v
    last_yn1 = cNp[1].v

    wslots = [(pt, 0), (pt, 128), (ph, 0), (ph, 128)]
    for pzb in pz2:
        for j in range(4):
            wslots.append((pzb, j * 128))
    NSLOT = len(wslots)
    add_evs = []

    def emit_wout_tail(n):
        tgn, ngn = n // 8, n % 8
        parn = n % 2
        for ncx in range(4):
            for tb in range(4):
                ti = ncx * 4 + tb
                u = n * 16 + ti
                qt = ti % 2
                pb, off = wslots[u % NSLOT]

                @block.tensor
                def _(tensor, n=n, ncx=ncx, tb=tb, ti=ti, u=u, pb=pb, off=off,
                      parn=parn):
                    if ti == 0:
                        tensor.wait_ge(sS, ev_pw_free[n])
                    if u >= NSLOT:
                        tensor.wait_ge(sV, add_evs[u - NSLOT])
                    tensor.transpose(pb[:, off:off + 128],
                                     attn4[:, parn * 2048 + ncx * 512 + tb * 128:
                                           parn * 2048 + ncx * 512 + tb * 128 + 128],
                                     identf[:]).then_inc(sP, 1)
                cP.add(1)
                ev_tp = cP.v

                @block.scalar
                def _(scalar, tgn=tgn, ngn=ngn, ncx=ncx, tb=tb, qt=qt, wv=cV.v):
                    scalar.wait_ge(sV, wv)
                    r0 = 3 + tgn * 512 + tb * 128
                    c0 = ngn * 512 + ncx * 128
                    scalar.dma_start(out=xres_sb[:, qt * 128:qt * 128 + 128],
                                     in_=xs[r0:r0 + 128, c0:c0 + 128]
                                     ).then_inc(sXp[qt], 16)
                cXp[qt].add(16)
                ev_xr = (sXp[qt], cXp[qt].v)

                @block.vector
                def _(vector, qt=qt, w1=ev_tp, w2=ev_xr, pb=pb, off=off,
                      w3=ev_x1dma.get(len(ev_x1dma) - 2)):
                    vector.wait_ge(sP, w1)
                    vector.wait_ge(w2[0], w2[1])
                    if w3 is not None:
                        vector.wait_ge(w3[0], w3[1])
                    vector.tensor_add(x1t_sb[:, qt * 128:qt * 128 + 128],
                                      pb[:, off:off + 128],
                                      xres_sb[:, qt * 128:qt * 128 + 128])
                    vector.drain()
                    vector.nop().then_inc(sV, 1)
                cV.add(1)
                add_evs.append(cV.v)

                @block.sync
                def _(sync, tgn=tgn, ngn=ngn, ncx=ncx, tb=tb, qt=qt, w=cV.v):
                    sync.wait_ge(sV, w)
                    r0 = tgn * 512 + tb * 128
                    c0 = ngn * 512 + ncx * 128
                    sync.dma_start(out=x1_st[r0:r0 + 128, c0:c0 + 128],
                                   in_=x1t_sb[:, qt * 128:qt * 128 + 128]
                                   ).then_inc(sTp[qt], 16)
                cTp[qt].add(16)
                ev_x1dma[len(ev_x1dma)] = (sTp[qt], cTp[qt].v)
        ev_unitsT[n] = cP.v

    for tg in range(2):
        @block.sync
        def _(sync, tg=tg):
            if tg == 0:
                sync.wait_ge(sNp[0], last_yn0)
                sync.wait_ge(sNp[1], last_yn1)
            else:
                sync.wait_ge(sS, ev_pw_free[7])
            for hh in range(NH):
                sync.dma_start(out=ynp[:, hh * 512:(hh + 1) * 512],
                               in_=yn_st[hh][:, tg * 512:(tg + 1) * 512]
                               ).then_inc(sYp[tg], 16)
        cYp[tg].add(16 * NH)
        ev_ynp = (sYp[tg], cYp[tg].v)

        for ng in range(8):
            for kg in range(8):
                si = (tg * 8 + ng) * 8 + kg
                qw = si % 4

                @block.gpsimd
                def _(gpsimd, tg=tg, ng=ng, kg=kg, si=si, qw=qw):
                    if si < 2:
                        gpsimd.wait_ge(sNp[0], last_yn0)
                        gpsimd.wait_ge(sNp[1], last_yn1)
                    if si >= 4:
                        gpsimd.wait_ge(sP, _later(ev_att_ev, si - 4, cP))
                    gpsimd.dma_start(
                        out=wobf[:, qw * 4096:(qw + 1) * 4096],
                        in_=wo[ng, kg * 8:(kg + 1) * 8].rearrange(
                            "k p n -> p k n")).then_inc(wsem[qw], 16)
                wcnt[qw].add(16)
                ev_wold[si] = (wsem[qw], wcnt[qw].v)

                @block.tensor
                def _(tensor, tg=tg, ng=ng, kg=kg, si=si, qw=qw, ev_ynp=ev_ynp):
                    tensor.wait_ge(ev_wold[si][0], ev_wold[si][1])
                    if kg == 0:
                        tensor.wait_ge(ev_ynp[0], ev_ynp[1])
                        if not (tg == 0 and ng == 0):
                            tensor.wait_ge(sS, ev_pw_free[tg * 8 + ng - 1])
                    last = None
                    for kk8 in range(8):
                        kc = kg * 8 + kk8
                        for ncx in range(4):
                            last = tensor.matmul(
                                pw[ncx][:, 0:512],
                                wobf[:, qw * 4096 + kk8 * 512 + ncx * 128:
                                     qw * 4096 + kk8 * 512 + ncx * 128 + 128],
                                ynp[:, kc * 512:(kc + 1) * 512],
                                start=(kc == 0), stop=(kc == 63))
                    last.then_inc(sP, 1)
                cP.add(1)
                ev_att_ev[si] = cP.v

            bi = tg * 8 + ng
            par = bi % 2

            @block.scalar
            def _(scalar, bi=bi, par=par, wmm=ev_att_ev[(tg * 8 + ng) * 8 + 7]):
                scalar.wait_ge(sP, wmm)
                if bi >= 2:
                    scalar.wait_ge(sP, ev_unitsT[bi - 2])
                for ncx in range(4):
                    scalar.activation(out=attn4[:, par * 2048 + ncx * 512:
                                                par * 2048 + ncx * 512 + 512],
                                      in_=pw[ncx][:, 0:512], func=AF.Identity)
                scalar.drain()
                scalar.nop().then_inc(sS, 1)
            cS.add(1)
            ev_pw_free[bi] = cS.v

            if bi >= 1:
                emit_wout_tail(bi - 1)

    emit_wout_tail(15)

    ev_x1_done = [(sTp[0], cTp[0].v), (sTp[1], cTp[1].v)]
    tap("x1st", x1_st[:, :])

    if phase_limit < 5:
        return _finish(nc, block, es, taps, sV, cV, sG, cG, [], out_t,
                       extra_ctx=[pwc, wctx], all_sems=all_dma_sems)

    wctx.close()
    pwc.close()
    fctx = ExitStack()

    def fsb(name, shape, dtype):
        return fctx.enter_context(nc.sbuf_tensor(name, shape, dtype))

    qxT = fsb("qxT", [128, 32 * 1024], BF16)
    x1row2 = fsb("x1row2", [128, 2 * 4096], F32)
    qxr = fsb("qxr", [128, 2 * 4096], F32)
    qs_f = fsb("qs_f", [128, 4096], F32)
    qi_i = fsb("qi_i", [128, 4096], I32)
    q256a = fsb("q256a", [128, 256], F32)
    q256b = fsb("q256b", [128, 256], F32)
    q256c = fsb("q256c", [128, 256], F32)
    q256d = fsb("q256d", [128, 256], F32)
    qwT_sb = fsb("qwT_sb", [128, 32 * 128], BF16)
    ffn_f = fsb("ffn_f", [128, 1024], F32)
    ost_sb = fsb("ost_sb", [128, 2 * 128], F32)
    x1r_sb = fsb("x1r_sb", [128, 2 * 128], F32)
    lnq_small = fsb("lnq_small", [128, 16], F32)

    pfc = ExitStack()
    pf = [pfc.enter_context(nc.psum_tensor(f"pf{i}", [128, 512], F32)) for i in range(2)]

    ev_q_src_done = {}
    ev_qr_done = {}
    ev_bfp_done = {}
    ev_ld = {}
    ev_outdma = {}
    ev_fev = {}
    ev_fmm = {}

    for tc in range(NCHUNK):
        i = tc
        qp = i % 2

        @block.scalar
        def _(scalar, tc=tc, i=i, qp=qp):
            if tc == 0:
                for s_, v_ in ev_x1_done:
                    scalar.wait_ge(s_, v_)
            if i >= 2:
                scalar.wait_ge(sV, ev_q_src_done[i - 2])
            scalar.dma_start(out=x1row2[:, qp * 4096:(qp + 1) * 4096],
                             in_=x1_st[tc * 128:(tc + 1) * 128, :]).then_inc(sXp[qp], 16)
        cXp[qp].add(16)
        ev_ld[i] = (sXp[qp], cXp[qp].v)

        @block.scalar
        def _(scalar, i=i, qp=qp, w=ev_ld[i]):
            scalar.wait_ge(w[0], w[1])
            if i >= 2:
                scalar.wait_ge(sP, ev_qr_done[i - 2])
            scalar.activation(out=qxr[:, qp * 4096:(qp + 1) * 4096],
                              in_=x1row2[:, qp * 4096:(qp + 1) * 4096],
                              func=AF.Square).then_inc(sS, 1)
        cS.add(1)

        @block.vector
        def _(vector, i=i, qp=qp, w=ev_ld[i], w2=cS.v):
            vector.wait_ge(w[0], w[1])
            vector.wait_ge(sS, w2)
            if i >= 2:
                vector.wait_ge(sP, ev_qr_done[i - 2])
            xt = x1row2[:, qp * 4096:(qp + 1) * 4096]
            vector.reduce_sum(lnq_small[:, 1:2], qxr[:, qp * 4096:(qp + 1) * 4096],
                              axis=AX)
            vector.reduce_sum(lnq_small[:, 0:1], xt, axis=AX)
            vector.drain()
            vector.tensor_scalar(out=lnq_small[:, 1:2], in0=lnq_small[:, 1:2],
                                 scalar1=1.0 / H, scalar2=None, op0=ALU.mult)
            vector.tensor_scalar(out=lnq_small[:, 0:1], in0=lnq_small[:, 0:1],
                                 scalar1=1.0 / H, scalar2=None, op0=ALU.mult)
            vector.drain()
            vector.tensor_tensor(out=lnq_small[:, 2:3], in0=lnq_small[:, 0:1],
                                 in1=lnq_small[:, 0:1], op=ALU.mult)
            vector.tensor_scalar(out=lnq_small[:, 5:6], in0=lnq_small[:, 0:1],
                                 scalar1=-1.0, scalar2=None, op0=ALU.mult)
            vector.drain()
            vector.tensor_sub(lnq_small[:, 2:3], lnq_small[:, 1:2], lnq_small[:, 2:3])
            vector.drain()
            vector.nop().then_inc(sV, 1)
        cV.add(1)

        @block.scalar
        def _(scalar, w=cV.v):
            scalar.wait_ge(sV, w)
            scalar.activation(out=lnq_small[:, 3:4], in_=lnq_small[:, 2:3],
                              func=AF.Sqrt, bias=cst[:, 0:1]).then_inc(sS, 1)
        cS.add(1)

        @block.vector
        def _(vector, i=i, qp=qp, w=cS.v):
            vector.wait_ge(sS, w)
            vector.reciprocal(out=lnq_small[:, 4:5], in_=lnq_small[:, 3:4])
            vector.drain()
            vector.tensor_scalar(out=qxr[:, qp * 4096:(qp + 1) * 4096],
                                 in0=x1row2[:, qp * 4096:(qp + 1) * 4096],
                                 scalar1=lnq_small[:, 5:6], scalar2=lnq_small[:, 4:5],
                                 op0=ALU.add, op1=ALU.mult)
            vector.drain()
            vector.nop().then_inc(sV, 1)
        cV.add(1)
        ev_q_src_done[i] = cV.v

        _emit_bfp(block, sV, cV, sS, cS, qxr[:, qp * 4096:(qp + 1) * 4096],
                  qs_f, qi_i, q256a, q256b, q256c, q256d,
                  qxr[:, qp * 4096:(qp + 1) * 4096], INVLN2, LN2C, cst)
        ev_bfp_done[i] = cV.v

        tev = []
        for fc in range(32):
            @block.tensor
            def _(tensor, tc=tc, fc=fc, qp=qp, w=cV.v):
                if fc == 0:
                    tensor.wait_ge(sV, w)
                if fc >= 2:
                    tensor.wait_ge(sV, tev[fc - 2])
                dstp = pt if fc % 2 == 0 else ph
                tensor.transpose(dstp[:, 0:128],
                                 qxr[:, qp * 4096 + fc * 128:qp * 4096 + fc * 128 + 128],
                                 identf[:]).then_inc(sP, 1)
            cP.add(1)

            @block.vector
            def _(vector, tc=tc, fc=fc, w=cP.v):
                vector.wait_ge(sP, w)
                srcp = pt if fc % 2 == 0 else ph
                vector.tensor_copy(out=qxT[:, fc * 1024 + tc * 128:fc * 1024 + tc * 128 + 128],
                                   in_=srcp[:, 0:128]).then_inc(sV, 1)
            cV.add(1)
            tev.append(cV.v)
        ev_qr_done[i] = cP.v
    ev_qx_done = cV.v

    def emit_outstage(ncx):
        for tb in range(8):
            qt = tb % 2

            @block.tensor
            def _(tensor, ncx=ncx, tb=tb, qt=qt, w=ev_fev[ncx]):
                tensor.wait_ge(sS, w)
                if tb >= 2:
                    tensor.wait_ge(sV, cV.v)
                dstp = pt if qt == 0 else ph
                tensor.transpose(dstp[:, 128:256], ffn_f[:, tb * 128:tb * 128 + 128],
                                 identf[:]).then_inc(sP, 1)
            cP.add(1)
            ev_otp = cP.v

            @block.scalar
            def _(scalar, ncx=ncx, tb=tb, qt=qt, wv=cV.v):
                scalar.wait_ge(sV, wv)
                scalar.dma_start(out=x1r_sb[:, qt * 128:qt * 128 + 128],
                                 in_=x1_st[tb * 128:(tb + 1) * 128,
                                           ncx * 128:(ncx + 1) * 128]
                                 ).then_inc(sXp[qt], 16)
            cXp[qt].add(16)
            ev_x1r = (sXp[qt], cXp[qt].v)

            @block.vector
            def _(vector, qt=qt, w1=ev_otp, w2=ev_x1r,
                  w3=ev_outdma.get(len(ev_outdma) - 2)):
                vector.wait_ge(sP, w1)
                vector.wait_ge(w2[0], w2[1])
                if w3 is not None:
                    vector.wait_ge(w3[0], w3[1])
                srcp = pt if qt == 0 else ph
                vector.tensor_add(ost_sb[:, qt * 128:qt * 128 + 128],
                                  srcp[:, 128:256], x1r_sb[:, qt * 128:qt * 128 + 128])
                vector.nop().then_inc(sV, 1)
            cV.add(1)

            @block.sync
            def _(sync, ncx=ncx, tb=tb, qt=qt, w=cV.v):
                sync.wait_ge(sV, w)
                sync.dma_start(out=out_t[tb * 128:(tb + 1) * 128,
                                         ncx * 128:(ncx + 1) * 128],
                               in_=ost_sb[:, qt * 128:qt * 128 + 128]
                               ).then_inc(sOp[qt], 16)
            cOp[qt].add(16)
            ev_outdma[len(ev_outdma)] = (sOp[qt], cOp[qt].v)

    def emit_ffn_tail(n):
        @block.scalar
        def _(scalar, ncx=n, w=ev_fmm[n], wv=cV.v):
            scalar.wait_ge(sP, w)
            if ncx >= 1:
                scalar.wait_ge(sV, wv)
            for tg in range(2):
                scalar.activation(out=ffn_f[:, tg * 512:(tg + 1) * 512],
                                  in_=pf[tg][:, 0:512], func=AF.Identity,
                                  bias=fbiasc[:, ncx:ncx + 1])
            scalar.drain()
            scalar.nop().then_inc(sS, 1)
        cS.add(1)
        ev_fev[n] = cS.v
        emit_outstage(n)

    for ncx in range(32):
        i = 8 + ncx
        qp = i % 2

        @block.scalar
        def _(scalar, ncx=ncx, i=i, qp=qp):
            if i >= 2:
                scalar.wait_ge(sV, ev_q_src_done[i - 2])
            scalar.dma_start(out=x1row2[:, qp * 4096:(qp + 1) * 4096],
                             in_=wf[ncx * 128:(ncx + 1) * 128, :]).then_inc(sXp[qp], 16)
        cXp[qp].add(16)
        ev_ld[i] = (sXp[qp], cXp[qp].v)

        @block.vector
        def _(vector, i=i, qp=qp, w=ev_ld[i]):
            vector.wait_ge(w[0], w[1])
            if i >= 2:
                vector.wait_ge(sP, ev_qr_done[i - 2])
            vector.nop().then_inc(sV, 1)
        cV.add(1)

        _emit_bfp(block, sV, cV, sS, cS, x1row2[:, qp * 4096:(qp + 1) * 4096],
                  qs_f, qi_i, q256a, q256b, q256c, q256d,
                  qxr[:, qp * 4096:(qp + 1) * 4096], INVLN2, LN2C, cst)
        ev_q_src_done[i] = cV.v

        if ncx >= 1:
            emit_ffn_tail(ncx - 1)

        tev = []
        for kk in range(32):
            @block.tensor
            def _(tensor, kk=kk, qp=qp, w=cV.v):
                if kk == 0:
                    tensor.wait_ge(sV, w)
                if kk >= 2:
                    tensor.wait_ge(sV, tev[kk - 2])
                dstp = pt if kk % 2 == 0 else ph
                tensor.transpose(dstp[:, 0:128],
                                 qxr[:, qp * 4096 + kk * 128:qp * 4096 + kk * 128 + 128],
                                 identf[:]).then_inc(sP, 1)
            cP.add(1)

            @block.vector
            def _(vector, kk=kk, w=cP.v):
                vector.wait_ge(sP, w)
                srcp = pt if kk % 2 == 0 else ph
                vector.tensor_copy(out=qwT_sb[:, kk * 128:kk * 128 + 128],
                                   in_=srcp[:, 0:128]).then_inc(sV, 1)
            cV.add(1)
            tev.append(cV.v)
        ev_qr_done[i] = cP.v
        ev_qwT = cV.v

        @block.tensor
        def _(tensor, ncx=ncx, w=ev_qwT):
            tensor.wait_ge(sV, w)
            if ncx >= 1:
                tensor.wait_ge(sS, ev_fev[ncx - 1])
            last = None
            for kk in range(32):
                for tg in range(2):
                    last = tensor.matmul(
                        pf[tg][:, 0:512], qwT_sb[:, kk * 128:kk * 128 + 128],
                        qxT[:, kk * 1024 + tg * 512:kk * 1024 + tg * 512 + 512],
                        start=(kk == 0), stop=(kk == 31))
            last.then_inc(sP, 1)
        cP.add(1)
        ev_fmm[ncx] = cP.v

    emit_ffn_tail(31)

    return _finish(nc, block, es, taps, sV, cV, sG, cG,
                   [(sOp[0], cOp[0].v), (sOp[1], cOp[1].v)], out_t,
                   extra_ctx=[pfc, fctx], all_sems=all_dma_sems)


def _later(d, t, c=None):
    return d.get(t, 0) if isinstance(d, dict) else d[t]


def _dt_math(block, L, sV, cV, sS, cS, sP, cP, upto=99):
    nc_l = L
    dt_raw = L["dt_raw"]; dtb_sb = L["dtb_sb"]; dt_sb = L["dt_sb"]
    lndt_sb = L["lndt_sb"]; alog_sb = L["alog_sb"]; lda_sb = L["lda_sb"]
    ll_sb = L["ll_sb"]; offs_sb = L["offs_sb"]
    ltot_sb = L["ltot_sb"]; wtm = L["wtm"]
    wctm = L["wctm"]; lndtm = L["lndtm"]; dbc = L["dbc"]
    etot_sb = L["etot_sb"]; cumtm = L["cumtm"]; ltm = L["ltm"]
    sm_a = L["sm_a"]; sm_b = L["sm_b"]; dslrow = L["dslrow"]
    identf = L["identf"]; onesrow = L["onesrow"]
    pt = L["pt"]; ph = L["ph"]
    evfree_dt = L["evfree"][0]

    @block.scalar
    def _(scalar):
        scalar.wait_ge(sV, evfree_dt[1])
        scalar.activation(out=dt_sb[:], in_=dt_raw[:], func=AF.Abs, bias=dtb_sb[:])
        scalar.drain()
        scalar.activation(out=dt_sb[:], in_=dt_sb[:], func=AF.Exp, scale=-1.0)
        scalar.drain()
        scalar.activation(out=dt_sb[:], in_=dt_sb[:], func=AF.Ln, bias=1.0)
        scalar.activation(out=lda_sb[:], in_=dt_raw[:], func=AF.Relu, bias=dtb_sb[:])
        scalar.activation(out=sm_a[:, 0:1], in_=alog_sb[:], func=AF.Exp)
        scalar.drain()
        scalar.drain()
        scalar.nop().then_inc(sS, 1)
    cS.add(1)

    if upto < 2:
        return (cV.v, cV.v)

    @block.vector
    def _(vector, w=cS.v):
        vector.wait_ge(sS, w)
        vector.tensor_add(dt_sb[:], dt_sb[:], lda_sb[:])
        vector.tensor_scalar(out=sm_a[:, 1:2], in0=sm_a[:, 0:1], scalar1=-1.0,
                             scalar2=None, op0=ALU.mult)
        vector.memset(offs_sb[:, 15:16], 0.0)
        vector.drain()
        vector.tensor_scalar(out=lda_sb[:], in0=dt_sb[:], scalar1=sm_a[:, 1:2],
                             scalar2=None, op0=ALU.mult)
        vector.drain()
        for c in range(NCHUNK):
            vector.tensor_tensor_scan(
                out=ll_sb[:, c * 128:(c + 1) * 128],
                data0=lda_sb[:, c * 128:(c + 1) * 128],
                data1=offs_sb[:, 15:16].broadcast_to([NH, 128]),
                initial=0.0, op0=ALU.add, op1=ALU.add)
        vector.drain()
        vector.tensor_copy(out=ltot_sb[:], in_=ll_sb[:, 127::128])
        vector.memset(offs_sb[:, 0:1], 0.0)
        vector.drain()
        vector.tensor_tensor_scan(
            out=offs_sb[:, 1:9], data0=ltot_sb[:],
            data1=offs_sb[:, 15:16].broadcast_to([NH, 8]),
            initial=0.0, op0=ALU.add, op1=ALU.add)
        vector.drain()
        vector.tensor_scalar(out=sm_b[:, 0:8],
                             in0=offs_sb[:, 0:8], scalar1=offs_sb[:, 8:9],
                             scalar2=-1.0, op0=ALU.subtract, op1=ALU.mult)
        vector.drain()
        vector.nop().then_inc(sV, 1)
    cV.add(1)

    if upto < 3:
        return (cV.v, cV.v)

    @block.scalar
    def _(scalar, w=cV.v):
        scalar.wait_ge(sV, w)
        scalar.activation(out=lndt_sb[:], in_=dt_sb[:], func=AF.Ln)
        scalar.activation(out=sm_a[:, 2:3], in_=offs_sb[:, 8:9], func=AF.Exp)
        scalar.drain()
        scalar.nop().then_inc(sS, 1)
    cS.add(1)
    ev_base = cS.v

    if upto < 4:
        return (cV.v, cV.v)

    ev_tr = []
    for c in range(NCHUNK):
        @block.scalar
        def _(scalar, c=c, w=ev_base):
            scalar.wait_ge(sS, w)
            if c >= 1:
                scalar.wait_ge(sV, ev_tr[c - 1])
            scalar.activation(out=sm_a[:, 0:128],
                              in_=ll_sb[:, c * 128:(c + 1) * 128], func=AF.Exp,
                              scale=-1.0, bias=sm_b[:, c:c + 1])
            scalar.drain()
            scalar.nop().then_inc(sS, 1)
        cS.add(1)

        @block.vector
        def _(vector, c=c, w=cS.v):
            vector.wait_ge(sS, w)
            vector.tensor_mul(sm_a[:, 0:128], sm_a[:, 0:128],
                              dt_sb[:, c * 128:(c + 1) * 128]).then_inc(sV, 1)
        cV.add(1)

        @block.tensor
        def _(tensor, c=c, w=cV.v):
            tensor.wait_ge(sV, w)
            tensor.transpose(pt[0:128, 0:64], sm_a[0:NH, 0:128],
                             identf[0:NH, 0:NH]).then_inc(sP, 1)
        cP.add(1)

        @block.vector
        def _(vector, c=c, w=cP.v):
            vector.wait_ge(sP, w)
            vector.tensor_copy(out=wtm[:, c * 64:(c + 1) * 64],
                               in_=pt[0:128, 0:64]).then_inc(sV, 1)
        cV.add(1)

        @block.scalar
        def _(scalar, c=c, w=cV.v):
            scalar.wait_ge(sV, w)
            scalar.activation(out=sm_a[:, 0:128],
                              in_=ll_sb[:, c * 128:(c + 1) * 128], func=AF.Exp,
                              scale=-1.0, bias=ltot_sb[:, c:c + 1])
            scalar.drain()
            scalar.nop().then_inc(sS, 1)
        cS.add(1)

        @block.vector
        def _(vector, c=c, w=cS.v):
            vector.wait_ge(sS, w)
            vector.tensor_mul(sm_a[:, 0:128], sm_a[:, 0:128],
                              dt_sb[:, c * 128:(c + 1) * 128])
            vector.tensor_sub(dt_raw[:, 0:128],
                              lndt_sb[:, c * 128:(c + 1) * 128],
                              ll_sb[:, c * 128:(c + 1) * 128])
            vector.drain()
            vector.nop().then_inc(sV, 1)
        cV.add(1)

        @block.tensor
        def _(tensor, c=c, w=cV.v):
            tensor.wait_ge(sV, w)
            tensor.transpose(pt[0:128, 64:128], sm_a[0:NH, 0:128],
                             identf[0:NH, 0:NH])
            tensor.transpose(ph[0:128, 0:64], dt_raw[0:NH, 0:128],
                             identf[0:NH, 0:NH])
            tensor.transpose(ph[0:128, 64:128], ll_sb[0:NH, c * 128:(c + 1) * 128],
                             identf[0:NH, 0:NH]).then_inc(sP, 1)
        cP.add(1)

        @block.vector
        def _(vector, c=c, w=cP.v):
            vector.wait_ge(sP, w)
            vector.tensor_copy(out=wctm[:, c * 64:(c + 1) * 64], in_=pt[0:128, 64:128])
            vector.tensor_copy(out=lndtm[:, c * 64:(c + 1) * 64],
                               in_=ph[0:128, 0:64])
            vector.tensor_copy(out=ltm[:, c * 64:(c + 1) * 64],
                               in_=ph[0:128, 64:128]).then_inc(sV, 1)
        cV.add(1)
        ev_tr.append(cV.v)

        @block.scalar
        def _(scalar, c=c, w=cV.v):
            scalar.wait_ge(sV, w)
            scalar.activation(out=cumtm[:, c * 64:(c + 1) * 64],
                              in_=ltm[:, c * 64:(c + 1) * 64],
                              func=AF.Exp).then_inc(sS, 1)
        cS.add(1)


    if upto < 5:
        return (cV.v, cV.v)

    @block.scalar
    def _(scalar, w=ev_tr[NCHUNK - 1]):
        scalar.wait_ge(sV, w)
        scalar.activation(out=etot_sb[:], in_=ltot_sb[:], func=AF.Exp).then_inc(sS, 1)
    cS.add(1)

    for c in range(NCHUNK):
        @block.vector
        def _(vector, c=c, w=cS.v):
            vector.wait_ge(sS, w)
            if c >= 1:
                vector.wait_ge(sP, cP.v)
            vector.drain()
            vector.tensor_scalar(out=sm_a[0:NH, 0:128], in0=sm_a[0:NH, 0:128],
                                 scalar1=0.0, scalar2=etot_sb[:, c:c + 1],
                                 op0=ALU.mult, op1=ALU.add)
            vector.drain()
            vector.drain()
            vector.nop().then_inc(sV, 1)
        cV.add(1)

        @block.tensor
        def _(tensor, c=c, w=cV.v, ws=cS.v):
            tensor.wait_ge(sV, w)
            tensor.wait_ge(sS, ws)
            tensor.transpose(ph[0:128, 128:192], sm_a[0:NH, 0:128],
                             identf[0:NH, 0:NH]).then_inc(sP, 1)
        cP.add(1)

        @block.vector
        def _(vector, c=c, w=cP.v):
            vector.wait_ge(sP, w)
            vector.tensor_copy(out=dbc[:, c * 64:(c + 1) * 64],
                               in_=ph[0:128, 128:192]).then_inc(sV, 1)
        cV.add(1)

    if upto < 6:
        return (cV.v, cV.v)

    pj0 = L["pj"][0]

    @block.tensor
    def _(tensor, w=cS.v):
        tensor.wait_ge(sS, w)
        tensor.transpose(pj0[0:1, 1280:1344], sm_a[0:NH, 2:3],
                         identf[0:NH, 0:NH]).then_inc(sP, 1)
    cP.add(1)

    @block.vector
    def _(vector, w=cP.v):
        vector.wait_ge(sP, w)
        vector.tensor_copy(out=dslrow[:], in_=pj0[0:1, 1280:1344]).then_inc(sV, 1)
    cV.add(1)
    ev_dsl = cV.v
    ev_smalls_done = cV.v
    return ev_smalls_done, ev_dsl


def _emit_bfp(block, sV, cV, sS, cS, srcf, qs_f, qi_i, q256a, q256b, q256c,
              q256d, dstf, invln2, ln2c, cstq=None):
    @block.vector
    def _(vector, w=cV.v):
        vector.wait_ge(sV, w)
        vector.tensor_reduce(
            out=q256a[:], in_=srcf.rearrange("p (g k) -> p g k", k=16),
            axis=AX, op=ALU.max, apply_absolute_value=True)
        vector.drain()
        vector.tensor_scalar(out=q256a[:], in0=q256a[:], scalar1=1e-38,
                             scalar2=None, op0=ALU.max)
        vector.drain()
        vector.drain()
        vector.nop().then_inc(sV, 1)
    cV.add(1)

    @block.scalar
    def _(scalar, w=cV.v):
        scalar.wait_ge(sV, w)
        scalar.activation(out=q256b[:], in_=q256a[:], func=AF.Ln).then_inc(sS, 1)
    cS.add(1)

    @block.vector
    def _(vector, w=cS.v):
        vector.wait_ge(sS, w)
        vector.tensor_scalar(out=q256b[:], in0=q256b[:], scalar1=invln2,
                             scalar2=None, op0=ALU.mult)
        vector.drain()
        vector.tensor_copy(out=qi_i[:, 0:256], in_=q256b[:])
        vector.drain()
        vector.tensor_copy(out=q256c[:], in_=qi_i[:, 0:256])
        vector.drain()
        vector.tensor_tensor(out=q256d[:], in0=q256c[:], in1=q256b[:], op=ALU.is_gt)
        vector.drain()
        vector.tensor_sub(q256c[:], q256c[:], q256d[:])
        vector.drain()
        vector.drain()
        vector.nop().then_inc(sV, 1)
    cV.add(1)

    @block.scalar
    def _(scalar, w=cV.v):
        scalar.wait_ge(sV, w)
        scalar.activation(out=q256a[:], in_=q256c[:], func=AF.Exp,
                          scale=ln2c, bias=cstq[:, 1:2])
        scalar.activation(out=q256b[:], in_=q256c[:], func=AF.Exp,
                          scale=-ln2c, bias=cstq[:, 2:3])
        scalar.drain()
        scalar.nop().then_inc(sS, 1)
    cS.add(1)

    @block.vector
    def _(vector, w=cS.v):
        vector.wait_ge(sS, w)
        vector.tensor_tensor(
            out=qs_f[:].rearrange("p (g k) -> p g k", k=16),
            in0=srcf.rearrange("p (g k) -> p g k", k=16),
            in1=q256b[:].unsqueeze(2).broadcast_to([128, 256, 16]), op=ALU.mult)
        vector.drain()
        vector.tensor_scalar(out=qs_f[:], in0=qs_f[:], scalar1=127.49999,
                             scalar2=-127.49999, op0=ALU.min, op1=ALU.max)
        vector.drain()
        vector.tensor_copy(out=qi_i[:], in_=qs_f[:])
        vector.drain()
        vector.tensor_tensor(
            out=dstf.rearrange("p (g k) -> p g k", k=16),
            in0=qi_i[:].rearrange("p (g k) -> p g k", k=16),
            in1=q256a[:].unsqueeze(2).broadcast_to([128, 256, 16]),
            op=ALU.mult).then_inc(sV, 1)
    cV.add(1)


def _finish(nc, block, es, taps, sV, cV, sG, cG, out_waits, out_t, extra_ctx=(),
            all_sems=()):
    @block.gpsimd
    def _(gpsimd):
        gpsimd.wait_ge(sV, cV.v)
        for d, src in taps:
            gpsimd.wait_ge(sG, cG.v)
            gpsimd.dma_start(out=d, in_=src).then_inc(sG, 16)
            cG.add(16)
        gpsimd.wait_ge(sG, cG.v)
        for s_, v_ in out_waits:
            gpsimd.wait_ge(s_, v_)
        for s_, c_ in all_sems:
            gpsimd.wait_ge(s_, c_.v)

    for ctx in extra_ctx:
        ctx.close()
    es.close()
    return nc


def _host_prep(inputs):
    x = np.ascontiguousarray(np.asarray(inputs["L0_attn_hidden_states"], np.float32))
    Wi = np.asarray(inputs["L0_attn_W_in"], np.float32)
    convw = np.asarray(inputs["L0_attn_conv_w"], np.float32)
    dtb = np.asarray(inputs["L0_attn_dt_bias"], np.float32)
    alog = np.asarray(inputs["L0_attn_A_log"], np.float32)
    dpar = np.asarray(inputs["L0_attn_D_param"], np.float32)
    gnorm = np.asarray(inputs["L0_attn_norm_weight"], np.float32)
    Wo = np.asarray(inputs["L0_attn_W_out"], np.float32)
    Wf = np.asarray(inputs["L0_ffn_weight"], np.float32)
    fb = np.asarray(inputs["L0_ffn_bias"], np.float32)

    wi_t = np.zeros((H, NCC * 128), np.float32)
    wi_t[:, :16704] = Wi
    wi_t = np.ascontiguousarray(
        wi_t.reshape(2, 16, 128, NCC, 128).transpose(3, 0, 2, 1, 4))
    wo_t = np.ascontiguousarray(
        Wo.reshape(64, 128, 8, 512).transpose(2, 0, 1, 3))
    convw_t = np.ascontiguousarray(convw.reshape(66, 128, 4).transpose(1, 0, 2)
                                   ).reshape(128, 264)
    dpar_bc = np.ascontiguousarray(np.broadcast_to(dpar[None, :], (128, NH)))
    gnorm_t = np.ascontiguousarray(gnorm.reshape(NH, 128).T)
    fbias_t = np.ascontiguousarray(fb.reshape(32, 128).T)
    j_idx = np.arange(128)
    maskbias = np.where(j_idx[:, None] <= j_idx[None, :], 0.0, -1e30).astype(np.float32)
    ident = np.eye(128, dtype=np.float32)

    shared = dict(wi=wi_t, wo=wo_t, wf=Wf, convw_t=convw_t,
                  dtb=dtb.reshape(NH, 1), alog=alog.reshape(NH, 1),
                  dpar=dpar_bc, gnorm=gnorm_t, fbias=fbias_t,
                  maskb=maskbias, maskbT=np.ascontiguousarray(maskbias.T),
                  identb=ident)

    in_maps = []
    for core in range(8):
        b, rk = core // 4, core % 4
        xsl = np.zeros((TOKX, H), np.float32)
        if rk > 0:
            xsl[0:3] = x[b, rk * 1024 - 3:rk * 1024]
        xsl[3:] = x[b, rk * 1024:(rk + 1) * 1024]
        cm = np.zeros((8, 16), np.float32)
        grp = range((core // 4) * 4, (core // 4) * 4 + 4)
        msel = np.zeros(8, np.float32)
        for j in grp:
            rj = j % 4
            if rj < rk:
                msel[j] = 1.0
                for k in grp:
                    if rj < (k % 4) < rk:
                        cm[j, k] = 1.0
        cm[:, 8:16] = 1.0 - cm[:, 0:8]
        cm[:, 15] = msel
        in_maps.append(dict(xs=np.ascontiguousarray(xsl), cmask=cm, **shared))
    return x, in_maps


_NC_CACHE = {}


def kernel(**inputs):
    x, in_maps = _host_prep(inputs)
    key = ("full",)
    if key not in _NC_CACHE:
        _NC_CACHE[key] = build()
    nc = _NC_CACHE[key]
    res = run_bass_kernel_spmd(nc, in_maps, core_ids=list(range(8)))
    out = np.zeros_like(x)
    for core in range(8):
        b, rk = core // 4, core % 4
        out[b, rk * 1024:(rk + 1) * 1024] = res.results[core]["out"]
    return out
